# version 07
# baseline (speedup 1.0000x reference)
import math
from contextlib import ExitStack

import numpy as np

import concourse.bass as bass
import concourse.mybir as mybir
from concourse.bass_utils import run_bass_kernel_spmd

F32 = mybir.dt.float32
BF16 = mybir.dt.bfloat16
I32 = mybir.dt.int32
AF = mybir.ActivationFunctionType
ALU = mybir.AluOpType
AX = mybir.AxisListType

D = 1024
L = 4096
BATCH = 4
NCORES = 8
OWN = 2048
HALO = 1664
RBN = HALO + OWN
RBT = RBN // 128
X1L = 768
X1N = X1L + OWN
HEADS_B = 8
DH = 64
INNER = 512
HID = 2730
HIDP = 2816
W1C = 5632
HEADS_A = 16
WIN = 512
NEG = -30000.0
NS1, NS2 = 6, 4

def _split_2pi():
    tp = 2.0 * math.pi
    def trunc12(v):
        f = np.float32(v)
        u = f.view(np.uint32) & np.uint32(0xFFFFF000)
        return float(u.view(np.float32))
    c1 = trunc12(tp)
    c2 = trunc12(tp - c1)
    c3 = float(np.float32(tp - c1 - c2))
    return c1, c2, c3

C2PI_1, C2PI_2, C2PI_3 = _split_2pi()
PI_LO = float(np.nextafter(np.float32(math.pi), np.float32(0.0)))


def _build(upto="all", taps=()):
    nc = bass.Bass("TRN2", target_bir_lowering=False)

    def di(name, shape, dt=F32):
        return nc.dram_tensor(name, shape, dt, kind="ExternalInput").ap()

    x_full = di("x_full", [L, D])
    x_b = di("x_b", [RBN, D])
    posk = di("posk", [1, L])
    posb = di("posb", [128, RBT])
    invf_hi = di("invf_hi", [1, 32])
    invf_lo = di("invf_lo", [1, 32])
    lnbs_hi = di("lnbs_hi", [1, 32])
    lnbs_lo = di("lnbs_lo", [1, 32])
    onecol = di("onecol", [128, 1])
    onerow = di("onerow", [1, 128])
    allones = di("allones", [128, 128])
    repm = di("repm", [128, 128])
    shf64 = di("shf64", [128, 128], BF16)
    poska = di("poska", [128, 32])
    ident_bf = di("ident_bf", [128, 128], BF16)
    ident_f32 = di("ident_f32", [128, 128])
    perm_a = di("perm_a", [128, 128], BF16)
    mask_lo = di("mask_lo", [128, 128], BF16)
    mask_hi = di("mask_hi", [128, 128], BF16)
    w0row = di("w0row", [1, NS1 + NS2])
    a_wqkv = di("a_wqkv", [D, 3 * D])
    a_wout = di("a_wout", [D, D])
    b_wqkv = di("b_wqkv", [2, D, 3 * INNER])
    b_wo = di("b_wo", [2, INNER, D])
    b_w1 = di("b_w1", [2, D, W1C])
    b_b1 = di("b_b1", [2, 128, W1C // 128])
    b_w2 = di("b_w2", [2, HIDP, D])
    b_b2 = di("b_b2", [2, 128, 8])
    b_lng = di("b_lng", [2, 128, 8])
    b_lnb = di("b_lnb", [2, 128, 8])
    b_qs = di("b_qs", [2, 1, INNER])
    b_ks = di("b_ks", [2, 1, INNER])

    xn_dram = [nc.dram_tensor(f"xn_dram{c}", [128, RBN], BF16,
                              kind="Internal").ap() for c in range(8)]
    g_dram = nc.dram_tensor("g_dram", [128, 16 * D], BF16, kind="Internal").ap()
    out = nc.dram_tensor("out", [OWN, D], F32, kind="ExternalOutput").ap()
    tap_out = {}
    for nm, shape in taps:
        tap_out[nm] = nc.dram_tensor("tap_" + nm, list(shape), F32,
                                     kind="ExternalOutput").ap()

    es = ExitStack()
    sems, cnts = {}, {}

    def sem(name):
        if name not in sems:
            sems[name] = es.enter_context(nc.semaphore(name))
            cnts[name] = 0
        return sems[name]

    def inc(instr, name, n=1):
        instr.then_inc(sem(name), n)
        cnts[name] += n

    def wait(eng, name, back=0):
        if name in cnts:
            eng.wait_ge(sem(name), max(0, cnts[name] - back))

    def wait_all(eng, *names):
        for n in names:
            wait(eng, n)

    def sb(name, shape, dt=F32):
        return es.enter_context(nc.sbuf_tensor(name, shape, dt))

    def ps(name, shape, dt=F32):
        return es.enter_context(nc.psum_tensor(name, shape, dt))

    TEN, VEC, ACT, GPS, SYN = nc.tensor, nc.vector, nc.scalar, nc.gpsimd, nc.sync

    def vd():
        VEC.drain()

    with es:
        es.enter_context(nc.Block())

        gstage = sb("gstage", [128, 1024], BF16)
        _xst_ctxs = []
        xst = []
        for c in range(8):
            cm_ = nc.sbuf_tensor(f"xst{c}", [128, X1N], F32, side="right")
            xst.append(cm_.__enter__())
            _xst_ctxs.append(cm_)
        identb = sb("identb", [128, 128], BF16)
        identf = sb("identf", [128, 128], F32)
        onec = sb("onec", [128, 1], F32)
        oner = sb("oner", [1, 128], F32)
        onebf = sb("onebf", [128, 1], BF16)
        masklo = sb("masklo", [128, 128], BF16)
        maskhi = sb("maskhi", [128, 128], BF16)
        w0r = sb("w0r", [1, NS1 + NS2], F32)
        consts = sb("consts", [128, 8], F32)
        _tb_ctxs = []
        tb_csq, tb_snq, tb_csk, tb_snk = [], [], [], []
        for nm in ("tb_csq", "tb_snq", "tb_csk", "tb_snk"):
            cm_ = nc.sbuf_tensor(nm, [128, RBT * 32], F32, side="right")
            t_ = cm_.__enter__()
            _tb_ctxs.append(cm_)
            if nm == "tb_csq":
                tb_csq = t_
            elif nm == "tb_snq":
                tb_snq = t_
            elif nm == "tb_csk":
                tb_csk = t_
            else:
                tb_snk = t_

        psA = ps("psA", [128, 512])
        psB = ps("psB", [128, 512])
        psC = ps("psC", [128, 512])
        psD = ps("psD", [128, 512])
        psE = ps("psE", [128, 512])
        psSm = ps("psSm", [128, 1024])
        psT = ps("psT", [128, 1024], BF16)
        bankrot = [psA, psB, psC, psD, psE]

        for dst, src in ((identb, ident_bf), (identf, ident_f32),
                         (onec, onecol), (oner, onerow), (w0r, w0row)):
            wait(SYN, "dx")
            inc(SYN.dma_start(out=dst[:], in_=src), "dx", 16)
        for dst, src in ((masklo, mask_lo), (maskhi, mask_hi)):
            wait(SYN, "dx")
            inc(SYN.dma_start(out=dst[:], in_=src), "dx", 16)
        wait(GPS, "dw")
        inc(GPS.dma_start(out=onebf[:], in_=onecol), "dw", 16)

        wait(ACT, "dx")
        inc(ACT.activation(consts[:, 0:1], onec[:], AF.Arctan, scale=1.0), "ac")
        wait(VEC, "ac")
        inc(VEC.tensor_scalar(consts[:, 0:1], consts[:, 0:1], 4.0, None, ALU.mult), "dv")
        vd()
        inc(VEC.tensor_scalar(consts[:, 1:2], consts[:, 0:1], 2.0, None, ALU.mult), "dv")
        vd()
        inc(VEC.reciprocal(consts[:, 2:3], consts[:, 1:2]), "dv")
        inc(VEC.tensor_scalar(consts[:, 3:4], consts[:, 0:1], 0.5, None, ALU.mult), "dv")
        inc(VEC.tensor_scalar(consts[:, 5:6], consts[:, 1:2], -1.0, None, ALU.mult), "dv")
        vd()
        wait(ACT, "dv")
        inc(ACT.activation(consts[:, 4:5], onec[:], AF.Ln, scale=10000.0), "ac")
        PI, TWOPI, INV2PI, HFPI, LN1E4, NEG2PI = (consts[:, k:k + 1] for k in range(6))

        def alc(name, shape, dt=F32):
            cm = nc.sbuf_tensor(name, shape, dt)
            return cm.__enter__(), cm

        def freel(ctxs):
            for cm in reversed(ctxs):
                cm.__exit__(None, None, None)

        cst = sb("cst", [128, 640], F32)
        _tsc = []
        def tscal(name, shape, dt=F32):
            t, cm = alc(name, shape, dt)
            _tsc.append(cm)
            return t
        cst_i = sb("cst_i", [128, 512], I32)

        def rrfold(r, y, kf, ki):
            inc(VEC.tensor_scalar(kf, y, INV2PI, None, ALU.mult), "dv")
            vd()
            inc(VEC.tensor_copy(ki, kf), "dv")
            vd()
            inc(VEC.tensor_copy(kf, ki), "dv")
            vd()
            inc(VEC.scalar_tensor_tensor(r, kf, -C2PI_1, y, ALU.mult, ALU.add), "dv")
            vd()
            inc(VEC.scalar_tensor_tensor(r, kf, -C2PI_2, r, ALU.mult, ALU.add), "dv")
            vd()
            inc(VEC.scalar_tensor_tensor(r, kf, -C2PI_3, r, ALU.mult, ALU.add), "dv")
            vd()

        def rrsin(dst, src_ap, width, phase=None, extra=None):
            wait(VEC, "ac")
            t0 = cst[:, 0:width]
            y = src_ap
            if phase is not None:
                inc(VEC.tensor_scalar(t0, src_ap, phase, None, ALU.add), "dv")
                vd()
                y = t0
            kf = cst[:, 512:512 + width]
            ki = cst_i[:, 0:width]
            r = cst[:, 64:64 + width]
            rrfold(r, y, kf, ki)
            rrfold(r, r, kf, ki)
            if extra is not None:
                inc(VEC.tensor_tensor(r, r, extra, ALU.add), "dv")
                vd()
            inc(VEC.tensor_scalar(r, r, PI_LO, -PI_LO, ALU.min, ALU.max), "dv")
            wait(ACT, "dv")
            inc(ACT.activation(dst, r, AF.Sin, scale=1.0), "ac")

        posb_t = tscal("posb_t", [128, RBT], F32)
        wait(SYN, "dx")
        inc(SYN.dma_start(out=posb_t[:], in_=posb), "dx", 16)
        rowin = tscal("rowin", [1, 32], F32)
        invf_b = tscal("invf_b", [128, 32], F32)
        invl_b = tscal("invl_b", [128, 32], F32)
        lnb_b = tscal("lnb_b", [128, 32], F32)
        lnl_b = tscal("lnl_b", [128, 32], F32)
        wait(TEN, "ac")
        for dst, src in ((invf_b, invf_hi), (invl_b, invf_lo),
                         (lnb_b, lnbs_hi), (lnl_b, lnbs_lo)):
            wait(SYN, "mm")
            wait(SYN, "dx")
            inc(SYN.dma_start(out=rowin[:], in_=src), "dx", 16)
            wait_all(TEN, "dx", "dv")
            inc(TEN.matmul(psA[:, 0:32], oner[:], rowin[:],
                           start=True, stop=True), "mm")
            wait(VEC, "mm")
            inc(VEC.tensor_copy(dst[:], psA[:, 0:32]), "dv")

        angt = tscal("angt", [128, 32], F32)
        angl = tscal("angl", [128, 32], F32)
        sclt = tscal("sclt", [128, 32], F32)
        pwrt = tscal("pwrt", [128, 1], F32)
        sint = tscal("sint", [128, 32], F32)
        cost = tscal("cost", [128, 32], F32)
        for ti in range(RBT):
            pc = posb_t[:, ti:ti + 1]
            inc(VEC.tensor_scalar(angt[:], invf_b[:], pc, None, ALU.mult), "dv")
            inc(VEC.tensor_scalar(angl[:], invl_b[:], pc, None, ALU.mult), "dv")
            inc(VEC.tensor_scalar(pwrt[:], pc, -2048.0, 1.0 / 256.0,
                                  ALU.add, ALU.mult), "dv")
            vd()
            inc(VEC.tensor_scalar(sclt[:], lnb_b[:], pwrt[:], None, ALU.mult), "dv")
            vd()
            inc(VEC.scalar_tensor_tensor(sclt[:], lnl_b[:], pwrt[:], sclt[:],
                                         ALU.mult, ALU.add), "dv")
            wait(ACT, "dv")
            inc(ACT.activation(sclt[:], sclt[:], AF.Exp, scale=1.0), "ac")
            rrsin(sint[:], angt[:], 32, extra=angl[:])
            rrsin(cost[:], angt[:], 32, phase=HFPI, extra=angl[:])
            sl = slice(ti * 32, (ti + 1) * 32)
            wait(VEC, "ac")
            inc(VEC.scalar_tensor_tensor(tb_csq[:, sl], cost[:], 8.0, sclt[:],
                                         ALU.mult, ALU.mult), "dv")
            inc(VEC.scalar_tensor_tensor(tb_snq[:, sl], sint[:], 8.0, sclt[:],
                                         ALU.mult, ALU.mult), "dv")
            vd()
            inc(VEC.reciprocal(sclt[:], sclt[:]), "dv")
            vd()
            inc(VEC.tensor_tensor(tb_csk[:, sl], cost[:], sclt[:], ALU.mult), "dv")
            inc(VEC.tensor_tensor(tb_snk[:, sl], sint[:], sclt[:], ALU.mult), "dv")

        freel(_tsc)
        wait_all(SYN, "dv", "ac")

        xrow = sb("xrow", [128, D], F32)
        lnsum = sb("lnsum", [128, 1], F32)
        lnssq = sb("lnssq", [128, 1], F32)
        lnmu = sb("lnmu", [128, 1], F32)
        lnrs = sb("lnrs", [128, 1], F32)
        lnbi = sb("lnbi", [128, 1], F32)
        sqg = sb("sqg", [128, D], F32)
        _fsc = []
        xnrow, _c = alc("xnrow", [128, D], BF16)
        _fsc.append(_c)

        def ln_tm(dst_ap, src_ap):
            inc(VEC.scalar_tensor_tensor(sqg[:], src_ap, 1.0, src_ap,
                                         ALU.mult, ALU.mult,
                                         accum_out=lnssq[:]), "dv")
            vd()
            inc(VEC.tensor_scalar(sqg[:], src_ap, 0.0, None, ALU.add,
                                  ALU.add, accum_out=lnsum[:]), "dv")
            vd()
            inc(VEC.tensor_scalar(lnmu[:], lnsum[:], 1.0 / D, None, ALU.mult), "dv")
            vd()
            inc(VEC.scalar_tensor_tensor(lnrs[:], lnmu[:], -1.0, lnmu[:],
                                         ALU.mult, ALU.mult), "dv")
            vd()
            inc(VEC.scalar_tensor_tensor(lnrs[:], lnssq[:], 1.0 / D, lnrs[:],
                                         ALU.mult, ALU.add), "dv")
            vd()
            inc(VEC.tensor_scalar(lnrs[:], lnrs[:], 1.0e-5, None, ALU.add), "dv")
            vd()
            wait(ACT, "dv")
            inc(ACT.activation(lnrs[:], lnrs[:], AF.Ln, scale=1.0), "ac")
            wait(VEC, "ac")
            inc(VEC.tensor_scalar(lnrs[:], lnrs[:], -0.5, None, ALU.mult), "dv")
            vd()
            wait(ACT, "dv")
            inc(ACT.activation(lnrs[:], lnrs[:], AF.Exp, scale=1.0), "ac")
            wait(VEC, "ac")
            inc(VEC.scalar_tensor_tensor(lnbi[:], lnmu[:], -1.0, lnrs[:],
                                         ALU.mult, ALU.mult), "dv")
            vd()
            wait_all(ACT, "dv", "mm")
            inc(ACT.activation(dst_ap, src_ap, AF.Identity,
                               scale=lnrs[:], bias=lnbi[:]), "ac")

        def tmcm(dst_list, dst_col, src_bf, nch=8):
            wait(TEN, "ac")
            for c in range(nch):
                wait(TEN, "dv")
                inc(TEN.transpose(psT[:, 0:128],
                                  src_bf[:, c * 128:(c + 1) * 128], identb[:]),
                    "mm")
                wait(VEC, "mm")
                inc(VEC.tensor_copy(dst_list[c][:, dst_col:dst_col + 128],
                                    psT[:, 0:128]), "dv")

        xns = []
        for c in range(8):
            t, _c = alc(f"xns{c}", [128, 1024], BF16)
            xns.append(t)
            _fsc.append(_c)
        for ti in range(RBT):
            bi = ti % 8
            if bi == 0 and ti > 0:
                wait(SYN, "dv")
                for c in range(8):
                    wait(SYN, "st")
                    inc(SYN.dma_start(out=xn_dram[c][:, (ti - 8) * 128:ti * 128],
                                      in_=xns[c][:]), "st", 16)
            wait(SYN, "ac")
            wait(SYN, "dx")
            inc(SYN.dma_start(out=xrow[:], in_=x_b[ti * 128:(ti + 1) * 128, :]),
                "dx", 16)
            wait(VEC, "dx")
            wait(VEC, "ac")
            if bi == 0 and ti > 0:
                wait(VEC, "st")
            ln_tm(xnrow[:], xrow[:])
            tmcm(xns, bi * 128, xnrow)
        wait(SYN, "dv")
        for c in range(8):
            wait(SYN, "st")
            inc(SYN.dma_start(out=xn_dram[c][:, 24 * 128:RBN],
                              in_=xns[c][:, 0:(RBT - 24) * 128]), "st", 16)
        wait(VEC, "st")
        freel(_fsc)

        def tap_sync():
            for e in (VEC, ACT, TEN, SYN, GPS):
                wait(e, "st")

        def tap1(nm, src_ap):
            if nm in tap_out:
                wait_all(GPS, "dv", "ac", "mm")
                wait(GPS, "st")
                inc(GPS.dma_start(out=tap_out[nm], in_=src_ap), "st", 16)
                tap_sync()

        tap1("csq", tb_csq[:])
        tap1("snq", tb_snq[:])
        tap1("csk", tb_csk[:])
        tap1("snk", tb_snk[:])

        def finish_stub():
            inc(VEC.memset(xrow[:], 0.0), "dv")
            wait(SYN, "dv")
            for ti in range(OWN // 128):
                wait(SYN, "st")
                inc(SYN.dma_start(out=out[ti * 128:(ti + 1) * 128, :],
                                  in_=xrow[:]), "st", 16)
            wait(SYN, "st")

        if upto == "front":
            finish_stub()
            return nc

        def alc(name, shape, dt=F32):
            cm = nc.sbuf_tensor(name, shape, dt)
            return cm.__enter__(), cm

        def freel(ctxs):
            for cm in reversed(ctxs):
                cm.__exit__(None, None, None)

        ssq8 = sb("ssq8", [128, 8], F32)
        rn8 = sb("rn8", [128, 8], F32)
        rop1 = sb("rop1", [128, 512], F32)
        rop2 = sb("rop2", [128, 512], F32)
        qpb = sb("qpb", [128, 512], BF16)
        mxc = sb("mxc", [128, 1], F32)
        negMb = sb("negMb", [128, 1], F32)
        biasM = sb("biasM", [128, 1], F32)
        rnc = sb("rnc", [128, 1], F32)
        nm1 = sb("nm1", [1, 1], F32)
        shst = sb("shst", [128, 8], BF16)
        qkraw = xrow

        def mk_attn_bufs(l):
            ctxs = []
            def a(name, shape, dt):
                t, cm = alc(f"{name}L{l}", shape, dt)
                ctxs.append(cm)
                return t
            bufs = dict(
                kcb=[[a(f"kcb{p}{c}", [128, 512], BF16) for c in range(4)]
                     for p in range(2)],
                qcb=[a(f"qcb{c}", [128, 512], BF16) for c in range(4)],
                vbf=[[a(f"vb{p}{t}", [128, 520], BF16) for t in range(4)]
                     for p in range(2)],
                ptb=a("ptb", [128, 1024], BF16),
                otb=a("otb", [128, 2048], BF16),
                ocb=[a(f"ocb{c}", [128, 512], BF16) for c in range(4)],
                wob=[a(f"wob{c}", [128, D], BF16) for c in range(4)],
                wqb=[a(f"wqb{c}", [128, 3 * INNER], BF16) for c in range(8)],
                qsl=a("qsl", [128, INNER], F32),
                ksl=a("ksl", [128, INNER], F32),
            )
            if l == 0:
                bufs["xnr"] = [a(f"xnr{c}", [128, 2 * 516], BF16)
                               for c in range(8)]
            else:
                bufs["stb"] = [a(f"stb{c}", [128, 520], BF16)
                               for c in range(8)]
            return bufs, ctxs

        def mk_ffn_bufs(l):
            ctxs = []
            def a(name, shape, dt):
                t, cm = alc(f"{name}F{l}", shape, dt)
                ctxs.append(cm)
                return t
            bufs = dict(
                hin=[a(f"hin{c}", [128, 512], BF16) for c in range(8)],
                gel=a("fgel", [128, 512], BF16),
                hb=[a(f"hb{j}", [128, 512], BF16) for j in range(22)],
                w1b=[a(f"w1b{p}", [128, 2048], BF16) for p in range(2)],
                w2b=[a(f"w2b{j}", [128, D], BF16) for j in range(22)],
                b1b=a("b1b", [128, W1C // 128], F32),
                b2b=a("b2b", [128, 8], F32),
                lngb=a("lngb", [128, 8], F32),
                lnbb=a("lnbb", [128, 8], F32),
            )
            return bufs, ctxs

        def rope_tm(ab, srcT, col0, is_q, rbti):
            src_ap = srcT[:, col0:col0 + 512]
            sc = ab["qsl"] if is_q else ab["ksl"]
            cs = (tb_csq if is_q else tb_csk)[:, rbti * 32:(rbti + 1) * 32]
            sn = (tb_snq if is_q else tb_snk)[:, rbti * 32:(rbti + 1) * 32]
            cs_b = cs.rearrange("p (a b d) -> p a b d", a=1, b=1
                                ).broadcast_to([128, 8, 2, 32])
            sn_b = sn.rearrange("p (a d) -> p a d", a=1
                                ).broadcast_to([128, 8, 32])
            inc(VEC.tensor_tensor(sqg[:, 0:512], src_ap, src_ap, ALU.mult), "dv")
            vd()
            inc(VEC.tensor_reduce(ssq8[:], sqg[:, 0:512].rearrange(
                "p (h d) -> p h d", h=8), AX.X, ALU.add), "dv")
            vd()
            inc(VEC.tensor_scalar(ssq8[:], ssq8[:], 1.0e-18, None, ALU.max), "dv")
            wait(ACT, "dv")
            inc(ACT.activation(rn8[:], ssq8[:], AF.Ln, scale=1.0), "ac")
            wait(VEC, "ac")
            inc(VEC.tensor_scalar(rn8[:], rn8[:], -0.5, None, ALU.mult), "dv")
            vd()
            wait(ACT, "dv")
            inc(ACT.activation(rn8[:], rn8[:], AF.Exp, scale=1.0), "ac")
            wait(VEC, "ac")
            for h in range(8):
                inc(VEC.tensor_scalar(sqg[:, h * 64:(h + 1) * 64],
                                      srcT[:, col0 + h * 64:col0 + (h + 1) * 64],
                                      rn8[:, h:h + 1], None, ALU.mult), "dv")
            vd()
            inc(VEC.tensor_tensor(sqg[:, 0:512], sqg[:, 0:512], sc[:],
                                  ALU.mult), "dv")
            vd()
            sq4 = sqg[:, 0:512].rearrange("p (h f d) -> p h f d", h=8, f=2)
            r14 = rop1[:].rearrange("p (h f d) -> p h f d", h=8, f=2)
            q4 = qpb[:].rearrange("p (h f d) -> p h f d", h=8, f=2)
            r23 = rop2[:].rearrange("p (h f d) -> p h f d", h=8, f=2)
            inc(VEC.tensor_tensor(r14, sq4, cs_b, ALU.mult), "dv")
            inc(VEC.tensor_tensor(r23[:, :, 0, :], sq4[:, :, 1, :], sn_b,
                                  ALU.mult), "dv")
            inc(VEC.tensor_tensor(r23[:, :, 1, :], sq4[:, :, 0, :], sn_b,
                                  ALU.mult), "dv")
            vd()
            inc(VEC.tensor_tensor(q4[:, :, 0, :], r14[:, :, 0, :],
                                  r23[:, :, 0, :], ALU.subtract), "dv")
            inc(VEC.tensor_tensor(q4[:, :, 1, :], r14[:, :, 1, :],
                                  r23[:, :, 1, :], ALU.add), "dv")

        def kv_build(l, ab, kw, build_q):
            kw0t = -1536 if l == 0 else -512
            wt = kw0t + 512 * kw
            par = kw & 1
            if l == 0:
                r = kw % 2
                wait_all(SYN, "mm", "dv", "st")
                for c in range(8):
                    wait(SYN, "dx")
                    inc(SYN.dma_start(
                        out=ab["xnr"][c][:, r * 516:r * 516 + 513],
                        in_=xn_dram[c][:, wt + 1664 - 1:wt + 1664 + 512]),
                        "dx", 16)
            else:
                wait(VEC, "mm")
                for c in range(8):
                    if kw == 0:
                        inc(VEC.tensor_copy(ab["stb"][c][:, 0:513],
                                            xst[c][:, wt + 768 - 1:
                                                   wt + 768 + 512]), "dv")
                    else:
                        inc(VEC.tensor_copy(ab["stb"][c][:, 0:1],
                                            shst[:, c:c + 1]), "dv")
                        inc(VEC.tensor_copy(ab["stb"][c][:, 1:513],
                                            xst[c][:, wt + 768:
                                                   wt + 768 + 512]), "dv")
                    vd()
                    inc(VEC.tensor_copy(shst[:, c:c + 1],
                                        ab["stb"][c][:, 512:513]), "dv")
            for tt in range(4):
                t0 = wt + tt * 128
                rbti = (t0 + 1664) // 128
                wait_all(TEN, "ac", "dv", "dx")
                for c in range(8):
                    d = 1 if c >= 4 else 0
                    if l == 0:
                        r = kw % 2
                        u = ab["xnr"][c][:, r * 516 + tt * 128 + 1 - d:
                                         r * 516 + tt * 128 + 129 - d]
                    else:
                        u = ab["stb"][c][:, tt * 128 + 1 - d:tt * 128 + 129 - d]
                    if build_q:
                        inc(TEN.matmul(psA[:, 0:512], u, ab["wqb"][c][:, 0:512],
                                       start=(c == 0), stop=(c == 7)), "mm")
                    inc(TEN.matmul(psB[:, 0:512], u, ab["wqb"][c][:, 512:1024],
                                   start=(c == 0), stop=(c == 7)), "mm")
                    inc(TEN.matmul(psC[:, 0:512], u, ab["wqb"][c][:, 1024:1536],
                                   start=(c == 0), stop=(c == 7)), "mm")
                wait_all(ACT, "mm", "dv")
                if build_q:
                    inc(ACT.copy(qkraw[:, 0:512], psA[:, 0:512]), "ac")
                inc(ACT.copy(qkraw[:, 512:1024], psB[:, 0:512]), "ac")
                wait(VEC, "mm")
                vvb = ab["vbf"][par][tt][:].rearrange("p (h d) -> p h d",
                                                      h=8, d=65)
                inc(VEC.tensor_copy(vvb[:, :, 0:64], psC[:, 0:512].rearrange(
                    "p (h d) -> p h d", h=8)), "dv")
                wait(VEC, "ac")
                if build_q:
                    rope_tm(ab, qkraw, 0, True, rbti)
                    wait_all(TEN, "dv")
                    for cc in range(4):
                        wait(TEN, "dv")
                        inc(TEN.transpose(psT[:, 0:128],
                                          qpb[:, cc * 128:(cc + 1) * 128],
                                          identb[:]), "mm")
                        wait(VEC, "mm")
                        inc(VEC.tensor_copy(
                            ab["qcb"][cc][:, tt * 128:(tt + 1) * 128],
                            psT[:, 0:128]), "dv")
                rope_tm(ab, qkraw, 512, False, rbti)
                wait(TEN, "dv")
                for cc in range(4):
                    wait(TEN, "dv")
                    inc(TEN.transpose(psT[:, 0:128],
                                      qpb[:, cc * 128:(cc + 1) * 128],
                                      identb[:]), "mm")
                    wait(VEC, "mm")
                    inc(VEC.tensor_copy(
                        ab["kcb"][par][cc][:, tt * 128:(tt + 1) * 128],
                        psT[:, 0:128]), "dv")

        def attn_slot(l, ab, s):
            sl0t = -1024 if l == 0 else 0
            qwt = sl0t + 512 * s
            pv, ow = s & 1, (s + 1) & 1
            w0i = s if l == 0 else NS1 + s
            wait_all(TEN, "dv", "ac")
            first = True
            for h in range(8):
                cc, hh = h // 2, (h % 2) * 64
                for qb in range(4):
                    lq = ab["qcb"][cc][hh:hh + 64, qb * 128:(qb + 1) * 128]
                    plen = 512 - qb * 128
                    olen = 128 * (qb + 1)
                    wait(TEN, "dv")
                    inc(TEN.matmul(psSm[:, 0:plen], lq,
                                   ab["kcb"][pv][cc][hh:hh + 64, qb * 128:512],
                                   start=True, stop=True), "mm")
                    inc(TEN.matmul(psSm[:, 512:512 + olen], lq,
                                   ab["kcb"][ow][cc][hh:hh + 64, 0:olen],
                                   start=True, stop=True), "mm")
                    wait(VEC, "mm")
                    inc(VEC.tensor_scalar(rop1[:, 0:plen], psSm[:, 0:plen],
                                          0.0, None if first else mxc[:],
                                          ALU.add, ALU.max,
                                          accum_out=mxc[:]), "dv")
                    vd()
                    inc(VEC.tensor_scalar(rop1[:, 0:olen],
                                          psSm[:, 512:512 + olen],
                                          0.0, mxc[:], ALU.add, ALU.max,
                                          accum_out=mxc[:]), "dv")
                    vd()
                    first = False
            wait(TEN, "dv")
            inc(TEN.transpose(psE[0:1, 0:128], mxc[:], identf[:]), "mm")
            wait(VEC, "mm")
            inc(VEC.tensor_reduce(nm1[:], psE[0:1, 0:128], AX.X, ALU.max,
                                  negate=True), "dv")
            vd()
            wait(TEN, "dv")
            inc(TEN.matmul(psE[:, 4:5], oner[:], nm1[:], start=True,
                           stop=True), "mm")
            wait(VEC, "mm")
            inc(VEC.tensor_copy(negMb[:], psE[:, 4:5]), "dv")
            vd()
            inc(VEC.tensor_tensor(biasM[:], negMb[:], w0b[:, w0i:w0i + 1],
                                  ALU.add), "dv")
            vd()
            psOsl = [(psC, 0), (psD, 0), (psSm, 0), (psSm, 512)]
            for h in range(8):
                cc, hh = h // 2, (h % 2) * 64
                wait_all(TEN, "dv", "ac")
                for kt in range(8):
                    qlo = max(0, (kt - 4) * 128)
                    qhi = min(512, (kt + 1) * 128)
                    par = pv if kt < 4 else ow
                    tt = kt % 4
                    stp = psA if (kt & 1) == 0 else psB
                    pcol = 512 * (kt & 1)
                    wait_all(TEN, "ac", "dv")
                    inc(TEN.matmul(stp[:, 0:qhi - qlo],
                                   ab["kcb"][par][cc][hh:hh + 64,
                                                      tt * 128:(tt + 1) * 128],
                                   ab["qcb"][cc][hh:hh + 64, qlo:qhi],
                                   start=True, stop=True), "mm")
                    wait(VEC, "mm")
                    if kt <= 3:
                        d0 = kt * 128 - qlo
                        inc(VEC.tensor_tensor(stp[:, d0:d0 + 128],
                                              stp[:, d0:d0 + 128], masklo[:],
                                              ALU.add), "dv")
                    else:
                        inc(VEC.tensor_tensor(stp[:, 0:128], stp[:, 0:128],
                                              maskhi[:], ALU.add), "dv")
                    wait_all(ACT, "dv", "mm")
                    bias = biasM if kt <= 3 else negMb
                    inc(ACT.activation(ab["ptb"][:, pcol + qlo:pcol + qhi],
                                       stp[:, 0:qhi - qlo], AF.Exp,
                                       bias=bias[:], scale=1.0), "ac")
                    wait(TEN, "ac")
                    for qb in range(max(0, kt - 4), min(3, kt) + 1):
                        pt, c0 = psOsl[qb]
                        inc(TEN.matmul(
                            pt[:, c0:c0 + 65],
                            ab["ptb"][:, pcol + qb * 128:pcol + (qb + 1) * 128],
                            ab["vbf"][par][tt][:, h * 65:h * 65 + 65],
                            start=(kt == qb), stop=(kt == qb + 4)), "mm")
                wait(VEC, "mm")
                for qb in range(4):
                    pt, c0 = psOsl[qb]
                    inc(VEC.reciprocal(rnc[:], pt[:, c0 + 64:c0 + 65]), "dv")
                    vd()
                    inc(VEC.tensor_scalar(
                        ab["otb"][:, qb * 512 + h * 64:qb * 512 + h * 64 + 64],
                        pt[:, c0:c0 + 64], rnc[:], None, ALU.mult), "dv")
                    vd()
            for qb in range(4):
                for cc in range(4):
                    wait_all(TEN, "dv", "ac")
                    inc(TEN.transpose(psT[:, 0:128],
                                      ab["otb"][:, qb * 512 + cc * 128:
                                                qb * 512 + (cc + 1) * 128],
                                      identb[:]), "mm")
                    wait(VEC, "mm")
                    inc(VEC.tensor_copy(ab["ocb"][cc][:, qb * 128:(qb + 1) * 128],
                                        psT[:, 0:128]), "dv")
            e0 = 256 if (l == 0 and s == 0) else 0
            for co in range(8):
                pso = psA if (co & 1) == 0 else psB
                wait_all(TEN, "dv", "ac")
                for cc in range(4):
                    inc(TEN.matmul(pso[:, 0:512 - e0],
                                   ab["wob"][cc][:, co * 128:(co + 1) * 128],
                                   ab["ocb"][cc][:, e0:512],
                                   start=(cc == 0), stop=(cc == 3)), "mm")
                wait(VEC, "mm")
                c0 = qwt + 768 + e0
                c1 = qwt + 768 + 512
                if l == 0:
                    r = (s + 1) % 2
                    inc(VEC.tensor_tensor(
                        xst[co][:, c0:c1], pso[:, 0:512 - e0],
                        ab["xnr"][co][:, r * 516 + 1 + e0:r * 516 + 513],
                        ALU.add), "dv")
                else:
                    inc(VEC.tensor_tensor(xst[co][:, c0:c1],
                                          pso[:, 0:512 - e0],
                                          xst[co][:, c0:c1], ALU.add), "dv")

        def ffn(l, fb):
            E0 = -640 if l == 0 else 0
            E1 = 2048
            org = 768
            wait_all(SYN, "mm", "dv", "ac")
            wait_all(GPS, "mm", "dv", "ac")
            for dst, srcd in ((fb["b1b"], b_b1), (fb["b2b"], b_b2),
                              (fb["lngb"], b_lng), (fb["lnbb"], b_lnb)):
                wait(SYN, "dx")
                inc(SYN.dma_start(out=dst[:], in_=srcd[l]), "dx", 16)
            for j in range(22):
                wait(GPS, "dw")
                inc(GPS.dma_start(out=fb["w2b"][j][:],
                                  in_=b_w2[l][j * 128:(j + 1) * 128, :]),
                    "dw", 16)
            w1src = b_w1[l].rearrange("(c p) w -> p c w", p=128)
            ngr = (E1 - E0 + 511) // 512
            for g in range(ngr - 1, -1, -1):
                T0 = E0 + 512 * g
                W = min(512, E1 - T0)
                wait(GPS, "mm")
                wait(GPS, "dw")
                inc(GPS.dma_start(out=fb["w1b"][0][:],
                                  in_=w1src[:, :, 0:256]), "dw", 16)
                wait_all(TEN, "dv", "ac")
                stb1, sqb = fb["hin"][0], fb["gel"]
                for c in range(8):
                    d = 1 if c >= 4 else 0
                    wait_all(VEC, "mm")
                    inc(VEC.tensor_copy(stb1[:, 0:W],
                                        xst[c][:, T0 + org - d:
                                               T0 + org - d + W]), "dv")
                    vd()
                    inc(VEC.tensor_tensor(sqb[:, 0:W], stb1[:, 0:W],
                                          stb1[:, 0:W], ALU.mult), "dv")
                    wait(TEN, "dv")
                    inc(TEN.matmul(psSm[0:1, 0:W], onebf[:], stb1[:, 0:W],
                                   start=(c == 0), stop=(c == 7)), "mm")
                    inc(TEN.matmul(psSm[0:1, 512:512 + W], onebf[:],
                                   sqb[:, 0:W],
                                   start=(c == 0), stop=(c == 7)), "mm")
                wait(VEC, "mm")
                inc(VEC.tensor_scalar(rop1[0:1, 0:W], psSm[0:1, 0:W], 1.0 / D,
                                      None, ALU.mult), "dv")
                inc(VEC.tensor_scalar(rop2[0:1, 0:W], psSm[0:1, 512:512 + W],
                                      1.0 / D, None, ALU.mult), "dv")
                vd()
                inc(VEC.tensor_tensor(cst[0:1, 0:W], rop1[0:1, 0:W],
                                      rop1[0:1, 0:W], ALU.mult), "dv")
                vd()
                inc(VEC.tensor_tensor(rop2[0:1, 0:W], rop2[0:1, 0:W],
                                      cst[0:1, 0:W], ALU.subtract), "dv")
                vd()
                inc(VEC.tensor_scalar(rop2[0:1, 0:W], rop2[0:1, 0:W], 1.0e-5,
                                      None, ALU.add), "dv")
                wait(ACT, "dv")
                inc(ACT.activation(rop2[0:1, 0:W], rop2[0:1, 0:W], AF.Ln,
                                   scale=1.0), "ac")
                wait(VEC, "ac")
                inc(VEC.tensor_scalar(rop2[0:1, 0:W], rop2[0:1, 0:W], -0.5,
                                      None, ALU.mult), "dv")
                vd()
                wait(ACT, "dv")
                inc(ACT.activation(rop2[0:1, 0:W], rop2[0:1, 0:W], AF.Exp,
                                   scale=1.0), "ac")
                wait_all(TEN, "ac", "dv")
                inc(TEN.matmul(psD[:, 0:W], oner[:], rop1[0:1, 0:W],
                               start=True, stop=True), "mm")
                inc(TEN.matmul(psE[:, 0:W], oner[:], rop2[0:1, 0:W],
                               start=True, stop=True), "mm")
                wait(VEC, "mm")
                wait(ACT, "dx")
                for c in range(8):
                    d = 1 if c >= 4 else 0
                    wait(VEC, "ac")
                    inc(VEC.tensor_tensor(sqg[:, 0:W],
                                          xst[c][:, T0 + org - d:
                                                 T0 + org - d + W],
                                          psD[:, 0:W], ALU.subtract), "dv")
                    vd()
                    inc(VEC.tensor_tensor(sqg[:, 0:W], sqg[:, 0:W],
                                          psE[:, 0:W], ALU.mult), "dv")
                    wait(ACT, "dv")
                    inc(ACT.activation(fb["hin"][c][:, 0:W], sqg[:, 0:W],
                                       AF.Identity,
                                       scale=fb["lngb"][:, c:c + 1],
                                       bias=fb["lnbb"][:, c:c + 1]), "ac")
                    if l == 0 and g == 2 and c == 0 and "sqgt" in tap_out:
                        wait_all(GPS, "dv", "ac", "mm")
                        wait(GPS, "st")
                        inc(GPS.dma_start(out=tap_out["sqgt"][:, 0:512],
                                          in_=sqg[:, 0:512]), "st", 16)
                        wait(GPS, "st")
                        inc(GPS.dma_start(out=tap_out["sqgt"][:, 512:1024],
                                          in_=fb["lngb"][:].broadcast_to(
                                              [128, 8]) if False else
                                          fb["lngb"][:]), "st", 16) if False \
                            else None
                        wait(GPS, "st")
                        inc(GPS.dma_start(out=tap_out["sqgt"][:, 512:520],
                                          in_=fb["lngb"][:]), "st", 16)
                        wait(GPS, "st")
                        inc(GPS.dma_start(out=tap_out["sqgt"][:, 520:528],
                                          in_=fb["lnbb"][:]), "st", 16)
                        tap_sync()
                if l == 0 and g == 2 and "stats" in tap_out:
                    wait_all(GPS, "dv", "ac", "mm")
                    wait(GPS, "st")
                    inc(GPS.dma_start(out=tap_out["stats"][:, 0:W],
                                      in_=rop1[0:1, 0:W]), "st", 16)
                    wait(GPS, "st")
                    inc(GPS.dma_start(out=tap_out["stats"][:, 512:512 + W],
                                      in_=rop2[0:1, 0:W]), "st", 16)
                    tap_sync()
                if l == 0 and g == 2 and "hin" in tap_out:
                    wait_all(GPS, "dv", "ac", "mm")
                    for c in range(8):
                        wait(GPS, "st")
                        inc(GPS.dma_start(out=tap_out["hin"][:, c * 512:(c + 1) * 512],
                                          in_=fb["hin"][c][:]), "st", 16)
                    tap_sync()
                for j in range(22):
                    pp = j & 1
                    if j < 21:
                        wait(GPS, "mm")
                        wait(GPS, "dw")
                        inc(GPS.dma_start(
                            out=fb["w1b"][1 - pp][:],
                            in_=w1src[:, :, (j + 1) * 256:(j + 2) * 256]),
                            "dw", 16)
                    wait_all(TEN, "dw", "ac", "dv")
                    for c in range(8):
                        inc(TEN.matmul(psA[:, 0:W],
                                       fb["w1b"][pp][:, c * 256:c * 256 + 128],
                                       fb["hin"][c][:, 0:W],
                                       start=(c == 0), stop=(c == 7)), "mm")
                        inc(TEN.matmul(psB[:, 0:W],
                                       fb["w1b"][pp][:, c * 256 + 128:
                                                     c * 256 + 256],
                                       fb["hin"][c][:, 0:W],
                                       start=(c == 0), stop=(c == 7)), "mm")
                    wait_all(ACT, "mm", "dv")
                    inc(ACT.activation(fb["gel"][:, 0:W], psB[:, 0:W], AF.Gelu,
                                       bias=fb["b1b"][:, 2 * j + 1:2 * j + 2],
                                       scale=1.0), "ac")
                    inc(ACT.activation(rop1[:, 0:W], psA[:, 0:W], AF.Identity,
                                       bias=fb["b1b"][:, 2 * j:2 * j + 1],
                                       scale=1.0), "ac")
                    wait(VEC, "ac")
                    inc(VEC.tensor_tensor(fb["hb"][j][:, 0:W], rop1[:, 0:W],
                                          fb["gel"][:, 0:W], ALU.mult), "dv")
                if l == 0 and g == 2 and "hbt" in tap_out:
                    wait_all(GPS, "dv", "ac", "mm")
                    for j in range(4):
                        wait(GPS, "st")
                        inc(GPS.dma_start(out=tap_out["hbt"][:, j * 512:(j + 1) * 512],
                                          in_=fb["hb"][j][:]), "st", 16)
                    tap_sync()
                for od in range(8):
                    wait_all(TEN, "dv", "ac", "dw")
                    for j in range(22):
                        inc(TEN.matmul(psC[:, 0:W],
                                       fb["w2b"][j][:, od * 128:(od + 1) * 128],
                                       fb["hb"][j][:, 0:W],
                                       start=(j == 0), stop=(j == 21)), "mm")
                    wait_all(ACT, "mm", "dv")
                    inc(ACT.activation(rop2[:, 0:W], psC[:, 0:W], AF.Identity,
                                       bias=fb["b2b"][:, od:od + 1],
                                       scale=1.0), "ac")
                    wait(VEC, "ac")
                    inc(VEC.tensor_tensor(xst[od][:, T0 + org:T0 + org + W],
                                          xst[od][:, T0 + org:T0 + org + W],
                                          rop2[:, 0:W], ALU.add), "dv")
                    vd()

        w0b = sb("w0b", [128, NS1 + NS2], F32)
        wait_all(TEN, "dx", "dv", "ac")
        inc(TEN.matmul(psE[:, 0:NS1 + NS2], oner[:], w0r[:],
                       start=True, stop=True), "mm")
        wait(VEC, "mm")
        inc(VEC.tensor_copy(w0b[:], psE[:, 0:NS1 + NS2]), "dv")

        def local_layer(l):
            ab, ctxs = mk_attn_bufs(l)
            wait_all(GPS, "mm", "dv", "ac", "st")
            for c in range(8):
                wait(GPS, "dw")
                inc(GPS.dma_start(out=ab["wqb"][c][:],
                                  in_=b_wqkv[l][c * 128:(c + 1) * 128, :]),
                    "dw", 16)
            for cc in range(4):
                wait(GPS, "dw")
                inc(GPS.dma_start(out=ab["wob"][cc][:],
                                  in_=b_wo[l][cc * 128:(cc + 1) * 128, :]),
                    "dw", 16)
            for dst, srcd in ((ab["qsl"], b_qs), (ab["ksl"], b_ks)):
                wait_all(SYN, "mm", "dv")
                wait(SYN, "dx")
                inc(SYN.dma_start(out=rop1[0:1, 0:INNER], in_=srcd[l]), "dx", 16)
                wait_all(TEN, "dx", "dv", "ac")
                inc(TEN.matmul(psE[:, 0:INNER], oner[:], rop1[0:1, 0:INNER],
                               start=True, stop=True), "mm")
                wait(VEC, "mm")
                inc(VEC.tensor_copy(dst[:], psE[:, 0:INNER]), "dv")
            wait(VEC, "dw")
            for p in range(2):
                for t in range(4):
                    inc(VEC.memset(ab["vbf"][p][t][:].rearrange(
                        "p (h d) -> p h d", h=8, d=65)[:, :, 64:65], 1.0), "dv")
            wait_all(TEN, "dw", "dx", "dv")
            nsl = NS1 if l == 0 else NS2
            kv_build(l, ab, 0, False)
            for s in range(nsl):
                kv_build(l, ab, s + 1, True)
                attn_slot(l, ab, s)
                if s == 0 and l == 1 and "otb1" in tap_out:
                    wait_all(GPS, "dv", "ac", "mm")
                    wait(GPS, "st")
                    inc(GPS.dma_start(out=tap_out["otb1"], in_=ab["otb"][:]),
                        "st", 16)
                    wait(GPS, "st")
                    inc(GPS.dma_start(out=tap_out["negm1"][:, 0:1],
                                      in_=negMb[:],
                                      allow_slow_non_contiguous=True), "st", 16)
                    for c in range(8):
                        wait(GPS, "st")
                        inc(GPS.dma_start(
                            out=tap_out["stb1t"][:, c * 520:(c + 1) * 520],
                            in_=ab["stb"][c][:]), "st", 16)
                    tap_sync()
                if s == 0 and l == 0:
                    if "otb" in tap_out:
                        wait_all(GPS, "dv", "ac", "mm")
                        wait(GPS, "st")
                        inc(GPS.dma_start(out=tap_out["otb"], in_=ab["otb"][:]),
                            "st", 16)
                        tap_sync()
                    if "qcb" in tap_out:
                        wait_all(GPS, "dv", "ac", "mm")
                        for cc in range(4):
                            wait(GPS, "st")
                            inc(GPS.dma_start(
                                out=tap_out["qcb"][:, cc * 512:(cc + 1) * 512],
                                in_=ab["qcb"][cc][:]), "st", 16)
                        for cc in range(4):
                            wait(GPS, "st")
                            inc(GPS.dma_start(
                                out=tap_out["qcb"][:, 2048 + cc * 512:
                                                 2048 + (cc + 1) * 512],
                                in_=ab["kcb"][0][cc][:]), "st", 16)
                        for cc in range(4):
                            wait(GPS, "st")
                            inc(GPS.dma_start(
                                out=tap_out["qcb"][:, 4096 + cc * 512:
                                                 4096 + (cc + 1) * 512],
                                in_=ab["kcb"][1][cc][:]), "st", 16)
                        tap_sync()
                    if "negm" in tap_out:
                        wait_all(GPS, "dv", "ac", "mm")
                        wait(GPS, "st")
                        inc(GPS.dma_start(out=tap_out["negm"][:, 0:1],
                                          in_=negMb[:],
                                          allow_slow_non_contiguous=True),
                            "st", 16)
                        wait(GPS, "st")
                        inc(GPS.dma_start(out=tap_out["negm"][:, 1:2],
                                          in_=mxc[:],
                                          allow_slow_non_contiguous=True),
                            "st", 16)
                        tap_sync()
                    if "xnr" in tap_out:
                        wait_all(GPS, "dv", "ac", "mm")
                        for c in range(8):
                            wait(GPS, "st")
                            inc(GPS.dma_start(
                                out=tap_out["xnr"][:, c * 1032:(c + 1) * 1032],
                                in_=ab["xnr"][c][:]), "st", 16)
                        tap_sync()
            if l == 1 and "negm1" in tap_out:
                pass
            if l == 1 and "pa1" in tap_out:
                wait_all(GPS, "dv", "ac", "mm")
                for c in range(8):
                    wait(GPS, "st")
                    inc(GPS.dma_start(out=tap_out["pa1"][:, c * X1N:(c + 1) * X1N],
                                      in_=xst[c][:]), "st", 16)
                tap_sync()
            if l == 0 and "pa0" in tap_out:
                wait_all(GPS, "dv", "ac", "mm")
                for c in range(8):
                    wait(GPS, "st")
                    inc(GPS.dma_start(out=tap_out["pa0"][:, c * X1N:(c + 1) * X1N],
                                      in_=xst[c][:]), "st", 16)
                tap_sync()
            freel(ctxs)
            fb, fctxs = mk_ffn_bufs(l)
            ffn(l, fb)
            freel(fctxs)

        def tap_state(nm):
            if nm in tap_out:
                wait_all(GPS, "dv", "ac", "mm")
                for c in range(8):
                    wait(GPS, "st")
                    inc(GPS.dma_start(out=tap_out[nm][:, c * X1N:(c + 1) * X1N],
                                      in_=xst[c][:]), "st", 16)
                tap_sync()

        local_layer(0)
        if upto == "l1":
            tap_state("xst")
            finish_stub()
            return nc
        local_layer(1)

        for ti in range(16):
            for c in range(8):
                wait_all(TEN, "ac", "dv")
                if c == 0:
                    wait(TEN, "st")
                    wait(ACT, "st")
                inc(TEN.transpose(psA[:, 0:128],
                                  xst[c][:, 768 + ti * 128:768 + (ti + 1) * 128],
                                  identf[:]), "mm")
                wait(ACT, "mm")
                inc(ACT.activation(gstage[:, c * 128:(c + 1) * 128],
                                   psA[:, 0:128], AF.Silu, scale=1.0), "ac")
            wait(SYN, "ac")
            wait(SYN, "st")
            inc(SYN.dma_start(out=g_dram[:, ti * 1024:(ti + 1) * 1024],
                              in_=gstage[:]), "st", 16)

        if "g" in tap_out:
            wait_all(GPS, "st", "ac")
            wait(GPS, "st")
            inc(GPS.dma_start(out=tap_out["g"], in_=g_dram), "st", 16)
            tap_sync()
        if upto in ("l2", "lb"):
            tap_state("xst")
            finish_stub()
            return nc

        for cm_ in reversed(_tb_ctxs):
            cm_.__exit__(None, None, None)
        for cm_ in reversed(_xst_ctxs):
            cm_.__exit__(None, None, None)
        _actx = []
        def aa(name, shape, dt):
            t, cm_ = alc(name + "A", shape, dt)
            _actx.append(cm_)
            return t
        xnf = [aa(f"xnf{c}", [128, L], BF16) for c in range(8)]
        acs = aa("acs", [128, L], F32)
        asn = aa("asn", [128, L], F32)
        oall = [aa(f"oall{c}", [128, OWN], BF16) for c in range(8)]
        onesA = aa("onesA", [128, 128], F32)
        permb = aa("permb", [128, 128], BF16)
        shf64b = aa("shf64b", [128, 128], BF16)
        _pctx = []
        def pa(name, shape, dt):
            t, cm_ = alc(name + "P", shape, dt)
            _pctx.append(cm_)
            return t
        K2 = pa("K2", [128, L], BF16)
        Q2 = pa("Q2", [128, OWN], BF16)
        Vc = pa("Vc", [128, L], BF16)
        vbh = [pa(f"vbh{h}", [128, 32 * 65], BF16) for h in range(2)]
        ptA = pa("ptA", [128, 1024], BF16)
        wkp = pa("wkp", [128, 1024], BF16)
        wqp = pa("wqp", [128, 1024], BF16)
        wvp = pa("wvp", [128, 1024], BF16)
        xnrA = pa("xnrA", [128, D], BF16)
        repA = pa("repA", [128, 128], F32)
        ivA = pa("ivA", [128, 32], F32)
        ivlA = pa("ivlA", [128, 32], F32)
        posA = pa("posA", [128, 32], F32)
        tmt = pa("tmt", [128, 32], F32)
        tml = pa("tml", [128, 32], F32)
        t32 = pa("t32", [128, 128], F32)
        ost = pa("ost", [128, 512], BF16)
        rowin2 = pa("rowin2", [1, 32], F32)
        kab = pa("kab", [128, L], BF16)
        qab = pa("qab", [128, OWN], BF16)
        nsc = pa("nsc", [1, 4], F32)

        wait_all(GPS, "mm", "dv", "ac", "st")
        wait_all(SYN, "mm", "dv", "ac", "st")
        wait_all(VEC, "mm", "ac")
        wait_all(TEN, "dv", "ac")
        wait(SYN, "dx")
        inc(SYN.dma_start(out=onesA[:], in_=allones), "dx", 16)
        wait(SYN, "dx")
        inc(SYN.dma_start(out=repA[:], in_=repm), "dx", 16)
        wait(SYN, "dx")
        inc(SYN.dma_start(out=posA[:], in_=poska), "dx", 16)
        wait(GPS, "dw")
        inc(GPS.dma_start(out=permb[:], in_=perm_a), "dw", 16)
        wait(GPS, "dw")
        inc(GPS.dma_start(out=shf64b[:], in_=shf64), "dw", 16)
        wait(SYN, "dx")
        inc(SYN.dma_start(out=rowin2[:], in_=invf_hi), "dx", 16)
        inc(VEC.memset(ost[:], 0.0), "dv")
        wait_all(TEN, "dx", "dv", "ac")
        inc(TEN.matmul(psA[:, 0:32], oner[:], rowin2[:], start=True, stop=True),
            "mm")
        wait(VEC, "mm")
        inc(VEC.tensor_copy(ivA[:], psA[:, 0:32]), "dv")
        wait(SYN, "mm")
        wait(SYN, "dx")
        inc(SYN.dma_start(out=rowin2[:], in_=invf_lo), "dx", 16)
        wait_all(TEN, "dx", "dv")
        inc(TEN.matmul(psA[:, 0:32], oner[:], rowin2[:], start=True, stop=True),
            "mm")
        wait(VEC, "mm")
        inc(VEC.tensor_copy(ivlA[:], psA[:, 0:32]), "dv")
        vd()

        for ti in range(32):
            pc = posA[:, ti:ti + 1]
            wait(VEC, "ac")
            inc(VEC.tensor_scalar(tmt[:], ivA[:], pc, None, ALU.mult), "dv")
            inc(VEC.tensor_scalar(tml[:], ivlA[:], pc, None, ALU.mult), "dv")
            vd()
            for tbl, ph in ((asn, None), (acs, HFPI)):
                rrsin(cst[:, 576:608], tmt[:], 32, phase=ph, extra=tml[:])
                wait_all(TEN, "ac", "dv")
                inc(TEN.transpose(psSm[0:32, 0:128], cst[:, 576:608],
                                  identf[:]), "mm")
                wait(VEC, "mm")
                inc(VEC.tensor_copy(t32[0:32, 0:128], psSm[0:32, 0:128]), "dv")
                wait(TEN, "dv")
                inc(TEN.matmul(psSm[:, 128:256], repA[0:32, :], t32[0:32, :],
                               start=True, stop=True), "mm")
                wait(VEC, "mm")
                inc(VEC.tensor_copy(tbl[:, ti * 128:(ti + 1) * 128],
                                    psSm[:, 128:256]), "dv")

        for ti in range(32):
            wait(SYN, "ac")
            wait(SYN, "dx")
            inc(SYN.dma_start(out=xrow[:], in_=x_full[ti * 128:(ti + 1) * 128, :]),
                "dx", 16)
            wait_all(VEC, "dx", "ac")
            ln_tm(xnrA[:], xrow[:])
            tmcm(xnf, ti * 128, xnrA)

        wait(VEC, "dv")
        for h in range(2):
            inc(VEC.memset(vbh[h][:].rearrange("p (t d) -> p t d", t=32,
                                               d=65)[:, :, 64:65], 1.0), "dv")

        wsrc = a_wqkv.rearrange("(c p) e -> p c e", p=128)
        for pr in range(8):
            wait_all(GPS, "mm", "dv")
            wait(GPS, "dw")
            inc(GPS.dma_start(out=wqp[:].rearrange("p (c e) -> p c e", c=8),
                              in_=wsrc[:, :, 128 * pr:128 * pr + 128]),
                "dw", 16)
            wait(GPS, "dw")
            inc(GPS.dma_start(out=wkp[:].rearrange("p (c e) -> p c e", c=8),
                              in_=wsrc[:, :, 1024 + 128 * pr:1024 + 128 * pr + 128]),
                "dw", 16)
            wait(GPS, "dw")
            inc(GPS.dma_start(out=wvp[:].rearrange("p (c e) -> p c e", c=8),
                              in_=wsrc[:, :, 2048 + 128 * pr:2048 + 128 * pr + 128]),
                "dw", 16)
            wait_all(TEN, "dw", "dv", "ac")
            for sp in range(8):
                s0 = sp * 512
                wait_all(TEN, "ac", "dv")
                for c in range(8):
                    inc(TEN.matmul(psA[:, 0:512], wkp[:, c * 128:(c + 1) * 128],
                                   xnf[c][:, s0:s0 + 512],
                                   start=(c == 0), stop=(c == 7)), "mm")
                    inc(TEN.matmul(psB[:, 0:512], wvp[:, c * 128:(c + 1) * 128],
                                   xnf[c][:, s0:s0 + 512],
                                   start=(c == 0), stop=(c == 7)), "mm")
                wait_all(ACT, "mm", "dv")
                inc(ACT.copy(K2[:, s0:s0 + 512], psA[:, 0:512]), "ac")
                inc(ACT.copy(Vc[:, s0:s0 + 512], psB[:, 0:512]), "ac")
                wait_all(TEN, "ac", "dv")
                inc(TEN.matmul(psC[:, 0:512], permb[:], K2[:, s0:s0 + 512],
                               start=True, stop=True), "mm")
                wait_all(VEC, "mm", "ac")
                inc(VEC.tensor_tensor(rop1[:], K2[:, s0:s0 + 512],
                                      acs[:, s0:s0 + 512], ALU.mult), "dv")
                inc(VEC.tensor_tensor(rop2[:], psC[:, 0:512],
                                      asn[:, s0:s0 + 512], ALU.mult), "dv")
                vd()
                inc(VEC.tensor_tensor(K2[:, s0:s0 + 512], rop1[:], rop2[:],
                                      ALU.add), "dv")
            for sp in range(4):
                s0 = sp * 512
                wait_all(TEN, "ac", "dv")
                for c in range(8):
                    inc(TEN.matmul(psA[:, 0:512], wqp[:, c * 128:(c + 1) * 128],
                                   xnf[c][:, s0:s0 + 512],
                                   start=(c == 0), stop=(c == 7)), "mm")
                wait_all(ACT, "mm", "dv")
                inc(ACT.copy(Q2[:, s0:s0 + 512], psA[:, 0:512]), "ac")
                wait_all(TEN, "ac", "dv")
                inc(TEN.matmul(psC[:, 0:512], permb[:], Q2[:, s0:s0 + 512],
                               start=True, stop=True), "mm")
                wait_all(VEC, "mm", "ac")
                inc(VEC.tensor_tensor(rop1[:], Q2[:, s0:s0 + 512],
                                      acs[:, s0:s0 + 512], ALU.mult), "dv")
                inc(VEC.tensor_tensor(rop2[:], psC[:, 0:512],
                                      asn[:, s0:s0 + 512], ALU.mult), "dv")
                vd()
                inc(VEC.tensor_tensor(Q2[:, s0:s0 + 512], rop1[:], rop2[:],
                                      ALU.add), "dv")
            for tt in range(32):
                wait_all(TEN, "dv", "ac")
                inc(TEN.transpose(psT[:, 0:128], Vc[:, tt * 128:(tt + 1) * 128],
                                  identb[:]), "mm")
                wait(VEC, "mm")
                inc(VEC.tensor_copy(vbh[0][:, tt * 65:tt * 65 + 64],
                                    psT[:, 0:64]), "dv")
                inc(VEC.tensor_copy(vbh[1][:, tt * 65:tt * 65 + 64],
                                    psT[:, 64:128]), "dv")
            for h01 in range(2):
                hh = h01 * 64
                wait_all(SYN, "mm", "ac", "dv")
                wait(SYN, "dx")
                inc(SYN.dma_start(out=kab[0:64, :], in_=K2[hh:hh + 64, :]),
                    "dx", 16)
                wait(SYN, "dx")
                inc(SYN.dma_start(out=qab[0:64, :], in_=Q2[hh:hh + 64, :]),
                    "dx", 16)
                wait_all(VEC, "mm", "ac")
                inc(VEC.memset(kab[64:65, :], 1.0), "dv")
                inc(VEC.memset(nsc[:, 0:1], -1.0), "dv")
                for qt in range(16):
                    for sp in range(8):
                        s0 = sp * 512
                        wait_all(TEN, "dx", "dv", "ac")
                        inc(TEN.matmul(psA[:, 0:512],
                                       qab[0:64, qt * 128:(qt + 1) * 128],
                                       kab[0:64, s0:s0 + 512],
                                       start=True, stop=True), "mm")
                        wait(VEC, "mm")
                        inc(VEC.tensor_scalar(
                            rop1[:, 0:512], psA[:, 0:512], 0.0,
                            None if sp == 0 else mxc[:],
                            ALU.add, ALU.max, accum_out=mxc[:]), "dv")
                        vd()
                    wait_all(TEN, "dv", "ac")
                    inc(TEN.transpose(psC[0:1, 0:128], mxc[:], identf[:]),
                        "mm")
                    wait(VEC, "mm")
                    inc(VEC.tensor_copy(rop1[0:1, 0:128], psC[0:1, 0:128]),
                        "dv")
                    vd()
                    wait(TEN, "dv")
                    inc(TEN.matmul(psC[64:65, 0:128], nsc[:, 0:1],
                                   rop1[0:1, 0:128], start=True, stop=True),
                        "mm")
                    wait(ACT, "mm")
                    inc(ACT.copy(qab[64:65, qt * 128:(qt + 1) * 128],
                                 psC[64:65, 0:128]), "ac")
                wait_all(TEN, "dx", "ac", "dv")
                if pr == 0 and h01 == 0 and "adbg" in tap_out:
                    wait_all(GPS, "dx", "ac", "mm", "dv")
                    wait(GPS, "st")
                    inc(GPS.dma_start(out=tap_out["adbg"][0:128, 0:2048],
                                      in_=qab[:]), "st", 16)
                    wait(GPS, "st")
                    inc(GPS.dma_start(out=tap_out["adbg"][0:128, 2048:6144],
                                      in_=kab[:]), "st", 16)
                    tap_sync()
                Ksl = kab[0:65, :]
                Qsl = qab[0:65, :]
                for qs4 in range(4):
                    q0 = qs4 * 512
                    for kc in range(16):
                        wait_all(TEN, "ac", "dv")
                        for k2_ in range(2):
                            kt = kc * 2 + k2_
                            inc(TEN.matmul(psSm[:, k2_ * 512:k2_ * 512 + 512],
                                           Ksl[:, kt * 128:(kt + 1) * 128],
                                           Qsl[:, q0:q0 + 512],
                                           start=True, stop=True), "mm")
                        wait_all(ACT, "mm", "dv")
                        inc(ACT.activation(ptA[:, 0:1024], psSm[:, 0:1024],
                                           AF.Exp, scale=0.125), "ac")
                        wait(TEN, "ac")
                        for k2_ in range(2):
                            kt = kc * 2 + k2_
                            inc(TEN.matmul(psE[0:65, 0:512],
                                           vbh[h01][:, kt * 65:kt * 65 + 65],
                                           ptA[:, k2_ * 512:k2_ * 512 + 512],
                                           start=(kt == 0), stop=(kt == 31)),
                                "mm")
                    wait(VEC, "mm")
                    if pr == 0 and h01 == 0 and qs4 == 0 and "adn" in tap_out:
                        wait_all(GPS, "mm", "ac", "dv")
                        inc(VEC.tensor_copy(rop2[0:65, 0:512],
                                            psE[0:65, 0:512]), "dv")
                        wait(GPS, "dv")
                        wait(GPS, "st")
                        inc(GPS.dma_start(out=tap_out["adn"][0:65, 0:512],
                                          in_=rop2[0:65, 0:512]), "st", 16)
                        wait(GPS, "st")
                        inc(GPS.dma_start(out=tap_out["adn"][0:128, 512:1024],
                                          in_=ptA[:, 0:512]), "st", 16)
                        tap_sync()
                    inc(VEC.tensor_copy(rop1[64:65, 0:512], psE[64:65, 0:512]),
                        "dv")
                    vd()
                    inc(VEC.reciprocal(rop1[64:65, 0:512], rop1[64:65, 0:512]),
                        "dv")
                    vd()
                    wait_all(TEN, "dv", "ac")
                    inc(TEN.matmul(psSm[:, 0:512], onesA[64:65, :],
                                   rop1[64:65, 0:512], start=True, stop=True),
                        "mm")
                    wait(VEC, "mm")
                    inc(VEC.tensor_copy(rop2[0:64, 0:512], psSm[0:64, 0:512]),
                        "dv")
                    vd()
                    inc(VEC.tensor_tensor(ost[0:64, 0:512], psE[0:64, 0:512],
                                          rop2[0:64, 0:512], ALU.mult), "dv")
                    vd()
                    if h01 == 0:
                        inc(VEC.tensor_copy(oall[pr][0:64, q0:q0 + 512],
                                            ost[0:64, 0:512]), "dv")
                    else:
                        wait_all(TEN, "dv", "ac")
                        inc(TEN.matmul(psD[:, 0:512], shf64b[:],
                                       ost[:, 0:512], start=True, stop=True),
                            "mm")
                        wait(VEC, "mm")
                        inc(VEC.tensor_copy(oall[pr][64:128, q0:q0 + 512],
                                            psD[64:128, 0:512]), "dv")

        if "atab" in tap_out:
            wait_all(GPS, "dv", "ac", "mm")
            wait(GPS, "st")
            inc(GPS.dma_start(out=tap_out["atab"][:, 0:L], in_=acs[:]), "st", 16)
            wait(GPS, "st")
            inc(GPS.dma_start(out=tap_out["atab"][:, L:2 * L], in_=asn[:]),
                "st", 16)
            for c in range(8):
                wait(GPS, "st")
                inc(GPS.dma_start(out=tap_out["atab"][:, 2 * L + c * OWN:
                                                      2 * L + (c + 1) * OWN],
                                  in_=oall[c][:]), "st", 16)
            tap_sync()
        freel(_pctx)
        wob2 = [aa(f"wob2{c}", [128, D], BF16) for c in range(8)]
        wait_all(GPS, "mm", "dv", "ac")
        for c in range(8):
            wait(GPS, "dw")
            inc(GPS.dma_start(out=wob2[c][:],
                              in_=a_wout[c * 128:(c + 1) * 128, :]), "dw", 16)
        for tti in range(16):
            wait(SYN, "dv")
            wait(VEC, "st")
            wait(SYN, "dx")
            inc(SYN.dma_start(out=xrow[:],
                              in_=x_full[tti * 128:(tti + 1) * 128, :]),
                "dx", 16)
            wait(SYN, "dx")
            inc(SYN.dma_start(out=gstage[:],
                              in_=g_dram[:, tti * 1024:(tti + 1) * 1024]),
                "dx", 16)
            for dh in range(2):
                pso = psA if dh == 0 else psB
                wait_all(TEN, "dw", "dv", "ac")
                for c in range(8):
                    inc(TEN.matmul(pso[:, 0:512],
                                   oall[c][:, tti * 128:(tti + 1) * 128],
                                   wob2[c][:, dh * 512:dh * 512 + 512],
                                   start=(c == 0), stop=(c == 7)), "mm")
                wait_all(VEC, "mm", "dx")
                inc(VEC.tensor_tensor(sqg[:, dh * 512:dh * 512 + 512],
                                      pso[:, 0:512],
                                      gstage[:, dh * 512:dh * 512 + 512],
                                      ALU.mult), "dv")
                vd()
                inc(VEC.tensor_tensor(sqg[:, dh * 512:dh * 512 + 512],
                                      sqg[:, dh * 512:dh * 512 + 512],
                                      xrow[:, dh * 512:dh * 512 + 512],
                                      ALU.add), "dv")
            wait(SYN, "dv")
            wait(SYN, "st")
            inc(SYN.dma_start(out=out[tti * 128:(tti + 1) * 128, :],
                              in_=sqg[:]), "st", 16)
        wait(SYN, "st")
        wait(GPS, "st")
        freel(_actx)
        return nc


def _host_inputs(L0_attn_x, L0_attn_w_qkv, L0_attn_w_out, L0_ffn_wqkv,
                 L0_ffn_q_scale, L0_ffn_k_scale, L0_ffn_wo, L0_ffn_ln_g,
                 L0_ffn_ln_b, L0_ffn_w1, L0_ffn_b1, L0_ffn_w2, L0_ffn_b2):
    import ml_dtypes
    x = np.asarray(L0_attn_x, dtype=np.float32)
    f32 = lambda a: np.ascontiguousarray(a, dtype=np.float32)

    a_wqkvT = f32(np.asarray(L0_attn_w_qkv).T)
    a_woutT = f32(np.asarray(L0_attn_w_out).T)
    b_wqkv = f32(L0_ffn_wqkv)
    b_wo = f32(L0_ffn_wo)
    w1 = np.asarray(L0_ffn_w1, dtype=np.float32)
    b1 = np.asarray(L0_ffn_b1, dtype=np.float32)
    w1p = np.zeros((2, D, W1C), dtype=np.float32)
    b1p = np.zeros((2, W1C, 1), dtype=np.float32)
    for j in range(HIDP // 128):
        a0, a1 = j * 128, min((j + 1) * 128, HID)
        n = a1 - a0
        if n > 0:
            w1p[:, :, 256 * j:256 * j + n] = w1[:, :, a0:a1]
            w1p[:, :, 256 * j + 128:256 * j + 128 + n] = w1[:, :, HID + a0:HID + a1]
            b1p[:, 256 * j:256 * j + n, 0] = b1[:, a0:a1]
            b1p[:, 256 * j + 128:256 * j + 128 + n, 0] = b1[:, HID + a0:HID + a1]
    b1p = f32(b1p[:, :, 0].reshape(2, W1C // 128, 128).transpose(0, 2, 1))
    w2p = np.zeros((2, HIDP, D), dtype=np.float32)
    w2p[:, :HID, :] = np.asarray(L0_ffn_w2, dtype=np.float32)
    b_b2 = f32(np.asarray(L0_ffn_b2).reshape(2, 8, 128).transpose(0, 2, 1))
    b_lng = f32(np.asarray(L0_ffn_ln_g).reshape(2, 8, 128).transpose(0, 2, 1))
    b_lnb = f32(np.asarray(L0_ffn_ln_b).reshape(2, 8, 128).transpose(0, 2, 1))
    b_qs = f32(np.tile(np.asarray(L0_ffn_q_scale), (1, HEADS_B))[:, None, :])
    b_ks = f32(np.tile(np.asarray(L0_ffn_k_scale), (1, HEADS_B))[:, None, :])

    _hi = 2.0 * np.arange(32).astype(np.float64)
    _invf64 = 1.0 / 10000.0 ** (_hi / 64.0)
    invf_hi_h = _invf64.astype(np.float32)
    invf_lo_h = (_invf64 - invf_hi_h.astype(np.float64)).astype(np.float32)
    _lnb64 = np.log((_hi + 0.4 * 64.0) / (1.4 * 64.0))
    lnbs_hi_h = _lnb64.astype(np.float32)
    lnbs_lo_h = (_lnb64 - lnbs_hi_h.astype(np.float64)).astype(np.float32)
    onecol = f32(np.ones((128, 1)))
    ident = np.eye(128, dtype=np.float32)
    ident_bf = ident.astype(ml_dtypes.bfloat16)
    P = np.zeros((128, 128), dtype=np.float32)
    for h0 in (0, 64):
        for d in range(32):
            P[h0 + d, h0 + d + 32] = -1.0
            P[h0 + d + 32, h0 + d] = 1.0
    perm_a = np.ascontiguousarray(P.T).astype(ml_dtypes.bfloat16)
    il = np.arange(128)
    mask_lo = np.where(il[:, None] < il[None, :], NEG,
                       0.0).astype(ml_dtypes.bfloat16)
    mask_hi = np.where(il[:, None] > il[None, :], NEG,
                       0.0).astype(ml_dtypes.bfloat16)

    onerow_h = f32(np.ones((1, 128)))
    allones_h = f32(np.ones((128, 128)))
    repm_h = f32((np.arange(128)[None, :] % 32 == np.arange(128)[:, None]) * 1.0)
    shf64_h = (np.arange(128)[None, :] == np.arange(128)[:, None] + 64
               ).astype(ml_dtypes.bfloat16)
    shared = dict(onerow=onerow_h, allones=allones_h, repm=repm_h,
                  shf64=shf64_h,
                  invf_hi=f32(invf_hi_h[None, :]), invf_lo=f32(invf_lo_h[None, :]),
                  lnbs_hi=f32(lnbs_hi_h[None, :]), lnbs_lo=f32(lnbs_lo_h[None, :]),
                  a_wqkv=a_wqkvT, a_wout=a_woutT, b_wqkv=b_wqkv, b_wo=b_wo,
                  b_w1=w1p, b_b1=b1p, b_w2=w2p, b_b2=b_b2, b_lng=b_lng,
                  b_lnb=b_lnb, b_qs=b_qs, b_ks=b_ks, onecol=onecol, ident_bf=ident_bf,
                  ident_f32=ident, perm_a=perm_a, mask_lo=mask_lo,
                  mask_hi=mask_hi)

    in_maps = []
    for c in range(NCORES):
        b, half = c // 2, c % 2
        o = OWN * half
        oth = OWN * (1 - half)
        x_fullm = np.ascontiguousarray(
            np.concatenate([x[b, o:o + OWN], x[b, oth:oth + OWN]], axis=0))
        x_bm = np.zeros((RBN, D), dtype=np.float32)
        lo = max(0, o - HALO)
        x_bm[HALO - (o - lo):] = x[b, lo:o + OWN]
        poskv = np.concatenate([np.arange(o, o + OWN),
                                np.arange(oth, oth + OWN)]).astype(np.float32)
        posk = np.ascontiguousarray(poskv[None, :])
        poska_h = np.ascontiguousarray(poskv.reshape(32, 128).T)
        posbm = np.ascontiguousarray(np.arange(o - HALO, o + OWN, dtype=np.float32).reshape(RBT, 128).T)
        w0 = np.zeros((1, NS1 + NS2), dtype=np.float32)
        if half == 0:
            w0[0, 2] = NEG
            w0[0, NS1 + 0] = NEG
        m = dict(shared)
        m.update(x_full=x_fullm, x_b=np.ascontiguousarray(x_bm), posk=posk,
                 poska=poska_h, posb=posbm, w0row=w0)
        in_maps.append(m)
    return in_maps


def kernel(**inputs):
    in_maps = _host_inputs(**inputs)
    nc = _build()
    res = run_bass_kernel_spmd(nc, in_maps, core_ids=list(range(NCORES)))
    out = np.empty((BATCH, L, D), dtype=np.float32)
    for c in range(NCORES):
        b, half = c // 2, c % 2
        out[b, OWN * half:OWN * (half + 1)] = res.results[c]["out"]
    return out


# version 08
# speedup vs baseline: 1.0036x; 1.0036x over previous
import math
from contextlib import ExitStack

import numpy as np

import concourse.bass as bass
import concourse.mybir as mybir
from concourse.bass_utils import run_bass_kernel_spmd

F32 = mybir.dt.float32
BF16 = mybir.dt.bfloat16
I32 = mybir.dt.int32
AF = mybir.ActivationFunctionType
ALU = mybir.AluOpType
AX = mybir.AxisListType

D = 1024
L = 4096
BATCH = 4
NCORES = 8
OWN = 2048
HALO = 1664
RBN = HALO + OWN
RBT = RBN // 128
X1L = 768
X1N = X1L + OWN
HEADS_B = 8
DH = 64
INNER = 512
HID = 2730
HIDP = 2816
W1C = 5632
HEADS_A = 16
WIN = 512
NEG = -30000.0
NS1, NS2 = 6, 4

def _split_2pi():
    tp = 2.0 * math.pi
    def trunc12(v):
        f = np.float32(v)
        u = f.view(np.uint32) & np.uint32(0xFFFFF000)
        return float(u.view(np.float32))
    c1 = trunc12(tp)
    c2 = trunc12(tp - c1)
    c3 = float(np.float32(tp - c1 - c2))
    return c1, c2, c3

C2PI_1, C2PI_2, C2PI_3 = _split_2pi()
PI_LO = float(np.nextafter(np.float32(math.pi), np.float32(0.0)))


def _build(upto="all", taps=()):
    nc = bass.Bass("TRN2", target_bir_lowering=False)

    def di(name, shape, dt=F32):
        return nc.dram_tensor(name, shape, dt, kind="ExternalInput").ap()

    x_full = di("x_full", [L, D])
    x_b = di("x_b", [RBN, D])
    posk = di("posk", [1, L])
    posb = di("posb", [128, RBT])
    invf_hi = di("invf_hi", [1, 32])
    invf_lo = di("invf_lo", [1, 32])
    lnbs_hi = di("lnbs_hi", [1, 32])
    lnbs_lo = di("lnbs_lo", [1, 32])
    onecol = di("onecol", [128, 1])
    onerow = di("onerow", [1, 128])
    allones = di("allones", [128, 128])
    repm = di("repm", [128, 128])
    shf64 = di("shf64", [128, 128], BF16)
    poska = di("poska", [128, 32])
    ident_bf = di("ident_bf", [128, 128], BF16)
    ident_f32 = di("ident_f32", [128, 128])
    perm_a = di("perm_a", [128, 128], BF16)
    mask_lo = di("mask_lo", [128, 128], BF16)
    mask_hi = di("mask_hi", [128, 128], BF16)
    w0row = di("w0row", [1, NS1 + NS2])
    a_wqkv = di("a_wqkv", [D, 3 * D])
    a_wout = di("a_wout", [D, D])
    b_wqkv = di("b_wqkv", [2, D, 3 * INNER])
    b_wo = di("b_wo", [2, INNER, D])
    b_w1 = di("b_w1", [2, D, W1C])
    b_b1 = di("b_b1", [2, 128, W1C // 128])
    b_w2 = di("b_w2", [2, HIDP, D])
    b_b2 = di("b_b2", [2, 128, 8])
    b_lng = di("b_lng", [2, 128, 8])
    b_lnb = di("b_lnb", [2, 128, 8])
    b_qs = di("b_qs", [2, 1, INNER])
    b_ks = di("b_ks", [2, 1, INNER])

    xn_dram = [nc.dram_tensor(f"xn_dram{c}", [128, RBN], BF16,
                              kind="Internal").ap() for c in range(8)]
    g_dram = nc.dram_tensor("g_dram", [128, 16 * D], BF16, kind="Internal").ap()
    out = nc.dram_tensor("out", [OWN, D], F32, kind="ExternalOutput").ap()
    tap_out = {}
    for nm, shape in taps:
        tap_out[nm] = nc.dram_tensor("tap_" + nm, list(shape), F32,
                                     kind="ExternalOutput").ap()

    es = ExitStack()
    sems, cnts = {}, {}

    def sem(name):
        if name not in sems:
            sems[name] = es.enter_context(nc.semaphore(name))
            cnts[name] = 0
        return sems[name]

    def inc(instr, name, n=1):
        instr.then_inc(sem(name), n)
        cnts[name] += n

    def wait(eng, name, back=0):
        if name in cnts:
            eng.wait_ge(sem(name), max(0, cnts[name] - back))

    def wait_all(eng, *names):
        for n in names:
            wait(eng, n)

    def sb(name, shape, dt=F32):
        return es.enter_context(nc.sbuf_tensor(name, shape, dt))

    def ps(name, shape, dt=F32):
        return es.enter_context(nc.psum_tensor(name, shape, dt))

    TEN, VEC, ACT, GPS, SYN = nc.tensor, nc.vector, nc.scalar, nc.gpsimd, nc.sync

    def vd():
        VEC.drain()

    with es:
        es.enter_context(nc.Block())

        gstage = sb("gstage", [128, 1024], BF16)
        _xst_ctxs = []
        xst = []
        for c in range(8):
            cm_ = nc.sbuf_tensor(f"xst{c}", [128, X1N], F32, side="right")
            xst.append(cm_.__enter__())
            _xst_ctxs.append(cm_)
        identb = sb("identb", [128, 128], BF16)
        identf = sb("identf", [128, 128], F32)
        onec = sb("onec", [128, 1], F32)
        oner = sb("oner", [1, 128], F32)
        onebf = sb("onebf", [128, 1], BF16)
        masklo = sb("masklo", [128, 128], BF16)
        maskhi = sb("maskhi", [128, 128], BF16)
        w0r = sb("w0r", [1, NS1 + NS2], F32)
        consts = sb("consts", [128, 8], F32)
        _tb_ctxs = []
        tb_csq, tb_snq, tb_csk, tb_snk = [], [], [], []
        for nm in ("tb_csq", "tb_snq", "tb_csk", "tb_snk"):
            cm_ = nc.sbuf_tensor(nm, [128, RBT * 32], F32, side="right")
            t_ = cm_.__enter__()
            _tb_ctxs.append(cm_)
            if nm == "tb_csq":
                tb_csq = t_
            elif nm == "tb_snq":
                tb_snq = t_
            elif nm == "tb_csk":
                tb_csk = t_
            else:
                tb_snk = t_

        psA = ps("psA", [128, 512])
        psB = ps("psB", [128, 512])
        psC = ps("psC", [128, 512])
        psD = ps("psD", [128, 512])
        psE = ps("psE", [128, 512])
        psSm = ps("psSm", [128, 1024])
        psT = ps("psT", [128, 1024], BF16)
        bankrot = [psA, psB, psC, psD, psE]

        for dst, src in ((identb, ident_bf), (identf, ident_f32),
                         (onec, onecol), (oner, onerow), (w0r, w0row)):
            wait(SYN, "dx")
            inc(SYN.dma_start(out=dst[:], in_=src), "dx", 16)
        for dst, src in ((masklo, mask_lo), (maskhi, mask_hi)):
            wait(SYN, "dx")
            inc(SYN.dma_start(out=dst[:], in_=src), "dx", 16)
        wait(GPS, "dw")
        inc(GPS.dma_start(out=onebf[:], in_=onecol), "dw", 16)

        wait(ACT, "dx")
        inc(ACT.activation(consts[:, 0:1], onec[:], AF.Arctan, scale=1.0), "ac")
        wait(VEC, "ac")
        inc(VEC.tensor_scalar(consts[:, 0:1], consts[:, 0:1], 4.0, None, ALU.mult), "dv")
        vd()
        inc(VEC.tensor_scalar(consts[:, 1:2], consts[:, 0:1], 2.0, None, ALU.mult), "dv")
        vd()
        inc(VEC.reciprocal(consts[:, 2:3], consts[:, 1:2]), "dv")
        inc(VEC.tensor_scalar(consts[:, 3:4], consts[:, 0:1], 0.5, None, ALU.mult), "dv")
        inc(VEC.tensor_scalar(consts[:, 5:6], consts[:, 1:2], -1.0, None, ALU.mult), "dv")
        vd()
        wait(ACT, "dv")
        inc(ACT.activation(consts[:, 4:5], onec[:], AF.Ln, scale=10000.0), "ac")
        PI, TWOPI, INV2PI, HFPI, LN1E4, NEG2PI = (consts[:, k:k + 1] for k in range(6))

        def alc(name, shape, dt=F32):
            cm = nc.sbuf_tensor(name, shape, dt)
            return cm.__enter__(), cm

        def freel(ctxs):
            for cm in reversed(ctxs):
                cm.__exit__(None, None, None)

        cst = sb("cst", [128, 640], F32)
        _tsc = []
        def tscal(name, shape, dt=F32):
            t, cm = alc(name, shape, dt)
            _tsc.append(cm)
            return t
        cst_i = sb("cst_i", [128, 512], I32)

        def rrfold(r, y, kf, ki, half=False):
            if half:
                inc(VEC.tensor_scalar(kf, y, INV2PI, 0.5, ALU.mult, ALU.add),
                    "dv")
            else:
                inc(VEC.tensor_scalar(kf, y, INV2PI, None, ALU.mult), "dv")
            vd()
            inc(VEC.tensor_copy(ki, kf), "dv")
            vd()
            inc(VEC.tensor_copy(kf, ki), "dv")
            vd()
            inc(VEC.scalar_tensor_tensor(r, kf, -C2PI_1, y, ALU.mult, ALU.add), "dv")
            vd()
            inc(VEC.scalar_tensor_tensor(r, kf, -C2PI_2, r, ALU.mult, ALU.add), "dv")
            vd()
            inc(VEC.scalar_tensor_tensor(r, kf, -C2PI_3, r, ALU.mult, ALU.add), "dv")
            vd()

        def rrsin(dst, src_ap, width, phase=None, extra=None):
            wait(VEC, "ac")
            t0 = cst[:, 0:width]
            y = src_ap
            if phase is not None:
                inc(VEC.tensor_scalar(t0, src_ap, phase, None, ALU.add), "dv")
                vd()
                y = t0
            kf = cst[:, 512:512 + width]
            ki = cst_i[:, 0:width]
            r = cst[:, 64:64 + width]
            rrfold(r, y, kf, ki, half=True)
            rrfold(r, r, kf, ki)
            if extra is not None:
                inc(VEC.tensor_tensor(r, r, extra, ALU.add), "dv")
                vd()
            inc(VEC.tensor_scalar(r, r, PI_LO, -PI_LO, ALU.min, ALU.max), "dv")
            wait(ACT, "dv")
            inc(ACT.activation(dst, r, AF.Sin, scale=1.0), "ac")

        posb_t = tscal("posb_t", [128, RBT], F32)
        wait(SYN, "dx")
        inc(SYN.dma_start(out=posb_t[:], in_=posb), "dx", 16)
        rowin = tscal("rowin", [1, 32], F32)
        invf_b = tscal("invf_b", [128, 32], F32)
        invl_b = tscal("invl_b", [128, 32], F32)
        lnb_b = tscal("lnb_b", [128, 32], F32)
        lnl_b = tscal("lnl_b", [128, 32], F32)
        wait(TEN, "ac")
        for dst, src in ((invf_b, invf_hi), (invl_b, invf_lo),
                         (lnb_b, lnbs_hi), (lnl_b, lnbs_lo)):
            wait(SYN, "mm")
            wait(SYN, "dx")
            inc(SYN.dma_start(out=rowin[:], in_=src), "dx", 16)
            wait_all(TEN, "dx", "dv")
            inc(TEN.matmul(psA[:, 0:32], oner[:], rowin[:],
                           start=True, stop=True), "mm")
            wait(VEC, "mm")
            inc(VEC.tensor_copy(dst[:], psA[:, 0:32]), "dv")

        angt = tscal("angt", [128, 32], F32)
        angl = tscal("angl", [128, 32], F32)
        sclt = tscal("sclt", [128, 32], F32)
        pwrt = tscal("pwrt", [128, 1], F32)
        sint = tscal("sint", [128, 32], F32)
        cost = tscal("cost", [128, 32], F32)
        for ti in range(RBT):
            pc = posb_t[:, ti:ti + 1]
            inc(VEC.tensor_scalar(angt[:], invf_b[:], pc, None, ALU.mult), "dv")
            inc(VEC.tensor_scalar(angl[:], invl_b[:], pc, None, ALU.mult), "dv")
            inc(VEC.tensor_scalar(pwrt[:], pc, -2048.0, 1.0 / 256.0,
                                  ALU.add, ALU.mult), "dv")
            vd()
            inc(VEC.tensor_scalar(sclt[:], lnb_b[:], pwrt[:], None, ALU.mult), "dv")
            vd()
            inc(VEC.scalar_tensor_tensor(sclt[:], lnl_b[:], pwrt[:], sclt[:],
                                         ALU.mult, ALU.add), "dv")
            wait(ACT, "dv")
            inc(ACT.activation(sclt[:], sclt[:], AF.Exp, scale=1.0), "ac")
            rrsin(sint[:], angt[:], 32, extra=angl[:])
            rrsin(cost[:], angt[:], 32, phase=HFPI, extra=angl[:])
            sl = slice(ti * 32, (ti + 1) * 32)
            wait(VEC, "ac")
            inc(VEC.scalar_tensor_tensor(tb_csq[:, sl], cost[:], 8.0, sclt[:],
                                         ALU.mult, ALU.mult), "dv")
            inc(VEC.scalar_tensor_tensor(tb_snq[:, sl], sint[:], 8.0, sclt[:],
                                         ALU.mult, ALU.mult), "dv")
            vd()
            inc(VEC.reciprocal(sclt[:], sclt[:]), "dv")
            vd()
            inc(VEC.tensor_tensor(tb_csk[:, sl], cost[:], sclt[:], ALU.mult), "dv")
            inc(VEC.tensor_tensor(tb_snk[:, sl], sint[:], sclt[:], ALU.mult), "dv")

        freel(_tsc)
        wait_all(SYN, "dv", "ac")

        xrow = sb("xrow", [128, D], F32)
        lnsum = sb("lnsum", [128, 1], F32)
        lnssq = sb("lnssq", [128, 1], F32)
        lnmu = sb("lnmu", [128, 1], F32)
        lnrs = sb("lnrs", [128, 1], F32)
        lnbi = sb("lnbi", [128, 1], F32)
        sqg = sb("sqg", [128, D], F32)
        _fsc = []
        xnrow, _c = alc("xnrow", [128, D], BF16)
        _fsc.append(_c)

        def ln_tm(dst_ap, src_ap):
            inc(VEC.scalar_tensor_tensor(sqg[:], src_ap, 1.0, src_ap,
                                         ALU.mult, ALU.mult,
                                         accum_out=lnssq[:]), "dv")
            vd()
            inc(VEC.tensor_scalar(sqg[:], src_ap, 0.0, None, ALU.add,
                                  ALU.add, accum_out=lnsum[:]), "dv")
            vd()
            inc(VEC.tensor_scalar(lnmu[:], lnsum[:], 1.0 / D, None, ALU.mult), "dv")
            vd()
            inc(VEC.scalar_tensor_tensor(lnrs[:], lnmu[:], -1.0, lnmu[:],
                                         ALU.mult, ALU.mult), "dv")
            vd()
            inc(VEC.scalar_tensor_tensor(lnrs[:], lnssq[:], 1.0 / D, lnrs[:],
                                         ALU.mult, ALU.add), "dv")
            vd()
            inc(VEC.tensor_scalar(lnrs[:], lnrs[:], 1.0e-5, None, ALU.add), "dv")
            vd()
            wait(ACT, "dv")
            inc(ACT.activation(lnrs[:], lnrs[:], AF.Ln, scale=1.0), "ac")
            wait(VEC, "ac")
            inc(VEC.tensor_scalar(lnrs[:], lnrs[:], -0.5, None, ALU.mult), "dv")
            vd()
            wait(ACT, "dv")
            inc(ACT.activation(lnrs[:], lnrs[:], AF.Exp, scale=1.0), "ac")
            wait(VEC, "ac")
            inc(VEC.scalar_tensor_tensor(lnbi[:], lnmu[:], -1.0, lnrs[:],
                                         ALU.mult, ALU.mult), "dv")
            vd()
            wait_all(ACT, "dv", "mm")
            inc(ACT.activation(dst_ap, src_ap, AF.Identity,
                               scale=lnrs[:], bias=lnbi[:]), "ac")

        def tmcm(dst_list, dst_col, src_bf, nch=8):
            wait(TEN, "ac")
            for c in range(nch):
                wait(TEN, "dv")
                inc(TEN.transpose(psT[:, 0:128],
                                  src_bf[:, c * 128:(c + 1) * 128], identb[:]),
                    "mm")
                wait(VEC, "mm")
                inc(VEC.tensor_copy(dst_list[c][:, dst_col:dst_col + 128],
                                    psT[:, 0:128]), "dv")

        xns = []
        for c in range(8):
            t, _c = alc(f"xns{c}", [128, 1024], BF16)
            xns.append(t)
            _fsc.append(_c)
        for ti in range(RBT):
            bi = ti % 8
            if bi == 0 and ti > 0:
                wait(SYN, "dv")
                for c in range(8):
                    wait(SYN, "st")
                    inc(SYN.dma_start(out=xn_dram[c][:, (ti - 8) * 128:ti * 128],
                                      in_=xns[c][:]), "st", 16)
            wait(SYN, "ac")
            wait(SYN, "dx")
            inc(SYN.dma_start(out=xrow[:], in_=x_b[ti * 128:(ti + 1) * 128, :]),
                "dx", 16)
            wait(VEC, "dx")
            wait(VEC, "ac")
            if bi == 0 and ti > 0:
                wait(VEC, "st")
            ln_tm(xnrow[:], xrow[:])
            tmcm(xns, bi * 128, xnrow)
        wait(SYN, "dv")
        for c in range(8):
            wait(SYN, "st")
            inc(SYN.dma_start(out=xn_dram[c][:, 24 * 128:RBN],
                              in_=xns[c][:, 0:(RBT - 24) * 128]), "st", 16)
        wait(VEC, "st")
        freel(_fsc)

        def tap_sync():
            for e in (VEC, ACT, TEN, SYN, GPS):
                wait(e, "st")
                wait(e, "tp")

        def tap1(nm, src_ap):
            if nm in tap_out:
                wait_all(GPS, "dv", "ac", "mm")
                wait(GPS, "tp")
                inc(GPS.dma_start(out=tap_out[nm], in_=src_ap), "tp", 16)
                tap_sync()

        tap1("csq", tb_csq[:])
        tap1("snq", tb_snq[:])
        tap1("csk", tb_csk[:])
        tap1("snk", tb_snk[:])

        def finish_stub():
            inc(VEC.memset(xrow[:], 0.0), "dv")
            wait(SYN, "dv")
            for ti in range(OWN // 128):
                wait(SYN, "st")
                inc(SYN.dma_start(out=out[ti * 128:(ti + 1) * 128, :],
                                  in_=xrow[:]), "st", 16)
            wait(SYN, "st")

        if upto == "front":
            finish_stub()
            return nc

        def alc(name, shape, dt=F32):
            cm = nc.sbuf_tensor(name, shape, dt)
            return cm.__enter__(), cm

        def freel(ctxs):
            for cm in reversed(ctxs):
                cm.__exit__(None, None, None)

        ssq8 = sb("ssq8", [128, 8], F32)
        rn8 = sb("rn8", [128, 8], F32)
        rop1 = sb("rop1", [128, 512], F32)
        rop2 = sb("rop2", [128, 512], F32)
        qpb = sb("qpb", [128, 512], BF16)
        mxc = sb("mxc", [128, 1], F32)
        negMb = sb("negMb", [128, 1], F32)
        biasM = sb("biasM", [128, 1], F32)
        rnc = sb("rnc", [128, 1], F32)
        nm1 = sb("nm1", [1, 1], F32)
        shst = sb("shst", [128, 8], BF16)
        qkraw = xrow

        def mk_attn_bufs(l):
            ctxs = []
            def a(name, shape, dt):
                t, cm = alc(f"{name}L{l}", shape, dt)
                ctxs.append(cm)
                return t
            bufs = dict(
                kcb=[[a(f"kcb{p}{c}", [128, 512], BF16) for c in range(4)]
                     for p in range(2)],
                qcb=[a(f"qcb{c}", [128, 512], BF16) for c in range(4)],
                vbf=[[a(f"vb{p}{t}", [128, 520], BF16) for t in range(4)]
                     for p in range(2)],
                ptb=a("ptb", [128, 1024], BF16),
                otb=a("otb", [128, 2048], BF16),
                ocb=[a(f"ocb{c}", [128, 512], BF16) for c in range(4)],
                wob=[a(f"wob{c}", [128, D], BF16) for c in range(4)],
                wqb=[a(f"wqb{c}", [128, 3 * INNER], BF16) for c in range(8)],
                qsl=a("qsl", [128, INNER], F32),
                ksl=a("ksl", [128, INNER], F32),
            )
            if l == 0:
                bufs["xnr"] = [a(f"xnr{c}", [128, 2 * 516], BF16)
                               for c in range(8)]
            else:
                bufs["stb"] = [a(f"stb{c}", [128, 520], BF16)
                               for c in range(8)]
            return bufs, ctxs

        def mk_ffn_bufs(l):
            ctxs = []
            def a(name, shape, dt):
                t, cm = alc(f"{name}F{l}", shape, dt)
                ctxs.append(cm)
                return t
            bufs = dict(
                hin=[a(f"hin{c}", [128, 512], BF16) for c in range(8)],
                gel=a("fgel", [128, 512], BF16),
                hb=[a(f"hb{j}", [128, 512], BF16) for j in range(22)],
                w1b=[a(f"w1b{p}", [128, 2048], BF16) for p in range(2)],
                w2b=[a(f"w2b{j}", [128, D], BF16) for j in range(22)],
                b1b=a("b1b", [128, W1C // 128], F32),
                b2b=a("b2b", [128, 8], F32),
                lngb=a("lngb", [128, 8], F32),
                lnbb=a("lnbb", [128, 8], F32),
            )
            return bufs, ctxs

        def rope_tm(ab, srcT, col0, is_q, rbti):
            src_ap = srcT[:, col0:col0 + 512]
            sc = ab["qsl"] if is_q else ab["ksl"]
            cs = (tb_csq if is_q else tb_csk)[:, rbti * 32:(rbti + 1) * 32]
            sn = (tb_snq if is_q else tb_snk)[:, rbti * 32:(rbti + 1) * 32]
            cs_b = cs.rearrange("p (a b d) -> p a b d", a=1, b=1
                                ).broadcast_to([128, 8, 2, 32])
            sn_b = sn.rearrange("p (a d) -> p a d", a=1
                                ).broadcast_to([128, 8, 32])
            inc(VEC.tensor_tensor(sqg[:, 0:512], src_ap, src_ap, ALU.mult), "dv")
            vd()
            inc(VEC.tensor_reduce(ssq8[:], sqg[:, 0:512].rearrange(
                "p (h d) -> p h d", h=8), AX.X, ALU.add), "dv")
            vd()
            inc(VEC.tensor_scalar(ssq8[:], ssq8[:], 1.0e-18, None, ALU.max), "dv")
            wait(ACT, "dv")
            inc(ACT.activation(rn8[:], ssq8[:], AF.Ln, scale=1.0), "ac")
            wait(VEC, "ac")
            inc(VEC.tensor_scalar(rn8[:], rn8[:], -0.5, None, ALU.mult), "dv")
            vd()
            wait(ACT, "dv")
            inc(ACT.activation(rn8[:], rn8[:], AF.Exp, scale=1.0), "ac")
            wait(VEC, "ac")
            for h in range(8):
                inc(VEC.tensor_scalar(sqg[:, h * 64:(h + 1) * 64],
                                      srcT[:, col0 + h * 64:col0 + (h + 1) * 64],
                                      rn8[:, h:h + 1], None, ALU.mult), "dv")
            vd()
            inc(VEC.tensor_tensor(sqg[:, 0:512], sqg[:, 0:512], sc[:],
                                  ALU.mult), "dv")
            vd()
            sq4 = sqg[:, 0:512].rearrange("p (h f d) -> p h f d", h=8, f=2)
            r14 = rop1[:].rearrange("p (h f d) -> p h f d", h=8, f=2)
            q4 = qpb[:].rearrange("p (h f d) -> p h f d", h=8, f=2)
            r23 = rop2[:].rearrange("p (h f d) -> p h f d", h=8, f=2)
            inc(VEC.tensor_tensor(r14, sq4, cs_b, ALU.mult), "dv")
            inc(VEC.tensor_tensor(r23[:, :, 0, :], sq4[:, :, 1, :], sn_b,
                                  ALU.mult), "dv")
            inc(VEC.tensor_tensor(r23[:, :, 1, :], sq4[:, :, 0, :], sn_b,
                                  ALU.mult), "dv")
            vd()
            inc(VEC.tensor_tensor(q4[:, :, 0, :], r14[:, :, 0, :],
                                  r23[:, :, 0, :], ALU.subtract), "dv")
            inc(VEC.tensor_tensor(q4[:, :, 1, :], r14[:, :, 1, :],
                                  r23[:, :, 1, :], ALU.add), "dv")

        def kv_build(l, ab, kw, build_q):
            kw0t = -1536 if l == 0 else -512
            wt = kw0t + 512 * kw
            par = kw & 1
            if l == 0:
                r = kw % 2
                wait_all(SYN, "mm", "dv", "st")
                for c in range(8):
                    wait(SYN, "dx")
                    inc(SYN.dma_start(
                        out=ab["xnr"][c][:, r * 516:r * 516 + 513],
                        in_=xn_dram[c][:, wt + 1664 - 1:wt + 1664 + 512]),
                        "dx", 16)
            else:
                wait(VEC, "mm")
                for c in range(8):
                    if kw == 0:
                        inc(VEC.tensor_copy(ab["stb"][c][:, 0:513],
                                            xst[c][:, wt + 768 - 1:
                                                   wt + 768 + 512]), "dv")
                    else:
                        inc(VEC.tensor_copy(ab["stb"][c][:, 0:1],
                                            shst[:, c:c + 1]), "dv")
                        inc(VEC.tensor_copy(ab["stb"][c][:, 1:513],
                                            xst[c][:, wt + 768:
                                                   wt + 768 + 512]), "dv")
                    vd()
                    inc(VEC.tensor_copy(shst[:, c:c + 1],
                                        ab["stb"][c][:, 512:513]), "dv")
            for tt in range(4):
                t0 = wt + tt * 128
                rbti = (t0 + 1664) // 128
                wait_all(TEN, "ac", "dv", "dx")
                for c in range(8):
                    d = 1 if c >= 4 else 0
                    if l == 0:
                        r = kw % 2
                        u = ab["xnr"][c][:, r * 516 + tt * 128 + 1 - d:
                                         r * 516 + tt * 128 + 129 - d]
                    else:
                        u = ab["stb"][c][:, tt * 128 + 1 - d:tt * 128 + 129 - d]
                    if build_q:
                        inc(TEN.matmul(psA[:, 0:512], u, ab["wqb"][c][:, 0:512],
                                       start=(c == 0), stop=(c == 7)), "mm")
                    inc(TEN.matmul(psB[:, 0:512], u, ab["wqb"][c][:, 512:1024],
                                   start=(c == 0), stop=(c == 7)), "mm")
                    inc(TEN.matmul(psC[:, 0:512], u, ab["wqb"][c][:, 1024:1536],
                                   start=(c == 0), stop=(c == 7)), "mm")
                wait_all(ACT, "mm", "dv")
                if build_q:
                    inc(ACT.copy(qkraw[:, 0:512], psA[:, 0:512]), "ac")
                inc(ACT.copy(qkraw[:, 512:1024], psB[:, 0:512]), "ac")
                wait(VEC, "mm")
                vvb = ab["vbf"][par][tt][:].rearrange("p (h d) -> p h d",
                                                      h=8, d=65)
                inc(VEC.tensor_copy(vvb[:, :, 0:64], psC[:, 0:512].rearrange(
                    "p (h d) -> p h d", h=8)), "dv")
                wait(VEC, "ac")
                if build_q:
                    rope_tm(ab, qkraw, 0, True, rbti)
                    wait_all(TEN, "dv")
                    for cc in range(4):
                        wait(TEN, "dv")
                        inc(TEN.transpose(psT[:, 0:128],
                                          qpb[:, cc * 128:(cc + 1) * 128],
                                          identb[:]), "mm")
                        wait(VEC, "mm")
                        inc(VEC.tensor_copy(
                            ab["qcb"][cc][:, tt * 128:(tt + 1) * 128],
                            psT[:, 0:128]), "dv")
                rope_tm(ab, qkraw, 512, False, rbti)
                wait(TEN, "dv")
                for cc in range(4):
                    wait(TEN, "dv")
                    inc(TEN.transpose(psT[:, 0:128],
                                      qpb[:, cc * 128:(cc + 1) * 128],
                                      identb[:]), "mm")
                    wait(VEC, "mm")
                    inc(VEC.tensor_copy(
                        ab["kcb"][par][cc][:, tt * 128:(tt + 1) * 128],
                        psT[:, 0:128]), "dv")

        def attn_slot(l, ab, s):
            sl0t = -1024 if l == 0 else 0
            qwt = sl0t + 512 * s
            pv, ow = s & 1, (s + 1) & 1
            w0i = s if l == 0 else NS1 + s
            wait_all(TEN, "dv", "ac")
            first = True
            for h in range(8):
                cc, hh = h // 2, (h % 2) * 64
                for qb in range(4):
                    lq = ab["qcb"][cc][hh:hh + 64, qb * 128:(qb + 1) * 128]
                    plen = 512 - qb * 128
                    olen = 128 * (qb + 1)
                    wait(TEN, "dv")
                    inc(TEN.matmul(psSm[:, 0:plen], lq,
                                   ab["kcb"][pv][cc][hh:hh + 64, qb * 128:512],
                                   start=True, stop=True), "mm")
                    inc(TEN.matmul(psSm[:, 512:512 + olen], lq,
                                   ab["kcb"][ow][cc][hh:hh + 64, 0:olen],
                                   start=True, stop=True), "mm")
                    wait(VEC, "mm")
                    inc(VEC.tensor_scalar(rop1[:, 0:plen], psSm[:, 0:plen],
                                          0.0, None if first else mxc[:],
                                          ALU.add, ALU.max,
                                          accum_out=mxc[:]), "dv")
                    vd()
                    inc(VEC.tensor_scalar(rop1[:, 0:olen],
                                          psSm[:, 512:512 + olen],
                                          0.0, mxc[:], ALU.add, ALU.max,
                                          accum_out=mxc[:]), "dv")
                    vd()
                    first = False
            wait(TEN, "dv")
            inc(TEN.transpose(psE[0:1, 0:128], mxc[:], identf[:]), "mm")
            wait(VEC, "mm")
            inc(VEC.tensor_reduce(nm1[:], psE[0:1, 0:128], AX.X, ALU.max,
                                  negate=True), "dv")
            vd()
            wait(TEN, "dv")
            inc(TEN.matmul(psE[:, 4:5], oner[:], nm1[:], start=True,
                           stop=True), "mm")
            wait(VEC, "mm")
            inc(VEC.tensor_copy(negMb[:], psE[:, 4:5]), "dv")
            vd()
            inc(VEC.tensor_tensor(biasM[:], negMb[:], w0b[:, w0i:w0i + 1],
                                  ALU.add), "dv")
            vd()
            psOsl = [(psC, 0), (psD, 0), (psSm, 0), (psSm, 512)]
            for h in range(8):
                cc, hh = h // 2, (h % 2) * 64
                wait_all(TEN, "dv", "ac")
                for kt in range(8):
                    qlo = max(0, (kt - 4) * 128)
                    qhi = min(512, (kt + 1) * 128)
                    par = pv if kt < 4 else ow
                    tt = kt % 4
                    stp = psA if (kt & 1) == 0 else psB
                    pcol = 512 * (kt & 1)
                    wait_all(TEN, "ac", "dv")
                    inc(TEN.matmul(stp[:, 0:qhi - qlo],
                                   ab["kcb"][par][cc][hh:hh + 64,
                                                      tt * 128:(tt + 1) * 128],
                                   ab["qcb"][cc][hh:hh + 64, qlo:qhi],
                                   start=True, stop=True), "mm")
                    wait(VEC, "mm")
                    if kt <= 3:
                        d0 = kt * 128 - qlo
                        inc(VEC.tensor_tensor(stp[:, d0:d0 + 128],
                                              stp[:, d0:d0 + 128], masklo[:],
                                              ALU.add), "dv")
                    else:
                        inc(VEC.tensor_tensor(stp[:, 0:128], stp[:, 0:128],
                                              maskhi[:], ALU.add), "dv")
                    wait_all(ACT, "dv", "mm")
                    bias = biasM if kt <= 3 else negMb
                    inc(ACT.activation(ab["ptb"][:, pcol + qlo:pcol + qhi],
                                       stp[:, 0:qhi - qlo], AF.Exp,
                                       bias=bias[:], scale=1.0), "ac")
                    wait(TEN, "ac")
                    for qb in range(max(0, kt - 4), min(3, kt) + 1):
                        pt, c0 = psOsl[qb]
                        inc(TEN.matmul(
                            pt[:, c0:c0 + 65],
                            ab["ptb"][:, pcol + qb * 128:pcol + (qb + 1) * 128],
                            ab["vbf"][par][tt][:, h * 65:h * 65 + 65],
                            start=(kt == qb), stop=(kt == qb + 4)), "mm")
                wait(VEC, "mm")
                for qb in range(4):
                    pt, c0 = psOsl[qb]
                    inc(VEC.reciprocal(rnc[:], pt[:, c0 + 64:c0 + 65]), "dv")
                    vd()
                    inc(VEC.tensor_scalar(
                        ab["otb"][:, qb * 512 + h * 64:qb * 512 + h * 64 + 64],
                        pt[:, c0:c0 + 64], rnc[:], None, ALU.mult), "dv")
                    vd()
            for qb in range(4):
                for cc in range(4):
                    wait_all(TEN, "dv", "ac")
                    inc(TEN.transpose(psT[:, 0:128],
                                      ab["otb"][:, qb * 512 + cc * 128:
                                                qb * 512 + (cc + 1) * 128],
                                      identb[:]), "mm")
                    wait(VEC, "mm")
                    inc(VEC.tensor_copy(ab["ocb"][cc][:, qb * 128:(qb + 1) * 128],
                                        psT[:, 0:128]), "dv")
            e0 = 256 if (l == 0 and s == 0) else 0
            for co in range(8):
                pso = psA if (co & 1) == 0 else psB
                wait_all(TEN, "dv", "ac")
                for cc in range(4):
                    inc(TEN.matmul(pso[:, 0:512 - e0],
                                   ab["wob"][cc][:, co * 128:(co + 1) * 128],
                                   ab["ocb"][cc][:, e0:512],
                                   start=(cc == 0), stop=(cc == 3)), "mm")
                wait(VEC, "mm")
                c0 = qwt + 768 + e0
                c1 = qwt + 768 + 512
                if l == 0:
                    r = (s + 1) % 2
                    inc(VEC.tensor_tensor(
                        xst[co][:, c0:c1], pso[:, 0:512 - e0],
                        ab["xnr"][co][:, r * 516 + 1 + e0:r * 516 + 513],
                        ALU.add), "dv")
                else:
                    inc(VEC.tensor_tensor(xst[co][:, c0:c1],
                                          pso[:, 0:512 - e0],
                                          xst[co][:, c0:c1], ALU.add), "dv")

        def ffn(l, fb):
            E0 = -640 if l == 0 else 0
            E1 = 2048
            org = 768
            wait_all(SYN, "mm", "dv", "ac")
            wait_all(GPS, "mm", "dv", "ac")
            for dst, srcd in ((fb["b1b"], b_b1), (fb["b2b"], b_b2),
                              (fb["lngb"], b_lng), (fb["lnbb"], b_lnb)):
                wait(SYN, "dx")
                inc(SYN.dma_start(out=dst[:], in_=srcd[l]), "dx", 16)
            for j in range(22):
                wait(GPS, "dw")
                inc(GPS.dma_start(out=fb["w2b"][j][:],
                                  in_=b_w2[l][j * 128:(j + 1) * 128, :]),
                    "dw", 16)
            w1src = b_w1[l].rearrange("(c p) w -> p c w", p=128)
            ngr = (E1 - E0 + 511) // 512
            for g in range(ngr - 1, -1, -1):
                T0 = E0 + 512 * g
                W = min(512, E1 - T0)
                wait(GPS, "mm")
                wait(GPS, "dw")
                inc(GPS.dma_start(out=fb["w1b"][0][:],
                                  in_=w1src[:, :, 0:256]), "dw", 16)
                wait_all(TEN, "dv", "ac")
                stb1, sqb = fb["hin"][0], fb["gel"]
                for c in range(8):
                    d = 1 if c >= 4 else 0
                    wait_all(VEC, "mm")
                    inc(VEC.tensor_copy(stb1[:, 0:W],
                                        xst[c][:, T0 + org - d:
                                               T0 + org - d + W]), "dv")
                    vd()
                    inc(VEC.tensor_tensor(sqb[:, 0:W], stb1[:, 0:W],
                                          stb1[:, 0:W], ALU.mult), "dv")
                    wait(TEN, "dv")
                    inc(TEN.matmul(psSm[0:1, 0:W], onebf[:], stb1[:, 0:W],
                                   start=(c == 0), stop=(c == 7)), "mm")
                    inc(TEN.matmul(psSm[0:1, 512:512 + W], onebf[:],
                                   sqb[:, 0:W],
                                   start=(c == 0), stop=(c == 7)), "mm")
                wait(VEC, "mm")
                inc(VEC.tensor_scalar(rop1[0:1, 0:W], psSm[0:1, 0:W], 1.0 / D,
                                      None, ALU.mult), "dv")
                inc(VEC.tensor_scalar(rop2[0:1, 0:W], psSm[0:1, 512:512 + W],
                                      1.0 / D, None, ALU.mult), "dv")
                vd()
                inc(VEC.tensor_tensor(cst[0:1, 0:W], rop1[0:1, 0:W],
                                      rop1[0:1, 0:W], ALU.mult), "dv")
                vd()
                inc(VEC.tensor_tensor(rop2[0:1, 0:W], rop2[0:1, 0:W],
                                      cst[0:1, 0:W], ALU.subtract), "dv")
                vd()
                inc(VEC.tensor_scalar(rop2[0:1, 0:W], rop2[0:1, 0:W], 1.0e-5,
                                      None, ALU.add), "dv")
                wait(ACT, "dv")
                inc(ACT.activation(rop2[0:1, 0:W], rop2[0:1, 0:W], AF.Ln,
                                   scale=1.0), "ac")
                wait(VEC, "ac")
                inc(VEC.tensor_scalar(rop2[0:1, 0:W], rop2[0:1, 0:W], -0.5,
                                      None, ALU.mult), "dv")
                vd()
                wait(ACT, "dv")
                inc(ACT.activation(rop2[0:1, 0:W], rop2[0:1, 0:W], AF.Exp,
                                   scale=1.0), "ac")
                wait_all(TEN, "ac", "dv")
                inc(TEN.matmul(psD[:, 0:W], oner[:], rop1[0:1, 0:W],
                               start=True, stop=True), "mm")
                inc(TEN.matmul(psE[:, 0:W], oner[:], rop2[0:1, 0:W],
                               start=True, stop=True), "mm")
                wait(VEC, "mm")
                wait(ACT, "dx")
                for c in range(8):
                    d = 1 if c >= 4 else 0
                    wait(VEC, "ac")
                    inc(VEC.tensor_tensor(sqg[:, 0:W],
                                          xst[c][:, T0 + org - d:
                                                 T0 + org - d + W],
                                          psD[:, 0:W], ALU.subtract), "dv")
                    vd()
                    inc(VEC.tensor_tensor(sqg[:, 0:W], sqg[:, 0:W],
                                          psE[:, 0:W], ALU.mult), "dv")
                    wait(ACT, "dv")
                    inc(ACT.activation(fb["hin"][c][:, 0:W], sqg[:, 0:W],
                                       AF.Identity,
                                       scale=fb["lngb"][:, c:c + 1],
                                       bias=fb["lnbb"][:, c:c + 1]), "ac")
                    if l == 0 and g == 2 and c == 0 and "sqgt" in tap_out:
                        wait_all(GPS, "dv", "ac", "mm")
                        wait(GPS, "tp")
                        inc(GPS.dma_start(out=tap_out["sqgt"][:, 0:512],
                                          in_=sqg[:, 0:512]), "tp", 16)
                        wait(GPS, "tp")
                        inc(GPS.dma_start(out=tap_out["sqgt"][:, 512:1024],
                                          in_=fb["lngb"][:].broadcast_to(
                                              [128, 8]) if False else
                                          fb["lngb"][:]), "tp", 16) if False \
                            else None
                        wait(GPS, "tp")
                        inc(GPS.dma_start(out=tap_out["sqgt"][:, 512:520],
                                          in_=fb["lngb"][:]), "tp", 16)
                        wait(GPS, "tp")
                        inc(GPS.dma_start(out=tap_out["sqgt"][:, 520:528],
                                          in_=fb["lnbb"][:]), "tp", 16)
                        tap_sync()
                if l == 0 and g == 2 and "stats" in tap_out:
                    wait_all(GPS, "dv", "ac", "mm")
                    wait(GPS, "tp")
                    inc(GPS.dma_start(out=tap_out["stats"][:, 0:W],
                                      in_=rop1[0:1, 0:W]), "tp", 16)
                    wait(GPS, "tp")
                    inc(GPS.dma_start(out=tap_out["stats"][:, 512:512 + W],
                                      in_=rop2[0:1, 0:W]), "tp", 16)
                    tap_sync()
                if l == 0 and g == 2 and "hin" in tap_out:
                    wait_all(GPS, "dv", "ac", "mm")
                    for c in range(8):
                        wait(GPS, "tp")
                        inc(GPS.dma_start(out=tap_out["hin"][:, c * 512:(c + 1) * 512],
                                          in_=fb["hin"][c][:]), "tp", 16)
                    tap_sync()
                for j in range(22):
                    pp = j & 1
                    if j < 21:
                        wait(GPS, "mm")
                        wait(GPS, "dw")
                        inc(GPS.dma_start(
                            out=fb["w1b"][1 - pp][:],
                            in_=w1src[:, :, (j + 1) * 256:(j + 2) * 256]),
                            "dw", 16)
                    wait_all(TEN, "dw", "ac", "dv")
                    for c in range(8):
                        inc(TEN.matmul(psA[:, 0:W],
                                       fb["w1b"][pp][:, c * 256:c * 256 + 128],
                                       fb["hin"][c][:, 0:W],
                                       start=(c == 0), stop=(c == 7)), "mm")
                        inc(TEN.matmul(psB[:, 0:W],
                                       fb["w1b"][pp][:, c * 256 + 128:
                                                     c * 256 + 256],
                                       fb["hin"][c][:, 0:W],
                                       start=(c == 0), stop=(c == 7)), "mm")
                    wait_all(ACT, "mm", "dv")
                    inc(ACT.activation(fb["gel"][:, 0:W], psB[:, 0:W], AF.Gelu,
                                       bias=fb["b1b"][:, 2 * j + 1:2 * j + 2],
                                       scale=1.0), "ac")
                    inc(ACT.activation(rop1[:, 0:W], psA[:, 0:W], AF.Identity,
                                       bias=fb["b1b"][:, 2 * j:2 * j + 1],
                                       scale=1.0), "ac")
                    wait(VEC, "ac")
                    inc(VEC.tensor_tensor(fb["hb"][j][:, 0:W], rop1[:, 0:W],
                                          fb["gel"][:, 0:W], ALU.mult), "dv")
                if l == 0 and g == 2 and "hbt" in tap_out:
                    wait_all(GPS, "dv", "ac", "mm")
                    for j in range(4):
                        wait(GPS, "tp")
                        inc(GPS.dma_start(out=tap_out["hbt"][:, j * 512:(j + 1) * 512],
                                          in_=fb["hb"][j][:]), "tp", 16)
                    tap_sync()
                for od in range(8):
                    wait_all(TEN, "dv", "ac", "dw")
                    for j in range(22):
                        inc(TEN.matmul(psC[:, 0:W],
                                       fb["w2b"][j][:, od * 128:(od + 1) * 128],
                                       fb["hb"][j][:, 0:W],
                                       start=(j == 0), stop=(j == 21)), "mm")
                    wait_all(ACT, "mm", "dv")
                    inc(ACT.activation(rop2[:, 0:W], psC[:, 0:W], AF.Identity,
                                       bias=fb["b2b"][:, od:od + 1],
                                       scale=1.0), "ac")
                    wait(VEC, "ac")
                    inc(VEC.tensor_tensor(xst[od][:, T0 + org:T0 + org + W],
                                          xst[od][:, T0 + org:T0 + org + W],
                                          rop2[:, 0:W], ALU.add), "dv")
                    vd()

        w0b = sb("w0b", [128, NS1 + NS2], F32)
        wait_all(TEN, "dx", "dv", "ac")
        inc(TEN.matmul(psE[:, 0:NS1 + NS2], oner[:], w0r[:],
                       start=True, stop=True), "mm")
        wait(VEC, "mm")
        inc(VEC.tensor_copy(w0b[:], psE[:, 0:NS1 + NS2]), "dv")

        def local_layer(l):
            ab, ctxs = mk_attn_bufs(l)
            wait_all(GPS, "mm", "dv", "ac", "st")
            for c in range(8):
                wait(GPS, "dw")
                inc(GPS.dma_start(out=ab["wqb"][c][:],
                                  in_=b_wqkv[l][c * 128:(c + 1) * 128, :]),
                    "dw", 16)
            for cc in range(4):
                wait(GPS, "dw")
                inc(GPS.dma_start(out=ab["wob"][cc][:],
                                  in_=b_wo[l][cc * 128:(cc + 1) * 128, :]),
                    "dw", 16)
            for dst, srcd in ((ab["qsl"], b_qs), (ab["ksl"], b_ks)):
                wait_all(SYN, "mm", "dv")
                wait(SYN, "dx")
                inc(SYN.dma_start(out=rop1[0:1, 0:INNER], in_=srcd[l]), "dx", 16)
                wait_all(TEN, "dx", "dv", "ac")
                inc(TEN.matmul(psE[:, 0:INNER], oner[:], rop1[0:1, 0:INNER],
                               start=True, stop=True), "mm")
                wait(VEC, "mm")
                inc(VEC.tensor_copy(dst[:], psE[:, 0:INNER]), "dv")
            wait(VEC, "dw")
            for p in range(2):
                for t in range(4):
                    inc(VEC.memset(ab["vbf"][p][t][:].rearrange(
                        "p (h d) -> p h d", h=8, d=65)[:, :, 64:65], 1.0), "dv")
            wait_all(TEN, "dw", "dx", "dv")
            nsl = NS1 if l == 0 else NS2
            kv_build(l, ab, 0, False)
            for s in range(nsl):
                kv_build(l, ab, s + 1, True)
                attn_slot(l, ab, s)
                if s == 0 and l == 1 and "otb1" in tap_out:
                    wait_all(GPS, "dv", "ac", "mm")
                    wait(GPS, "tp")
                    inc(GPS.dma_start(out=tap_out["otb1"], in_=ab["otb"][:]),
                        "tp", 16)
                    wait(GPS, "tp")
                    inc(GPS.dma_start(out=tap_out["negm1"][:, 0:1],
                                      in_=negMb[:],
                                      allow_slow_non_contiguous=True), "tp", 16)
                    for c in range(8):
                        wait(GPS, "tp")
                        inc(GPS.dma_start(
                            out=tap_out["stb1t"][:, c * 520:(c + 1) * 520],
                            in_=ab["stb"][c][:]), "tp", 16)
                    tap_sync()
                if s == 0 and l == 0:
                    if "otb" in tap_out:
                        wait_all(GPS, "dv", "ac", "mm")
                        wait(GPS, "tp")
                        inc(GPS.dma_start(out=tap_out["otb"], in_=ab["otb"][:]),
                            "tp", 16)
                        tap_sync()
                    if "qcb" in tap_out:
                        wait_all(GPS, "dv", "ac", "mm")
                        for cc in range(4):
                            wait(GPS, "tp")
                            inc(GPS.dma_start(
                                out=tap_out["qcb"][:, cc * 512:(cc + 1) * 512],
                                in_=ab["qcb"][cc][:]), "tp", 16)
                        for cc in range(4):
                            wait(GPS, "tp")
                            inc(GPS.dma_start(
                                out=tap_out["qcb"][:, 2048 + cc * 512:
                                                 2048 + (cc + 1) * 512],
                                in_=ab["kcb"][0][cc][:]), "tp", 16)
                        for cc in range(4):
                            wait(GPS, "tp")
                            inc(GPS.dma_start(
                                out=tap_out["qcb"][:, 4096 + cc * 512:
                                                 4096 + (cc + 1) * 512],
                                in_=ab["kcb"][1][cc][:]), "tp", 16)
                        tap_sync()
                    if "negm" in tap_out:
                        wait_all(GPS, "dv", "ac", "mm")
                        wait(GPS, "tp")
                        inc(GPS.dma_start(out=tap_out["negm"][:, 0:1],
                                          in_=negMb[:],
                                          allow_slow_non_contiguous=True),
                            "tp", 16)
                        wait(GPS, "tp")
                        inc(GPS.dma_start(out=tap_out["negm"][:, 1:2],
                                          in_=mxc[:],
                                          allow_slow_non_contiguous=True),
                            "tp", 16)
                        tap_sync()
                    if "xnr" in tap_out:
                        wait_all(GPS, "dv", "ac", "mm")
                        for c in range(8):
                            wait(GPS, "tp")
                            inc(GPS.dma_start(
                                out=tap_out["xnr"][:, c * 1032:(c + 1) * 1032],
                                in_=ab["xnr"][c][:]), "tp", 16)
                        tap_sync()
            if l == 1 and "negm1" in tap_out:
                pass
            if l == 1 and "pa1" in tap_out:
                wait_all(GPS, "dv", "ac", "mm")
                for c in range(8):
                    wait(GPS, "tp")
                    inc(GPS.dma_start(out=tap_out["pa1"][:, c * X1N:(c + 1) * X1N],
                                      in_=xst[c][:]), "tp", 16)
                tap_sync()
            if l == 0 and "pa0" in tap_out:
                wait_all(GPS, "dv", "ac", "mm")
                for c in range(8):
                    wait(GPS, "tp")
                    inc(GPS.dma_start(out=tap_out["pa0"][:, c * X1N:(c + 1) * X1N],
                                      in_=xst[c][:]), "tp", 16)
                tap_sync()
            freel(ctxs)
            fb, fctxs = mk_ffn_bufs(l)
            ffn(l, fb)
            freel(fctxs)

        def tap_state(nm):
            if nm in tap_out:
                wait_all(GPS, "dv", "ac", "mm")
                for c in range(8):
                    wait(GPS, "tp")
                    inc(GPS.dma_start(out=tap_out[nm][:, c * X1N:(c + 1) * X1N],
                                      in_=xst[c][:]), "tp", 16)
                tap_sync()

        local_layer(0)
        if upto == "l1":
            tap_state("xst")
            finish_stub()
            return nc
        local_layer(1)

        for ti in range(16):
            for c in range(8):
                wait_all(TEN, "ac", "dv")
                if c == 0:
                    wait(TEN, "st")
                    wait(ACT, "st")
                inc(TEN.transpose(psA[:, 0:128],
                                  xst[c][:, 768 + ti * 128:768 + (ti + 1) * 128],
                                  identf[:]), "mm")
                wait(ACT, "mm")
                inc(ACT.activation(gstage[:, c * 128:(c + 1) * 128],
                                   psA[:, 0:128], AF.Silu, scale=1.0), "ac")
            wait(SYN, "ac")
            wait(SYN, "st")
            inc(SYN.dma_start(out=g_dram[:, ti * 1024:(ti + 1) * 1024],
                              in_=gstage[:]), "st", 16)

        if "g" in tap_out:
            wait_all(GPS, "st", "ac")
            wait(GPS, "tp")
            inc(GPS.dma_start(out=tap_out["g"], in_=g_dram), "tp", 16)
            tap_sync()
        if upto in ("l2", "lb"):
            tap_state("xst")
            finish_stub()
            return nc

        for cm_ in reversed(_tb_ctxs):
            cm_.__exit__(None, None, None)
        for cm_ in reversed(_xst_ctxs):
            cm_.__exit__(None, None, None)
        _actx = []
        def aa(name, shape, dt):
            t, cm_ = alc(name + "A", shape, dt)
            _actx.append(cm_)
            return t
        xnf = [aa(f"xnf{c}", [128, L], BF16) for c in range(8)]
        acs = aa("acs", [128, L], F32)
        asn = aa("asn", [128, L], F32)
        oall = [aa(f"oall{c}", [128, OWN], BF16) for c in range(8)]
        onesA = aa("onesA", [128, 128], F32)
        permb = aa("permb", [128, 128], BF16)
        shf64b = aa("shf64b", [128, 128], BF16)
        _pctx = []
        def pa(name, shape, dt):
            t, cm_ = alc(name + "P", shape, dt)
            _pctx.append(cm_)
            return t
        K2 = pa("K2", [128, L], BF16)
        Q2 = pa("Q2", [128, OWN], BF16)
        Vc = pa("Vc", [128, L], BF16)
        vbh = [pa(f"vbh{h}", [128, 32 * 65], BF16) for h in range(2)]
        ptA = pa("ptA", [128, 1024], BF16)
        wkp = pa("wkp", [128, 1024], BF16)
        wqp = pa("wqp", [128, 1024], BF16)
        wvp = pa("wvp", [128, 1024], BF16)
        xnrA = pa("xnrA", [128, D], BF16)
        repA = pa("repA", [128, 128], F32)
        ivA = pa("ivA", [128, 32], F32)
        ivlA = pa("ivlA", [128, 32], F32)
        posA = pa("posA", [128, 32], F32)
        tmt = pa("tmt", [128, 32], F32)
        tml = pa("tml", [128, 32], F32)
        t32 = pa("t32", [128, 128], F32)
        ost = pa("ost", [128, 512], BF16)
        rowin2 = pa("rowin2", [1, 32], F32)
        kab = pa("kab", [128, L], BF16)
        qab = pa("qab", [128, OWN], BF16)
        nsc = pa("nsc", [1, 4], F32)

        wait_all(GPS, "mm", "dv", "ac", "st")
        wait_all(SYN, "mm", "dv", "ac", "st")
        wait_all(VEC, "mm", "ac")
        wait_all(TEN, "dv", "ac")
        wait(SYN, "dx")
        inc(SYN.dma_start(out=onesA[:], in_=allones), "dx", 16)
        wait(SYN, "dx")
        inc(SYN.dma_start(out=repA[:], in_=repm), "dx", 16)
        wait(SYN, "dx")
        inc(SYN.dma_start(out=posA[:], in_=poska), "dx", 16)
        wait(GPS, "dw")
        inc(GPS.dma_start(out=permb[:], in_=perm_a), "dw", 16)
        wait(GPS, "dw")
        inc(GPS.dma_start(out=shf64b[:], in_=shf64), "dw", 16)
        wait(SYN, "dx")
        inc(SYN.dma_start(out=rowin2[:], in_=invf_hi), "dx", 16)
        inc(VEC.memset(ost[:], 0.0), "dv")
        wait_all(TEN, "dx", "dv", "ac")
        inc(TEN.matmul(psA[:, 0:32], oner[:], rowin2[:], start=True, stop=True),
            "mm")
        wait(VEC, "mm")
        inc(VEC.tensor_copy(ivA[:], psA[:, 0:32]), "dv")
        wait(SYN, "mm")
        wait(SYN, "dx")
        inc(SYN.dma_start(out=rowin2[:], in_=invf_lo), "dx", 16)
        wait_all(TEN, "dx", "dv")
        inc(TEN.matmul(psA[:, 0:32], oner[:], rowin2[:], start=True, stop=True),
            "mm")
        wait(VEC, "mm")
        inc(VEC.tensor_copy(ivlA[:], psA[:, 0:32]), "dv")
        vd()

        for ti in range(32):
            pc = posA[:, ti:ti + 1]
            wait(VEC, "ac")
            inc(VEC.tensor_scalar(tmt[:], ivA[:], pc, None, ALU.mult), "dv")
            inc(VEC.tensor_scalar(tml[:], ivlA[:], pc, None, ALU.mult), "dv")
            vd()
            for tbl, ph in ((asn, None), (acs, HFPI)):
                rrsin(cst[:, 576:608], tmt[:], 32, phase=ph, extra=tml[:])
                wait_all(TEN, "ac", "dv")
                inc(TEN.transpose(psSm[0:32, 0:128], cst[:, 576:608],
                                  identf[:]), "mm")
                wait(VEC, "mm")
                inc(VEC.tensor_copy(t32[0:32, 0:128], psSm[0:32, 0:128]), "dv")
                wait(TEN, "dv")
                inc(TEN.matmul(psSm[:, 128:256], repA[0:32, :], t32[0:32, :],
                               start=True, stop=True), "mm")
                wait(VEC, "mm")
                inc(VEC.tensor_copy(tbl[:, ti * 128:(ti + 1) * 128],
                                    psSm[:, 128:256]), "dv")

        for ti in range(32):
            wait(SYN, "ac")
            wait(SYN, "dx")
            inc(SYN.dma_start(out=xrow[:], in_=x_full[ti * 128:(ti + 1) * 128, :]),
                "dx", 16)
            wait_all(VEC, "dx", "ac")
            ln_tm(xnrA[:], xrow[:])
            tmcm(xnf, ti * 128, xnrA)

        wait(VEC, "dv")
        for h in range(2):
            inc(VEC.memset(vbh[h][:].rearrange("p (t d) -> p t d", t=32,
                                               d=65)[:, :, 64:65], 1.0), "dv")

        wsrc = a_wqkv.rearrange("(c p) e -> p c e", p=128)
        for pr in range(8):
            wait_all(GPS, "mm", "dv")
            wait(GPS, "dw")
            inc(GPS.dma_start(out=wqp[:].rearrange("p (c e) -> p c e", c=8),
                              in_=wsrc[:, :, 128 * pr:128 * pr + 128]),
                "dw", 16)
            wait(GPS, "dw")
            inc(GPS.dma_start(out=wkp[:].rearrange("p (c e) -> p c e", c=8),
                              in_=wsrc[:, :, 1024 + 128 * pr:1024 + 128 * pr + 128]),
                "dw", 16)
            wait(GPS, "dw")
            inc(GPS.dma_start(out=wvp[:].rearrange("p (c e) -> p c e", c=8),
                              in_=wsrc[:, :, 2048 + 128 * pr:2048 + 128 * pr + 128]),
                "dw", 16)
            wait_all(TEN, "dw", "dv", "ac")
            for sp in range(8):
                s0 = sp * 512
                wait_all(TEN, "ac", "dv")
                for c in range(8):
                    inc(TEN.matmul(psA[:, 0:512], wkp[:, c * 128:(c + 1) * 128],
                                   xnf[c][:, s0:s0 + 512],
                                   start=(c == 0), stop=(c == 7)), "mm")
                    inc(TEN.matmul(psB[:, 0:512], wvp[:, c * 128:(c + 1) * 128],
                                   xnf[c][:, s0:s0 + 512],
                                   start=(c == 0), stop=(c == 7)), "mm")
                wait_all(ACT, "mm", "dv")
                inc(ACT.copy(K2[:, s0:s0 + 512], psA[:, 0:512]), "ac")
                inc(ACT.copy(Vc[:, s0:s0 + 512], psB[:, 0:512]), "ac")
                wait_all(TEN, "ac", "dv")
                inc(TEN.matmul(psC[:, 0:512], permb[:], K2[:, s0:s0 + 512],
                               start=True, stop=True), "mm")
                wait_all(VEC, "mm", "ac")
                inc(VEC.tensor_tensor(rop1[:], K2[:, s0:s0 + 512],
                                      acs[:, s0:s0 + 512], ALU.mult), "dv")
                inc(VEC.tensor_tensor(rop2[:], psC[:, 0:512],
                                      asn[:, s0:s0 + 512], ALU.mult), "dv")
                vd()
                inc(VEC.tensor_tensor(K2[:, s0:s0 + 512], rop1[:], rop2[:],
                                      ALU.add), "dv")
            for sp in range(4):
                s0 = sp * 512
                wait_all(TEN, "ac", "dv")
                for c in range(8):
                    inc(TEN.matmul(psA[:, 0:512], wqp[:, c * 128:(c + 1) * 128],
                                   xnf[c][:, s0:s0 + 512],
                                   start=(c == 0), stop=(c == 7)), "mm")
                wait_all(ACT, "mm", "dv")
                inc(ACT.copy(Q2[:, s0:s0 + 512], psA[:, 0:512]), "ac")
                wait_all(TEN, "ac", "dv")
                inc(TEN.matmul(psC[:, 0:512], permb[:], Q2[:, s0:s0 + 512],
                               start=True, stop=True), "mm")
                wait_all(VEC, "mm", "ac")
                inc(VEC.tensor_tensor(rop1[:], Q2[:, s0:s0 + 512],
                                      acs[:, s0:s0 + 512], ALU.mult), "dv")
                inc(VEC.tensor_tensor(rop2[:], psC[:, 0:512],
                                      asn[:, s0:s0 + 512], ALU.mult), "dv")
                vd()
                inc(VEC.tensor_tensor(Q2[:, s0:s0 + 512], rop1[:], rop2[:],
                                      ALU.add), "dv")
            for tt in range(32):
                wait_all(TEN, "dv", "ac")
                inc(TEN.transpose(psT[:, 0:128], Vc[:, tt * 128:(tt + 1) * 128],
                                  identb[:]), "mm")
                wait(VEC, "mm")
                inc(VEC.tensor_copy(vbh[0][:, tt * 65:tt * 65 + 64],
                                    psT[:, 0:64]), "dv")
                inc(VEC.tensor_copy(vbh[1][:, tt * 65:tt * 65 + 64],
                                    psT[:, 64:128]), "dv")
            for h01 in range(2):
                hh = h01 * 64
                wait_all(SYN, "mm", "ac", "dv")
                wait(SYN, "dx")
                inc(SYN.dma_start(out=kab[0:64, :], in_=K2[hh:hh + 64, :]),
                    "dx", 16)
                wait(SYN, "dx")
                inc(SYN.dma_start(out=qab[0:64, :], in_=Q2[hh:hh + 64, :]),
                    "dx", 16)
                wait_all(VEC, "mm", "ac")
                inc(VEC.memset(kab[64:65, :], 1.0), "dv")
                inc(VEC.memset(nsc[:, 0:1], -1.0), "dv")
                for qt in range(16):
                    for sp in range(8):
                        s0 = sp * 512
                        wait_all(TEN, "dx", "dv", "ac")
                        inc(TEN.matmul(psA[:, 0:512],
                                       qab[0:64, qt * 128:(qt + 1) * 128],
                                       kab[0:64, s0:s0 + 512],
                                       start=True, stop=True), "mm")
                        wait(VEC, "mm")
                        inc(VEC.tensor_scalar(
                            rop1[:, 0:512], psA[:, 0:512], 0.0,
                            None if sp == 0 else mxc[:],
                            ALU.add, ALU.max, accum_out=mxc[:]), "dv")
                        vd()
                    wait_all(TEN, "dv", "ac")
                    inc(TEN.transpose(psC[0:1, 0:128], mxc[:], identf[:]),
                        "mm")
                    wait(VEC, "mm")
                    inc(VEC.tensor_copy(rop1[0:1, 0:128], psC[0:1, 0:128]),
                        "dv")
                    vd()
                    wait(TEN, "dv")
                    inc(TEN.matmul(psC[64:65, 0:128], nsc[:, 0:1],
                                   rop1[0:1, 0:128], start=True, stop=True),
                        "mm")
                    wait(ACT, "mm")
                    inc(ACT.copy(qab[64:65, qt * 128:(qt + 1) * 128],
                                 psC[64:65, 0:128]), "ac")
                wait_all(TEN, "dx", "ac", "dv")
                if pr == 0 and h01 == 0 and "adbg" in tap_out:
                    wait_all(GPS, "dx", "ac", "mm", "dv")
                    wait(GPS, "tp")
                    inc(GPS.dma_start(out=tap_out["adbg"][0:128, 0:2048],
                                      in_=qab[:]), "tp", 16)
                    wait(GPS, "tp")
                    inc(GPS.dma_start(out=tap_out["adbg"][0:128, 2048:6144],
                                      in_=kab[:]), "tp", 16)
                    tap_sync()
                Ksl = kab[0:65, :]
                Qsl = qab[0:65, :]
                for qs4 in range(4):
                    q0 = qs4 * 512
                    for kc in range(16):
                        wait_all(TEN, "ac", "dv")
                        for k2_ in range(2):
                            kt = kc * 2 + k2_
                            inc(TEN.matmul(psSm[:, k2_ * 512:k2_ * 512 + 512],
                                           Ksl[:, kt * 128:(kt + 1) * 128],
                                           Qsl[:, q0:q0 + 512],
                                           start=True, stop=True), "mm")
                        wait_all(ACT, "mm", "dv")
                        inc(ACT.activation(ptA[:, 0:1024], psSm[:, 0:1024],
                                           AF.Exp, scale=0.125), "ac")
                        wait(TEN, "ac")
                        for k2_ in range(2):
                            kt = kc * 2 + k2_
                            inc(TEN.matmul(psE[0:65, 0:512],
                                           vbh[h01][:, kt * 65:kt * 65 + 65],
                                           ptA[:, k2_ * 512:k2_ * 512 + 512],
                                           start=(kt == 0), stop=(kt == 31)),
                                "mm")
                    wait(VEC, "mm")
                    if pr == 0 and h01 == 0 and qs4 == 0 and "adn" in tap_out:
                        wait_all(GPS, "mm", "ac", "dv")
                        inc(VEC.tensor_copy(rop2[0:65, 0:512],
                                            psE[0:65, 0:512]), "dv")
                        wait(GPS, "dv")
                        wait(GPS, "tp")
                        inc(GPS.dma_start(out=tap_out["adn"][0:65, 0:512],
                                          in_=rop2[0:65, 0:512]), "tp", 16)
                        wait(GPS, "tp")
                        inc(GPS.dma_start(out=tap_out["adn"][0:128, 512:1024],
                                          in_=ptA[:, 0:512]), "tp", 16)
                        tap_sync()
                    inc(VEC.tensor_copy(rop1[64:65, 0:512], psE[64:65, 0:512]),
                        "dv")
                    vd()
                    inc(VEC.reciprocal(rop1[64:65, 0:512], rop1[64:65, 0:512]),
                        "dv")
                    vd()
                    wait_all(TEN, "dv", "ac")
                    inc(TEN.matmul(psSm[:, 0:512], onesA[64:65, :],
                                   rop1[64:65, 0:512], start=True, stop=True),
                        "mm")
                    wait(VEC, "mm")
                    inc(VEC.tensor_copy(rop2[0:64, 0:512], psSm[0:64, 0:512]),
                        "dv")
                    vd()
                    inc(VEC.tensor_tensor(ost[0:64, 0:512], psE[0:64, 0:512],
                                          rop2[0:64, 0:512], ALU.mult), "dv")
                    vd()
                    if h01 == 0:
                        inc(VEC.tensor_copy(oall[pr][0:64, q0:q0 + 512],
                                            ost[0:64, 0:512]), "dv")
                    else:
                        wait_all(TEN, "dv", "ac")
                        inc(TEN.matmul(psD[:, 0:512], shf64b[:],
                                       ost[:, 0:512], start=True, stop=True),
                            "mm")
                        wait(VEC, "mm")
                        inc(VEC.tensor_copy(oall[pr][64:128, q0:q0 + 512],
                                            psD[64:128, 0:512]), "dv")

        if "atab" in tap_out:
            wait_all(GPS, "dv", "ac", "mm")
            wait(GPS, "tp")
            inc(GPS.dma_start(out=tap_out["atab"][:, 0:L], in_=acs[:]), "tp", 16)
            wait(GPS, "tp")
            inc(GPS.dma_start(out=tap_out["atab"][:, L:2 * L], in_=asn[:]),
                "tp", 16)
            for c in range(8):
                wait(GPS, "tp")
                inc(GPS.dma_start(out=tap_out["atab"][:, 2 * L + c * OWN:
                                                      2 * L + (c + 1) * OWN],
                                  in_=oall[c][:]), "tp", 16)
            tap_sync()
        freel(_pctx)
        wob2 = [aa(f"wob2{c}", [128, D], BF16) for c in range(8)]
        wait_all(GPS, "mm", "dv", "ac")
        for c in range(8):
            wait(GPS, "dw")
            inc(GPS.dma_start(out=wob2[c][:],
                              in_=a_wout[c * 128:(c + 1) * 128, :]), "dw", 16)
        for tti in range(16):
            wait(SYN, "dv")
            wait(VEC, "st")
            wait(SYN, "dx")
            inc(SYN.dma_start(out=xrow[:],
                              in_=x_full[tti * 128:(tti + 1) * 128, :]),
                "dx", 16)
            wait(SYN, "dx")
            inc(SYN.dma_start(out=gstage[:],
                              in_=g_dram[:, tti * 1024:(tti + 1) * 1024]),
                "dx", 16)
            for dh in range(2):
                pso = psA if dh == 0 else psB
                wait_all(TEN, "dw", "dv", "ac")
                for c in range(8):
                    inc(TEN.matmul(pso[:, 0:512],
                                   oall[c][:, tti * 128:(tti + 1) * 128],
                                   wob2[c][:, dh * 512:dh * 512 + 512],
                                   start=(c == 0), stop=(c == 7)), "mm")
                wait_all(VEC, "mm", "dx")
                inc(VEC.tensor_tensor(sqg[:, dh * 512:dh * 512 + 512],
                                      pso[:, 0:512],
                                      gstage[:, dh * 512:dh * 512 + 512],
                                      ALU.mult), "dv")
                vd()
                inc(VEC.tensor_tensor(sqg[:, dh * 512:dh * 512 + 512],
                                      sqg[:, dh * 512:dh * 512 + 512],
                                      xrow[:, dh * 512:dh * 512 + 512],
                                      ALU.add), "dv")
            wait(SYN, "dv")
            wait(SYN, "st")
            inc(SYN.dma_start(out=out[tti * 128:(tti + 1) * 128, :],
                              in_=sqg[:]), "st", 16)
        wait(SYN, "st")
        wait(GPS, "st")
        freel(_actx)
        return nc


def _host_inputs(L0_attn_x, L0_attn_w_qkv, L0_attn_w_out, L0_ffn_wqkv,
                 L0_ffn_q_scale, L0_ffn_k_scale, L0_ffn_wo, L0_ffn_ln_g,
                 L0_ffn_ln_b, L0_ffn_w1, L0_ffn_b1, L0_ffn_w2, L0_ffn_b2):
    import ml_dtypes
    x = np.asarray(L0_attn_x, dtype=np.float32)
    f32 = lambda a: np.ascontiguousarray(a, dtype=np.float32)

    a_wqkvT = f32(np.asarray(L0_attn_w_qkv).T)
    a_woutT = f32(np.asarray(L0_attn_w_out).T)
    b_wqkv = f32(L0_ffn_wqkv)
    b_wo = f32(L0_ffn_wo)
    w1 = np.asarray(L0_ffn_w1, dtype=np.float32)
    b1 = np.asarray(L0_ffn_b1, dtype=np.float32)
    w1p = np.zeros((2, D, W1C), dtype=np.float32)
    b1p = np.zeros((2, W1C, 1), dtype=np.float32)
    for j in range(HIDP // 128):
        a0, a1 = j * 128, min((j + 1) * 128, HID)
        n = a1 - a0
        if n > 0:
            w1p[:, :, 256 * j:256 * j + n] = w1[:, :, a0:a1]
            w1p[:, :, 256 * j + 128:256 * j + 128 + n] = w1[:, :, HID + a0:HID + a1]
            b1p[:, 256 * j:256 * j + n, 0] = b1[:, a0:a1]
            b1p[:, 256 * j + 128:256 * j + 128 + n, 0] = b1[:, HID + a0:HID + a1]
    b1p = f32(b1p[:, :, 0].reshape(2, W1C // 128, 128).transpose(0, 2, 1))
    w2p = np.zeros((2, HIDP, D), dtype=np.float32)
    w2p[:, :HID, :] = np.asarray(L0_ffn_w2, dtype=np.float32)
    b_b2 = f32(np.asarray(L0_ffn_b2).reshape(2, 8, 128).transpose(0, 2, 1))
    b_lng = f32(np.asarray(L0_ffn_ln_g).reshape(2, 8, 128).transpose(0, 2, 1))
    b_lnb = f32(np.asarray(L0_ffn_ln_b).reshape(2, 8, 128).transpose(0, 2, 1))
    b_qs = f32(np.tile(np.asarray(L0_ffn_q_scale), (1, HEADS_B))[:, None, :])
    b_ks = f32(np.tile(np.asarray(L0_ffn_k_scale), (1, HEADS_B))[:, None, :])

    _hi = 2.0 * np.arange(32).astype(np.float64)
    _invf64 = 1.0 / 10000.0 ** (_hi / 64.0)
    invf_hi_h = _invf64.astype(np.float32)
    invf_lo_h = (_invf64 - invf_hi_h.astype(np.float64)).astype(np.float32)
    _lnb64 = np.log((_hi + 0.4 * 64.0) / (1.4 * 64.0))
    lnbs_hi_h = _lnb64.astype(np.float32)
    lnbs_lo_h = (_lnb64 - lnbs_hi_h.astype(np.float64)).astype(np.float32)
    onecol = f32(np.ones((128, 1)))
    ident = np.eye(128, dtype=np.float32)
    ident_bf = ident.astype(ml_dtypes.bfloat16)
    P = np.zeros((128, 128), dtype=np.float32)
    for h0 in (0, 64):
        for d in range(32):
            P[h0 + d, h0 + d + 32] = -1.0
            P[h0 + d + 32, h0 + d] = 1.0
    perm_a = np.ascontiguousarray(P.T).astype(ml_dtypes.bfloat16)
    il = np.arange(128)
    mask_lo = np.where(il[:, None] < il[None, :], NEG,
                       0.0).astype(ml_dtypes.bfloat16)
    mask_hi = np.where(il[:, None] > il[None, :], NEG,
                       0.0).astype(ml_dtypes.bfloat16)

    onerow_h = f32(np.ones((1, 128)))
    allones_h = f32(np.ones((128, 128)))
    repm_h = f32((np.arange(128)[None, :] % 32 == np.arange(128)[:, None]) * 1.0)
    shf64_h = (np.arange(128)[None, :] == np.arange(128)[:, None] + 64
               ).astype(ml_dtypes.bfloat16)
    shared = dict(onerow=onerow_h, allones=allones_h, repm=repm_h,
                  shf64=shf64_h,
                  invf_hi=f32(invf_hi_h[None, :]), invf_lo=f32(invf_lo_h[None, :]),
                  lnbs_hi=f32(lnbs_hi_h[None, :]), lnbs_lo=f32(lnbs_lo_h[None, :]),
                  a_wqkv=a_wqkvT, a_wout=a_woutT, b_wqkv=b_wqkv, b_wo=b_wo,
                  b_w1=w1p, b_b1=b1p, b_w2=w2p, b_b2=b_b2, b_lng=b_lng,
                  b_lnb=b_lnb, b_qs=b_qs, b_ks=b_ks, onecol=onecol, ident_bf=ident_bf,
                  ident_f32=ident, perm_a=perm_a, mask_lo=mask_lo,
                  mask_hi=mask_hi)

    in_maps = []
    for c in range(NCORES):
        b, half = c // 2, c % 2
        o = OWN * half
        oth = OWN * (1 - half)
        x_fullm = np.ascontiguousarray(
            np.concatenate([x[b, o:o + OWN], x[b, oth:oth + OWN]], axis=0))
        x_bm = np.zeros((RBN, D), dtype=np.float32)
        lo = max(0, o - HALO)
        x_bm[HALO - (o - lo):] = x[b, lo:o + OWN]
        poskv = np.concatenate([np.arange(o, o + OWN),
                                np.arange(oth, oth + OWN)]).astype(np.float32)
        posk = np.ascontiguousarray(poskv[None, :])
        poska_h = np.ascontiguousarray(poskv.reshape(32, 128).T)
        posbm = np.ascontiguousarray(np.arange(o - HALO, o + OWN, dtype=np.float32).reshape(RBT, 128).T)
        w0 = np.zeros((1, NS1 + NS2), dtype=np.float32)
        if half == 0:
            w0[0, 2] = NEG
            w0[0, NS1 + 0] = NEG
        m = dict(shared)
        m.update(x_full=x_fullm, x_b=np.ascontiguousarray(x_bm), posk=posk,
                 poska=poska_h, posb=posbm, w0row=w0)
        in_maps.append(m)
    return in_maps


def kernel(**inputs):
    in_maps = _host_inputs(**inputs)
    nc = _build()
    res = run_bass_kernel_spmd(nc, in_maps, core_ids=list(range(NCORES)))
    out = np.empty((BATCH, L, D), dtype=np.float32)
    for c in range(NCORES):
        b, half = c // 2, c % 2
        out[b, OWN * half:OWN * (half + 1)] = res.results[c]["out"]
    return out


# version 09 (final)
# speedup vs baseline: 1.0717x; 1.0678x over previous
import math
from contextlib import ExitStack

import numpy as np

import concourse.bass as bass
import concourse.mybir as mybir
from concourse.bass_utils import run_bass_kernel_spmd

F32 = mybir.dt.float32
BF16 = mybir.dt.bfloat16
I32 = mybir.dt.int32
AF = mybir.ActivationFunctionType
ALU = mybir.AluOpType
AX = mybir.AxisListType

D = 1024
L = 4096
BATCH = 4
NCORES = 8
OWN = 2048
HALO = 1664
RBN = HALO + OWN
RBT = RBN // 128
X1L = 768
X1N = X1L + OWN
HEADS_B = 8
DH = 64
INNER = 512
HID = 2730
HIDP = 2816
W1C = 5632
HEADS_A = 16
WIN = 512
NEG = -30000.0
NS1, NS2 = 6, 4

def _split_2pi():
    tp = 2.0 * math.pi
    def trunc12(v):
        f = np.float32(v)
        u = f.view(np.uint32) & np.uint32(0xFFFFF000)
        return float(u.view(np.float32))
    c1 = trunc12(tp)
    c2 = trunc12(tp - c1)
    c3 = float(np.float32(tp - c1 - c2))
    return c1, c2, c3

C2PI_1, C2PI_2, C2PI_3 = _split_2pi()
PI_LO = float(np.nextafter(np.float32(math.pi), np.float32(0.0)))


def _build(upto="all", taps=()):
    nc = bass.Bass("TRN2", target_bir_lowering=False)

    def di(name, shape, dt=F32):
        return nc.dram_tensor(name, shape, dt, kind="ExternalInput").ap()

    x_full = di("x_full", [L, D])
    x_b = di("x_b", [RBN, D])
    posk = di("posk", [1, L])
    posb = di("posb", [128, RBT])
    invf_hi = di("invf_hi", [1, 32])
    invf_lo = di("invf_lo", [1, 32])
    lnbs_hi = di("lnbs_hi", [1, 32])
    lnbs_lo = di("lnbs_lo", [1, 32])
    onecol = di("onecol", [128, 1])
    onerow = di("onerow", [1, 128])
    allones = di("allones", [128, 128])
    repm = di("repm", [128, 128])
    shf64 = di("shf64", [128, 128], BF16)
    poska = di("poska", [128, 32])
    ident_bf = di("ident_bf", [128, 128], BF16)
    ident_f32 = di("ident_f32", [128, 128])
    perm_a = di("perm_a", [128, 128], BF16)
    mask_lo = di("mask_lo", [128, 128], BF16)
    mask_hi = di("mask_hi", [128, 128], BF16)
    w0row = di("w0row", [1, NS1 + NS2])
    a_wqkv = di("a_wqkv", [D, 3 * D])
    a_wout = di("a_wout", [D, D])
    b_wqkv = di("b_wqkv", [2, D, 3 * INNER])
    b_wo = di("b_wo", [2, INNER, D])
    b_w1 = di("b_w1", [2, D, W1C])
    b_b1 = di("b_b1", [2, 128, W1C // 128])
    b_w2 = di("b_w2", [2, HIDP, D])
    b_b2 = di("b_b2", [2, 128, 8])
    b_lng = di("b_lng", [2, 128, 8])
    b_lnb = di("b_lnb", [2, 128, 8])
    b_qs = di("b_qs", [2, 1, INNER])
    b_ks = di("b_ks", [2, 1, INNER])

    xn_dram = [nc.dram_tensor(f"xn_dram{c}", [128, RBN], BF16,
                              kind="Internal").ap() for c in range(8)]
    g_dram = nc.dram_tensor("g_dram", [128, 16 * D], BF16, kind="Internal").ap()
    out = nc.dram_tensor("out", [OWN, D], F32, kind="ExternalOutput").ap()
    tap_out = {}
    for nm, shape in taps:
        tap_out[nm] = nc.dram_tensor("tap_" + nm, list(shape), F32,
                                     kind="ExternalOutput").ap()

    es = ExitStack()
    sems, cnts = {}, {}

    def sem(name):
        if name not in sems:
            sems[name] = es.enter_context(nc.semaphore(name))
            cnts[name] = 0
        return sems[name]

    def inc(instr, name, n=1):
        instr.then_inc(sem(name), n)
        cnts[name] += n

    def wait(eng, name, back=0):
        if name in cnts:
            eng.wait_ge(sem(name), max(0, cnts[name] - back))

    def wait_all(eng, *names):
        for n in names:
            wait(eng, n)

    def sb(name, shape, dt=F32):
        return es.enter_context(nc.sbuf_tensor(name, shape, dt))

    def ps(name, shape, dt=F32):
        return es.enter_context(nc.psum_tensor(name, shape, dt))

    TEN, VEC, ACT, GPS, SYN = nc.tensor, nc.vector, nc.scalar, nc.gpsimd, nc.sync

    def vd():
        VEC.drain()

    with es:
        es.enter_context(nc.Block())

        gstage = sb("gstage", [128, 1024], BF16)
        _xst_ctxs = []
        xst = []
        for c in range(8):
            cm_ = nc.sbuf_tensor(f"xst{c}", [128, X1N], F32, side="right")
            xst.append(cm_.__enter__())
            _xst_ctxs.append(cm_)
        identb = sb("identb", [128, 128], BF16)
        identf = sb("identf", [128, 128], F32)
        onec = sb("onec", [128, 1], F32)
        oner = sb("oner", [1, 128], F32)
        onebf = sb("onebf", [128, 1], BF16)
        masklo = sb("masklo", [128, 128], BF16)
        maskhi = sb("maskhi", [128, 128], BF16)
        w0r = sb("w0r", [1, NS1 + NS2], F32)
        consts = sb("consts", [128, 8], F32)
        _tb_ctxs = []
        tb_csq, tb_snq, tb_csk, tb_snk = [], [], [], []
        for nm in ("tb_csq", "tb_snq", "tb_csk", "tb_snk"):
            cm_ = nc.sbuf_tensor(nm, [128, RBT * 32], F32, side="right")
            t_ = cm_.__enter__()
            _tb_ctxs.append(cm_)
            if nm == "tb_csq":
                tb_csq = t_
            elif nm == "tb_snq":
                tb_snq = t_
            elif nm == "tb_csk":
                tb_csk = t_
            else:
                tb_snk = t_

        psA = ps("psA", [128, 512])
        psB = ps("psB", [128, 512])
        psC = ps("psC", [128, 512])
        psD = ps("psD", [128, 512])
        psE = ps("psE", [128, 512])
        psSm = ps("psSm", [128, 1024])
        psT = ps("psT", [128, 1024], BF16)
        bankrot = [psA, psB, psC, psD, psE]

        for dst, src in ((identb, ident_bf), (identf, ident_f32),
                         (onec, onecol), (oner, onerow), (w0r, w0row)):
            wait(SYN, "dx")
            inc(SYN.dma_start(out=dst[:], in_=src), "dx", 16)
        for dst, src in ((masklo, mask_lo), (maskhi, mask_hi)):
            wait(SYN, "dx")
            inc(SYN.dma_start(out=dst[:], in_=src), "dx", 16)
        wait(GPS, "dw")
        inc(GPS.dma_start(out=onebf[:], in_=onecol), "dw", 16)

        wait(ACT, "dx")
        inc(ACT.activation(consts[:, 0:1], onec[:], AF.Arctan, scale=1.0), "ac")
        wait(VEC, "ac")
        inc(VEC.tensor_scalar(consts[:, 0:1], consts[:, 0:1], 4.0, None, ALU.mult), "dv")
        vd()
        inc(VEC.tensor_scalar(consts[:, 1:2], consts[:, 0:1], 2.0, None, ALU.mult), "dv")
        vd()
        inc(VEC.reciprocal(consts[:, 2:3], consts[:, 1:2]), "dv")
        inc(VEC.tensor_scalar(consts[:, 3:4], consts[:, 0:1], 0.5, None, ALU.mult), "dv")
        inc(VEC.tensor_scalar(consts[:, 5:6], consts[:, 1:2], -1.0, None, ALU.mult), "dv")
        vd()
        wait(ACT, "dv")
        inc(ACT.activation(consts[:, 4:5], onec[:], AF.Ln, scale=10000.0), "ac")
        PI, TWOPI, INV2PI, HFPI, LN1E4, NEG2PI = (consts[:, k:k + 1] for k in range(6))

        def alc(name, shape, dt=F32):
            cm = nc.sbuf_tensor(name, shape, dt)
            return cm.__enter__(), cm

        def freel(ctxs):
            for cm in reversed(ctxs):
                cm.__exit__(None, None, None)

        cst = sb("cst", [128, 640], F32)
        _tsc = []
        def tscal(name, shape, dt=F32):
            t, cm = alc(name, shape, dt)
            _tsc.append(cm)
            return t
        cst_i = sb("cst_i", [128, 512], I32)

        def rrfold(r, y, kf, ki, half=False):
            if half:
                inc(VEC.tensor_scalar(kf, y, INV2PI, 0.5, ALU.mult, ALU.add),
                    "dv")
            else:
                inc(VEC.tensor_scalar(kf, y, INV2PI, None, ALU.mult), "dv")
            vd()
            inc(VEC.tensor_copy(ki, kf), "dv")
            vd()
            inc(VEC.tensor_copy(kf, ki), "dv")
            vd()
            inc(VEC.scalar_tensor_tensor(r, kf, -C2PI_1, y, ALU.mult, ALU.add), "dv")
            vd()
            inc(VEC.scalar_tensor_tensor(r, kf, -C2PI_2, r, ALU.mult, ALU.add), "dv")
            vd()
            inc(VEC.scalar_tensor_tensor(r, kf, -C2PI_3, r, ALU.mult, ALU.add), "dv")
            vd()

        def rrsin(dst, src_ap, width, phase=None, extra=None):
            wait(VEC, "ac")
            t0 = cst[:, 0:width]
            y = src_ap
            if phase is not None:
                inc(VEC.tensor_scalar(t0, src_ap, phase, None, ALU.add), "dv")
                vd()
                y = t0
            kf = cst[:, 512:512 + width]
            ki = cst_i[:, 0:width]
            r = cst[:, 64:64 + width]
            rrfold(r, y, kf, ki, half=True)
            rrfold(r, r, kf, ki)
            if extra is not None:
                inc(VEC.tensor_tensor(r, r, extra, ALU.add), "dv")
                vd()
            inc(VEC.tensor_scalar(r, r, PI_LO, -PI_LO, ALU.min, ALU.max), "dv")
            wait(ACT, "dv")
            inc(ACT.activation(dst, r, AF.Sin, scale=1.0), "ac")

        posb_t = tscal("posb_t", [128, RBT], F32)
        wait(SYN, "dx")
        inc(SYN.dma_start(out=posb_t[:], in_=posb), "dx", 16)
        rowin = tscal("rowin", [1, 32], F32)
        invf_b = tscal("invf_b", [128, 32], F32)
        invl_b = tscal("invl_b", [128, 32], F32)
        lnb_b = tscal("lnb_b", [128, 32], F32)
        lnl_b = tscal("lnl_b", [128, 32], F32)
        wait(TEN, "ac")
        for dst, src in ((invf_b, invf_hi), (invl_b, invf_lo),
                         (lnb_b, lnbs_hi), (lnl_b, lnbs_lo)):
            wait(SYN, "mm")
            wait(SYN, "dx")
            inc(SYN.dma_start(out=rowin[:], in_=src), "dx", 16)
            wait_all(TEN, "dx", "dv")
            inc(TEN.matmul(psA[:, 0:32], oner[:], rowin[:],
                           start=True, stop=True), "mm")
            wait(VEC, "mm")
            inc(VEC.tensor_copy(dst[:], psA[:, 0:32]), "dv")

        angt = tscal("angt", [128, 32], F32)
        angl = tscal("angl", [128, 32], F32)
        sclt = tscal("sclt", [128, 32], F32)
        pwrt = tscal("pwrt", [128, 1], F32)
        sint = tscal("sint", [128, 32], F32)
        cost = tscal("cost", [128, 32], F32)
        for ti in range(RBT):
            pc = posb_t[:, ti:ti + 1]
            inc(VEC.tensor_scalar(angt[:], invf_b[:], pc, None, ALU.mult), "dv")
            inc(VEC.tensor_scalar(angl[:], invl_b[:], pc, None, ALU.mult), "dv")
            inc(VEC.tensor_scalar(pwrt[:], pc, -2048.0, 1.0 / 256.0,
                                  ALU.add, ALU.mult), "dv")
            vd()
            inc(VEC.tensor_scalar(sclt[:], lnb_b[:], pwrt[:], None, ALU.mult), "dv")
            vd()
            inc(VEC.scalar_tensor_tensor(sclt[:], lnl_b[:], pwrt[:], sclt[:],
                                         ALU.mult, ALU.add), "dv")
            wait(ACT, "dv")
            inc(ACT.activation(sclt[:], sclt[:], AF.Exp, scale=1.0), "ac")
            rrsin(sint[:], angt[:], 32, extra=angl[:])
            rrsin(cost[:], angt[:], 32, phase=HFPI, extra=angl[:])
            sl = slice(ti * 32, (ti + 1) * 32)
            wait(VEC, "ac")
            inc(VEC.scalar_tensor_tensor(tb_csq[:, sl], cost[:], 8.0, sclt[:],
                                         ALU.mult, ALU.mult), "dv")
            inc(VEC.scalar_tensor_tensor(tb_snq[:, sl], sint[:], 8.0, sclt[:],
                                         ALU.mult, ALU.mult), "dv")
            vd()
            inc(VEC.reciprocal(sclt[:], sclt[:]), "dv")
            vd()
            inc(VEC.tensor_tensor(tb_csk[:, sl], cost[:], sclt[:], ALU.mult), "dv")
            inc(VEC.tensor_tensor(tb_snk[:, sl], sint[:], sclt[:], ALU.mult), "dv")

        freel(_tsc)
        wait_all(SYN, "dv", "ac")

        xrow = sb("xrow", [128, D], F32)
        lnsum = sb("lnsum", [128, 1], F32)
        lnssq = sb("lnssq", [128, 1], F32)
        lnmu = sb("lnmu", [128, 1], F32)
        lnrs = sb("lnrs", [128, 1], F32)
        lnbi = sb("lnbi", [128, 1], F32)
        sqg = sb("sqg", [128, D], F32)
        _fsc = []
        xnrow, _c = alc("xnrow", [128, D], BF16)
        _fsc.append(_c)

        def ln_tm(dst_ap, src_ap):
            inc(VEC.scalar_tensor_tensor(sqg[:], src_ap, 1.0, src_ap,
                                         ALU.mult, ALU.mult,
                                         accum_out=lnssq[:]), "dv")
            vd()
            inc(VEC.tensor_scalar(sqg[:], src_ap, 0.0, None, ALU.add,
                                  ALU.add, accum_out=lnsum[:]), "dv")
            vd()
            inc(VEC.tensor_scalar(lnmu[:], lnsum[:], 1.0 / D, None, ALU.mult), "dv")
            vd()
            inc(VEC.scalar_tensor_tensor(lnrs[:], lnmu[:], -1.0, lnmu[:],
                                         ALU.mult, ALU.mult), "dv")
            vd()
            inc(VEC.scalar_tensor_tensor(lnrs[:], lnssq[:], 1.0 / D, lnrs[:],
                                         ALU.mult, ALU.add), "dv")
            vd()
            inc(VEC.tensor_scalar(lnrs[:], lnrs[:], 1.0e-5, None, ALU.add), "dv")
            vd()
            wait(ACT, "dv")
            inc(ACT.activation(lnrs[:], lnrs[:], AF.Ln, scale=1.0), "ac")
            wait(VEC, "ac")
            inc(VEC.tensor_scalar(lnrs[:], lnrs[:], -0.5, None, ALU.mult), "dv")
            vd()
            wait(ACT, "dv")
            inc(ACT.activation(lnrs[:], lnrs[:], AF.Exp, scale=1.0), "ac")
            wait(VEC, "ac")
            inc(VEC.scalar_tensor_tensor(lnbi[:], lnmu[:], -1.0, lnrs[:],
                                         ALU.mult, ALU.mult), "dv")
            vd()
            wait_all(ACT, "dv", "mm")
            inc(ACT.activation(dst_ap, src_ap, AF.Identity,
                               scale=lnrs[:], bias=lnbi[:]), "ac")

        def tmcm(dst_list, dst_col, src_bf, nch=8):
            wait(TEN, "ac")
            for c in range(nch):
                wait(TEN, "dv")
                inc(TEN.transpose(psT[:, 0:128],
                                  src_bf[:, c * 128:(c + 1) * 128], identb[:]),
                    "mm")
                wait(VEC, "mm")
                inc(VEC.tensor_copy(dst_list[c][:, dst_col:dst_col + 128],
                                    psT[:, 0:128]), "dv")

        xns = []
        for c in range(8):
            t, _c = alc(f"xns{c}", [128, 1024], BF16)
            xns.append(t)
            _fsc.append(_c)
        for ti in range(RBT):
            bi = ti % 8
            if bi == 0 and ti > 0:
                wait(SYN, "dv")
                for c in range(8):
                    wait(SYN, "st")
                    inc(SYN.dma_start(out=xn_dram[c][:, (ti - 8) * 128:ti * 128],
                                      in_=xns[c][:]), "st", 16)
            wait(SYN, "ac")
            wait(SYN, "dx")
            inc(SYN.dma_start(out=xrow[:], in_=x_b[ti * 128:(ti + 1) * 128, :]),
                "dx", 16)
            wait(VEC, "dx")
            wait(VEC, "ac")
            if bi == 0 and ti > 0:
                wait(VEC, "st")
            ln_tm(xnrow[:], xrow[:])
            tmcm(xns, bi * 128, xnrow)
        wait(SYN, "dv")
        for c in range(8):
            wait(SYN, "st")
            inc(SYN.dma_start(out=xn_dram[c][:, 24 * 128:RBN],
                              in_=xns[c][:, 0:(RBT - 24) * 128]), "st", 16)
        wait(VEC, "st")
        freel(_fsc)

        def tap_sync():
            for e in (VEC, ACT, TEN, SYN, GPS):
                wait(e, "st")
                wait(e, "tp")

        def tap1(nm, src_ap):
            if nm in tap_out:
                wait_all(GPS, "dv", "ac", "mm")
                wait(GPS, "tp")
                inc(GPS.dma_start(out=tap_out[nm], in_=src_ap), "tp", 16)
                tap_sync()

        tap1("csq", tb_csq[:])
        tap1("snq", tb_snq[:])
        tap1("csk", tb_csk[:])
        tap1("snk", tb_snk[:])

        def finish_stub():
            inc(VEC.memset(xrow[:], 0.0), "dv")
            wait(SYN, "dv")
            for ti in range(OWN // 128):
                wait(SYN, "st")
                inc(SYN.dma_start(out=out[ti * 128:(ti + 1) * 128, :],
                                  in_=xrow[:]), "st", 16)
            wait(SYN, "st")

        if upto == "front":
            finish_stub()
            return nc

        def alc(name, shape, dt=F32):
            cm = nc.sbuf_tensor(name, shape, dt)
            return cm.__enter__(), cm

        def freel(ctxs):
            for cm in reversed(ctxs):
                cm.__exit__(None, None, None)

        ssq8 = sb("ssq8", [128, 8], F32)
        rn8 = sb("rn8", [128, 8], F32)
        rop1 = sb("rop1", [128, 512], F32)
        rop2 = sb("rop2", [128, 512], F32)
        qpb = sb("qpb", [128, 512], BF16)
        mxc = sb("mxc", [128, 1], F32)
        negMb = sb("negMb", [128, 1], F32)
        biasM = sb("biasM", [128, 1], F32)
        rnc = sb("rnc", [128, 1], F32)
        nm1 = sb("nm1", [1, 1], F32)
        shst = sb("shst", [128, 8], BF16)
        qkraw = xrow

        def mk_attn_bufs(l):
            ctxs = []
            def a(name, shape, dt):
                t, cm = alc(f"{name}L{l}", shape, dt)
                ctxs.append(cm)
                return t
            bufs = dict(
                kcb=[[a(f"kcb{p}{c}", [128, 512], BF16) for c in range(4)]
                     for p in range(2)],
                qcb=[a(f"qcb{c}", [128, 512], BF16) for c in range(4)],
                vbf=[[a(f"vb{p}{t}", [128, 520], BF16) for t in range(4)]
                     for p in range(2)],
                ptb=a("ptb", [128, 1024], BF16),
                otb=a("otb", [128, 2048], BF16),
                ocb=[a(f"ocb{c}", [128, 512], BF16) for c in range(4)],
                wob=[a(f"wob{c}", [128, D], BF16) for c in range(4)],
                wqb=[a(f"wqb{c}", [128, 3 * INNER], BF16) for c in range(8)],
                qsl=a("qsl", [128, INNER], F32),
                ksl=a("ksl", [128, INNER], F32),
            )
            if l == 0:
                bufs["xnr"] = [a(f"xnr{c}", [128, 2 * 516], BF16)
                               for c in range(8)]
            else:
                bufs["stb"] = [a(f"stb{c}", [128, 520], BF16)
                               for c in range(8)]
            return bufs, ctxs

        def mk_ffn_bufs(l):
            ctxs = []
            def a(name, shape, dt):
                t, cm = alc(f"{name}F{l}", shape, dt)
                ctxs.append(cm)
                return t
            bufs = dict(
                hin=[a(f"hin{c}", [128, 512], BF16) for c in range(8)],
                gel=a("fgel", [128, 512], BF16),
                hb=[a(f"hb{j}", [128, 512], BF16) for j in range(22)],
                w1b=[a(f"w1b{p}", [128, 2048], BF16) for p in range(2)],
                w2b=[a(f"w2b{j}", [128, D], BF16) for j in range(22)],
                b1b=a("b1b", [128, W1C // 128], F32),
                b2b=a("b2b", [128, 8], F32),
                lngb=a("lngb", [128, 8], F32),
                lnbb=a("lnbb", [128, 8], F32),
            )
            return bufs, ctxs

        def rope_tm(ab, srcT, col0, is_q, rbti):
            src_ap = srcT[:, col0:col0 + 512]
            sc = ab["qsl"] if is_q else ab["ksl"]
            cs = (tb_csq if is_q else tb_csk)[:, rbti * 32:(rbti + 1) * 32]
            sn = (tb_snq if is_q else tb_snk)[:, rbti * 32:(rbti + 1) * 32]
            cs_b = cs.rearrange("p (a b d) -> p a b d", a=1, b=1
                                ).broadcast_to([128, 8, 2, 32])
            sn_b = sn.rearrange("p (a d) -> p a d", a=1
                                ).broadcast_to([128, 8, 32])
            inc(VEC.tensor_tensor(sqg[:, 0:512], src_ap, src_ap, ALU.mult), "dv")
            vd()
            inc(VEC.tensor_reduce(ssq8[:], sqg[:, 0:512].rearrange(
                "p (h d) -> p h d", h=8), AX.X, ALU.add), "dv")
            vd()
            inc(VEC.tensor_scalar(ssq8[:], ssq8[:], 1.0e-18, None, ALU.max), "dv")
            wait(ACT, "dv")
            inc(ACT.activation(rn8[:], ssq8[:], AF.Ln, scale=1.0), "ac")
            wait(VEC, "ac")
            inc(VEC.tensor_scalar(rn8[:], rn8[:], -0.5, None, ALU.mult), "dv")
            vd()
            wait(ACT, "dv")
            inc(ACT.activation(rn8[:], rn8[:], AF.Exp, scale=1.0), "ac")
            wait(VEC, "ac")
            for h in range(8):
                inc(VEC.tensor_scalar(sqg[:, h * 64:(h + 1) * 64],
                                      srcT[:, col0 + h * 64:col0 + (h + 1) * 64],
                                      rn8[:, h:h + 1], None, ALU.mult), "dv")
            vd()
            inc(VEC.tensor_tensor(sqg[:, 0:512], sqg[:, 0:512], sc[:],
                                  ALU.mult), "dv")
            vd()
            sq4 = sqg[:, 0:512].rearrange("p (h f d) -> p h f d", h=8, f=2)
            r14 = rop1[:].rearrange("p (h f d) -> p h f d", h=8, f=2)
            q4 = qpb[:].rearrange("p (h f d) -> p h f d", h=8, f=2)
            r23 = rop2[:].rearrange("p (h f d) -> p h f d", h=8, f=2)
            inc(VEC.tensor_tensor(r14, sq4, cs_b, ALU.mult), "dv")
            inc(VEC.tensor_tensor(r23[:, :, 0, :], sq4[:, :, 1, :], sn_b,
                                  ALU.mult), "dv")
            inc(VEC.tensor_tensor(r23[:, :, 1, :], sq4[:, :, 0, :], sn_b,
                                  ALU.mult), "dv")
            vd()
            inc(VEC.tensor_tensor(q4[:, :, 0, :], r14[:, :, 0, :],
                                  r23[:, :, 0, :], ALU.subtract), "dv")
            inc(VEC.tensor_tensor(q4[:, :, 1, :], r14[:, :, 1, :],
                                  r23[:, :, 1, :], ALU.add), "dv")

        def kv_build(l, ab, kw, build_q):
            kw0t = -1536 if l == 0 else -512
            wt = kw0t + 512 * kw
            par = kw & 1
            if l == 0:
                r = kw % 2
                wait_all(SYN, "mm", "dv", "st")
                for c in range(8):
                    wait(SYN, "dx")
                    inc(SYN.dma_start(
                        out=ab["xnr"][c][:, r * 516:r * 516 + 513],
                        in_=xn_dram[c][:, wt + 1664 - 1:wt + 1664 + 512]),
                        "dx", 16)
            else:
                wait(VEC, "mm")
                for c in range(8):
                    if kw == 0:
                        inc(VEC.tensor_copy(ab["stb"][c][:, 0:513],
                                            xst[c][:, wt + 768 - 1:
                                                   wt + 768 + 512]), "dv")
                    else:
                        inc(VEC.tensor_copy(ab["stb"][c][:, 0:1],
                                            shst[:, c:c + 1]), "dv")
                        inc(VEC.tensor_copy(ab["stb"][c][:, 1:513],
                                            xst[c][:, wt + 768:
                                                   wt + 768 + 512]), "dv")
                    vd()
                    inc(VEC.tensor_copy(shst[:, c:c + 1],
                                        ab["stb"][c][:, 512:513]), "dv")
            for tt in range(4):
                t0 = wt + tt * 128
                rbti = (t0 + 1664) // 128
                wait_all(TEN, "ac", "dv", "dx")
                for c in range(8):
                    d = 1 if c >= 4 else 0
                    if l == 0:
                        r = kw % 2
                        u = ab["xnr"][c][:, r * 516 + tt * 128 + 1 - d:
                                         r * 516 + tt * 128 + 129 - d]
                    else:
                        u = ab["stb"][c][:, tt * 128 + 1 - d:tt * 128 + 129 - d]
                    if build_q:
                        inc(TEN.matmul(psA[:, 0:512], u, ab["wqb"][c][:, 0:512],
                                       start=(c == 0), stop=(c == 7)), "mm")
                    inc(TEN.matmul(psB[:, 0:512], u, ab["wqb"][c][:, 512:1024],
                                   start=(c == 0), stop=(c == 7)), "mm")
                    inc(TEN.matmul(psC[:, 0:512], u, ab["wqb"][c][:, 1024:1536],
                                   start=(c == 0), stop=(c == 7)), "mm")
                wait_all(ACT, "mm", "dv")
                if build_q:
                    inc(ACT.copy(qkraw[:, 0:512], psA[:, 0:512]), "ac")
                inc(ACT.copy(qkraw[:, 512:1024], psB[:, 0:512]), "ac")
                wait(VEC, "mm")
                vvb = ab["vbf"][par][tt][:].rearrange("p (h d) -> p h d",
                                                      h=8, d=65)
                inc(VEC.tensor_copy(vvb[:, :, 0:64], psC[:, 0:512].rearrange(
                    "p (h d) -> p h d", h=8)), "dv")
                wait(VEC, "ac")
                if build_q:
                    rope_tm(ab, qkraw, 0, True, rbti)
                    wait_all(TEN, "dv")
                    for cc in range(4):
                        wait(TEN, "dv")
                        inc(TEN.transpose(psT[:, 0:128],
                                          qpb[:, cc * 128:(cc + 1) * 128],
                                          identb[:]), "mm")
                        wait(VEC, "mm")
                        inc(VEC.tensor_copy(
                            ab["qcb"][cc][:, tt * 128:(tt + 1) * 128],
                            psT[:, 0:128]), "dv")
                rope_tm(ab, qkraw, 512, False, rbti)
                wait(TEN, "dv")
                for cc in range(4):
                    wait(TEN, "dv")
                    inc(TEN.transpose(psT[:, 0:128],
                                      qpb[:, cc * 128:(cc + 1) * 128],
                                      identb[:]), "mm")
                    wait(VEC, "mm")
                    inc(VEC.tensor_copy(
                        ab["kcb"][par][cc][:, tt * 128:(tt + 1) * 128],
                        psT[:, 0:128]), "dv")

        def attn_slot(l, ab, s):
            sl0t = -1024 if l == 0 else 0
            qwt = sl0t + 512 * s
            pv, ow = s & 1, (s + 1) & 1
            w0i = s if l == 0 else NS1 + s
            wait_all(TEN, "dv", "ac")
            first = True
            for h in range(8):
                cc, hh = h // 2, (h % 2) * 64
                for qb in range(4):
                    lq = ab["qcb"][cc][hh:hh + 64, qb * 128:(qb + 1) * 128]
                    plen = 512 - qb * 128
                    olen = 128 * (qb + 1)
                    wait(TEN, "dv")
                    inc(TEN.matmul(psSm[:, 0:plen], lq,
                                   ab["kcb"][pv][cc][hh:hh + 64, qb * 128:512],
                                   start=True, stop=True), "mm")
                    inc(TEN.matmul(psSm[:, 512:512 + olen], lq,
                                   ab["kcb"][ow][cc][hh:hh + 64, 0:olen],
                                   start=True, stop=True), "mm")
                    wait(VEC, "mm")
                    inc(VEC.tensor_scalar(rop1[:, 0:plen], psSm[:, 0:plen],
                                          0.0, None if first else mxc[:],
                                          ALU.add, ALU.max,
                                          accum_out=mxc[:]), "dv")
                    vd()
                    inc(VEC.tensor_scalar(rop1[:, 0:olen],
                                          psSm[:, 512:512 + olen],
                                          0.0, mxc[:], ALU.add, ALU.max,
                                          accum_out=mxc[:]), "dv")
                    vd()
                    first = False
            wait(TEN, "dv")
            inc(TEN.transpose(psE[0:1, 0:128], mxc[:], identf[:]), "mm")
            wait(VEC, "mm")
            inc(VEC.tensor_reduce(nm1[:], psE[0:1, 0:128], AX.X, ALU.max,
                                  negate=True), "dv")
            vd()
            wait(TEN, "dv")
            inc(TEN.matmul(psE[:, 4:5], oner[:], nm1[:], start=True,
                           stop=True), "mm")
            wait(VEC, "mm")
            inc(VEC.tensor_copy(negMb[:], psE[:, 4:5]), "dv")
            vd()
            inc(VEC.tensor_tensor(biasM[:], negMb[:], w0b[:, w0i:w0i + 1],
                                  ALU.add), "dv")
            vd()
            psOsl = [(psC, 0), (psD, 0), (psSm, 0), (psSm, 512)]
            for h in range(8):
                cc, hh = h // 2, (h % 2) * 64
                wait_all(TEN, "dv", "ac")
                for kt in range(8):
                    qlo = max(0, (kt - 4) * 128)
                    qhi = min(512, (kt + 1) * 128)
                    par = pv if kt < 4 else ow
                    tt = kt % 4
                    stp = psA if (kt & 1) == 0 else psB
                    pcol = 512 * (kt & 1)
                    wait_all(TEN, "ac", "dv")
                    inc(TEN.matmul(stp[:, 0:qhi - qlo],
                                   ab["kcb"][par][cc][hh:hh + 64,
                                                      tt * 128:(tt + 1) * 128],
                                   ab["qcb"][cc][hh:hh + 64, qlo:qhi],
                                   start=True, stop=True), "mm")
                    wait(VEC, "mm")
                    if kt <= 3:
                        d0 = kt * 128 - qlo
                        inc(VEC.tensor_tensor(stp[:, d0:d0 + 128],
                                              stp[:, d0:d0 + 128], masklo[:],
                                              ALU.add), "dv")
                    else:
                        inc(VEC.tensor_tensor(stp[:, 0:128], stp[:, 0:128],
                                              maskhi[:], ALU.add), "dv")
                    wait_all(ACT, "dv", "mm")
                    bias = biasM if kt <= 3 else negMb
                    inc(ACT.activation(ab["ptb"][:, pcol + qlo:pcol + qhi],
                                       stp[:, 0:qhi - qlo], AF.Exp,
                                       bias=bias[:], scale=1.0), "ac")
                    wait(TEN, "ac")
                    for qb in range(max(0, kt - 4), min(3, kt) + 1):
                        pt, c0 = psOsl[qb]
                        inc(TEN.matmul(
                            pt[:, c0:c0 + 65],
                            ab["ptb"][:, pcol + qb * 128:pcol + (qb + 1) * 128],
                            ab["vbf"][par][tt][:, h * 65:h * 65 + 65],
                            start=(kt == qb), stop=(kt == qb + 4)), "mm")
                wait(VEC, "mm")
                for qb in range(4):
                    pt, c0 = psOsl[qb]
                    inc(VEC.reciprocal(rnc[:], pt[:, c0 + 64:c0 + 65]), "dv")
                    vd()
                    inc(VEC.tensor_scalar(
                        ab["otb"][:, qb * 512 + h * 64:qb * 512 + h * 64 + 64],
                        pt[:, c0:c0 + 64], rnc[:], None, ALU.mult), "dv")
                    vd()
            for qb in range(4):
                for cc in range(4):
                    wait_all(TEN, "dv", "ac")
                    inc(TEN.transpose(psT[:, 0:128],
                                      ab["otb"][:, qb * 512 + cc * 128:
                                                qb * 512 + (cc + 1) * 128],
                                      identb[:]), "mm")
                    wait(VEC, "mm")
                    inc(VEC.tensor_copy(ab["ocb"][cc][:, qb * 128:(qb + 1) * 128],
                                        psT[:, 0:128]), "dv")
            e0 = 256 if (l == 0 and s == 0) else 0
            for co in range(8):
                pso = psA if (co & 1) == 0 else psB
                wait_all(TEN, "dv", "ac")
                for cc in range(4):
                    inc(TEN.matmul(pso[:, 0:512 - e0],
                                   ab["wob"][cc][:, co * 128:(co + 1) * 128],
                                   ab["ocb"][cc][:, e0:512],
                                   start=(cc == 0), stop=(cc == 3)), "mm")
                wait(VEC, "mm")
                c0 = qwt + 768 + e0
                c1 = qwt + 768 + 512
                if l == 0:
                    r = (s + 1) % 2
                    inc(VEC.tensor_tensor(
                        xst[co][:, c0:c1], pso[:, 0:512 - e0],
                        ab["xnr"][co][:, r * 516 + 1 + e0:r * 516 + 513],
                        ALU.add), "dv")
                else:
                    inc(VEC.tensor_tensor(xst[co][:, c0:c1],
                                          pso[:, 0:512 - e0],
                                          xst[co][:, c0:c1], ALU.add), "dv")

        def ffn(l, fb):
            E0 = -640 if l == 0 else 0
            E1 = 2048
            org = 768
            wait_all(SYN, "mm", "dv", "ac")
            wait_all(GPS, "mm", "dv", "ac")
            for dst, srcd in ((fb["b1b"], b_b1), (fb["b2b"], b_b2),
                              (fb["lngb"], b_lng), (fb["lnbb"], b_lnb)):
                wait(SYN, "dx")
                inc(SYN.dma_start(out=dst[:], in_=srcd[l]), "dx", 16)
            for j in range(22):
                wait(GPS, "dw")
                inc(GPS.dma_start(out=fb["w2b"][j][:],
                                  in_=b_w2[l][j * 128:(j + 1) * 128, :]),
                    "dw", 16)
            w1src = b_w1[l].rearrange("(c p) w -> p c w", p=128)
            ngr = (E1 - E0 + 511) // 512
            for g in range(ngr - 1, -1, -1):
                T0 = E0 + 512 * g
                W = min(512, E1 - T0)
                wait(GPS, "mm")
                wait(GPS, "dw")
                inc(GPS.dma_start(out=fb["w1b"][0][:],
                                  in_=w1src[:, :, 0:256]), "dw", 16)
                wait_all(TEN, "dv", "ac")
                stb1, sqb = fb["hin"][0], fb["gel"]
                for c in range(8):
                    d = 1 if c >= 4 else 0
                    wait_all(VEC, "mm")
                    inc(VEC.tensor_copy(stb1[:, 0:W],
                                        xst[c][:, T0 + org - d:
                                               T0 + org - d + W]), "dv")
                    vd()
                    inc(VEC.tensor_tensor(sqb[:, 0:W], stb1[:, 0:W],
                                          stb1[:, 0:W], ALU.mult), "dv")
                    wait(TEN, "dv")
                    inc(TEN.matmul(psSm[0:1, 0:W], onebf[:], stb1[:, 0:W],
                                   start=(c == 0), stop=(c == 7)), "mm")
                    inc(TEN.matmul(psSm[0:1, 512:512 + W], onebf[:],
                                   sqb[:, 0:W],
                                   start=(c == 0), stop=(c == 7)), "mm")
                wait(VEC, "mm")
                inc(VEC.tensor_scalar(rop1[0:1, 0:W], psSm[0:1, 0:W], 1.0 / D,
                                      None, ALU.mult), "dv")
                inc(VEC.tensor_scalar(rop2[0:1, 0:W], psSm[0:1, 512:512 + W],
                                      1.0 / D, None, ALU.mult), "dv")
                vd()
                inc(VEC.tensor_tensor(cst[0:1, 0:W], rop1[0:1, 0:W],
                                      rop1[0:1, 0:W], ALU.mult), "dv")
                vd()
                inc(VEC.tensor_tensor(rop2[0:1, 0:W], rop2[0:1, 0:W],
                                      cst[0:1, 0:W], ALU.subtract), "dv")
                vd()
                inc(VEC.tensor_scalar(rop2[0:1, 0:W], rop2[0:1, 0:W], 1.0e-5,
                                      None, ALU.add), "dv")
                wait(ACT, "dv")
                inc(ACT.activation(rop2[0:1, 0:W], rop2[0:1, 0:W], AF.Ln,
                                   scale=1.0), "ac")
                wait(VEC, "ac")
                inc(VEC.tensor_scalar(rop2[0:1, 0:W], rop2[0:1, 0:W], -0.5,
                                      None, ALU.mult), "dv")
                vd()
                wait(ACT, "dv")
                inc(ACT.activation(rop2[0:1, 0:W], rop2[0:1, 0:W], AF.Exp,
                                   scale=1.0), "ac")
                wait_all(TEN, "ac", "dv")
                inc(TEN.matmul(psD[:, 0:W], oner[:], rop1[0:1, 0:W],
                               start=True, stop=True), "mm")
                inc(TEN.matmul(psE[:, 0:W], oner[:], rop2[0:1, 0:W],
                               start=True, stop=True), "mm")
                wait(VEC, "mm")
                wait(ACT, "dx")
                for c in range(8):
                    d = 1 if c >= 4 else 0
                    wait(VEC, "ac")
                    inc(VEC.tensor_tensor(sqg[:, 0:W],
                                          xst[c][:, T0 + org - d:
                                                 T0 + org - d + W],
                                          psD[:, 0:W], ALU.subtract), "dv")
                    vd()
                    inc(VEC.tensor_tensor(sqg[:, 0:W], sqg[:, 0:W],
                                          psE[:, 0:W], ALU.mult), "dv")
                    wait(ACT, "dv")
                    inc(ACT.activation(fb["hin"][c][:, 0:W], sqg[:, 0:W],
                                       AF.Identity,
                                       scale=fb["lngb"][:, c:c + 1],
                                       bias=fb["lnbb"][:, c:c + 1]), "ac")
                    if l == 0 and g == 2 and c == 0 and "sqgt" in tap_out:
                        wait_all(GPS, "dv", "ac", "mm")
                        wait(GPS, "tp")
                        inc(GPS.dma_start(out=tap_out["sqgt"][:, 0:512],
                                          in_=sqg[:, 0:512]), "tp", 16)
                        wait(GPS, "tp")
                        inc(GPS.dma_start(out=tap_out["sqgt"][:, 512:1024],
                                          in_=fb["lngb"][:].broadcast_to(
                                              [128, 8]) if False else
                                          fb["lngb"][:]), "tp", 16) if False \
                            else None
                        wait(GPS, "tp")
                        inc(GPS.dma_start(out=tap_out["sqgt"][:, 512:520],
                                          in_=fb["lngb"][:]), "tp", 16)
                        wait(GPS, "tp")
                        inc(GPS.dma_start(out=tap_out["sqgt"][:, 520:528],
                                          in_=fb["lnbb"][:]), "tp", 16)
                        tap_sync()
                if l == 0 and g == 2 and "stats" in tap_out:
                    wait_all(GPS, "dv", "ac", "mm")
                    wait(GPS, "tp")
                    inc(GPS.dma_start(out=tap_out["stats"][:, 0:W],
                                      in_=rop1[0:1, 0:W]), "tp", 16)
                    wait(GPS, "tp")
                    inc(GPS.dma_start(out=tap_out["stats"][:, 512:512 + W],
                                      in_=rop2[0:1, 0:W]), "tp", 16)
                    tap_sync()
                if l == 0 and g == 2 and "hin" in tap_out:
                    wait_all(GPS, "dv", "ac", "mm")
                    for c in range(8):
                        wait(GPS, "tp")
                        inc(GPS.dma_start(out=tap_out["hin"][:, c * 512:(c + 1) * 512],
                                          in_=fb["hin"][c][:]), "tp", 16)
                    tap_sync()
                for j in range(22):
                    pp = j & 1
                    if j < 21:
                        wait(GPS, "mm")
                        wait(GPS, "dw")
                        inc(GPS.dma_start(
                            out=fb["w1b"][1 - pp][:],
                            in_=w1src[:, :, (j + 1) * 256:(j + 2) * 256]),
                            "dw", 16)
                    wait_all(TEN, "dw", "ac", "dv")
                    for c in range(8):
                        inc(TEN.matmul(psA[:, 0:W],
                                       fb["w1b"][pp][:, c * 256:c * 256 + 128],
                                       fb["hin"][c][:, 0:W],
                                       start=(c == 0), stop=(c == 7)), "mm")
                        inc(TEN.matmul(psB[:, 0:W],
                                       fb["w1b"][pp][:, c * 256 + 128:
                                                     c * 256 + 256],
                                       fb["hin"][c][:, 0:W],
                                       start=(c == 0), stop=(c == 7)), "mm")
                    wait_all(ACT, "mm", "dv")
                    inc(ACT.activation(fb["gel"][:, 0:W], psB[:, 0:W], AF.Gelu,
                                       bias=fb["b1b"][:, 2 * j + 1:2 * j + 2],
                                       scale=1.0), "ac")
                    inc(ACT.activation(rop1[:, 0:W], psA[:, 0:W], AF.Identity,
                                       bias=fb["b1b"][:, 2 * j:2 * j + 1],
                                       scale=1.0), "ac")
                    wait(VEC, "ac")
                    inc(VEC.tensor_tensor(fb["hb"][j][:, 0:W], rop1[:, 0:W],
                                          fb["gel"][:, 0:W], ALU.mult), "dv")
                if l == 0 and g == 2 and "hbt" in tap_out:
                    wait_all(GPS, "dv", "ac", "mm")
                    for j in range(4):
                        wait(GPS, "tp")
                        inc(GPS.dma_start(out=tap_out["hbt"][:, j * 512:(j + 1) * 512],
                                          in_=fb["hb"][j][:]), "tp", 16)
                    tap_sync()
                for od in range(8):
                    wait_all(TEN, "dv", "ac", "dw")
                    for j in range(22):
                        inc(TEN.matmul(psC[:, 0:W],
                                       fb["w2b"][j][:, od * 128:(od + 1) * 128],
                                       fb["hb"][j][:, 0:W],
                                       start=(j == 0), stop=(j == 21)), "mm")
                    wait_all(ACT, "mm", "dv")
                    inc(ACT.activation(rop2[:, 0:W], psC[:, 0:W], AF.Identity,
                                       bias=fb["b2b"][:, od:od + 1],
                                       scale=1.0), "ac")
                    wait(VEC, "ac")
                    inc(VEC.tensor_tensor(xst[od][:, T0 + org:T0 + org + W],
                                          xst[od][:, T0 + org:T0 + org + W],
                                          rop2[:, 0:W], ALU.add), "dv")
                    vd()

        w0b = sb("w0b", [128, NS1 + NS2], F32)
        wait_all(TEN, "dx", "dv", "ac")
        inc(TEN.matmul(psE[:, 0:NS1 + NS2], oner[:], w0r[:],
                       start=True, stop=True), "mm")
        wait(VEC, "mm")
        inc(VEC.tensor_copy(w0b[:], psE[:, 0:NS1 + NS2]), "dv")

        def local_layer(l):
            ab, ctxs = mk_attn_bufs(l)
            wait_all(GPS, "mm", "dv", "ac", "st")
            for c in range(8):
                wait(GPS, "dw")
                inc(GPS.dma_start(out=ab["wqb"][c][:],
                                  in_=b_wqkv[l][c * 128:(c + 1) * 128, :]),
                    "dw", 16)
            for cc in range(4):
                wait(GPS, "dw")
                inc(GPS.dma_start(out=ab["wob"][cc][:],
                                  in_=b_wo[l][cc * 128:(cc + 1) * 128, :]),
                    "dw", 16)
            for dst, srcd in ((ab["qsl"], b_qs), (ab["ksl"], b_ks)):
                wait_all(SYN, "mm", "dv")
                wait(SYN, "dx")
                inc(SYN.dma_start(out=rop1[0:1, 0:INNER], in_=srcd[l]), "dx", 16)
                wait_all(TEN, "dx", "dv", "ac")
                inc(TEN.matmul(psE[:, 0:INNER], oner[:], rop1[0:1, 0:INNER],
                               start=True, stop=True), "mm")
                wait(VEC, "mm")
                inc(VEC.tensor_copy(dst[:], psE[:, 0:INNER]), "dv")
            wait(VEC, "dw")
            for p in range(2):
                for t in range(4):
                    inc(VEC.memset(ab["vbf"][p][t][:].rearrange(
                        "p (h d) -> p h d", h=8, d=65)[:, :, 64:65], 1.0), "dv")
            wait_all(TEN, "dw", "dx", "dv")
            nsl = NS1 if l == 0 else NS2
            kv_build(l, ab, 0, False)
            for s in range(nsl):
                kv_build(l, ab, s + 1, True)
                attn_slot(l, ab, s)
                if s == 0 and l == 1 and "otb1" in tap_out:
                    wait_all(GPS, "dv", "ac", "mm")
                    wait(GPS, "tp")
                    inc(GPS.dma_start(out=tap_out["otb1"], in_=ab["otb"][:]),
                        "tp", 16)
                    wait(GPS, "tp")
                    inc(GPS.dma_start(out=tap_out["negm1"][:, 0:1],
                                      in_=negMb[:],
                                      allow_slow_non_contiguous=True), "tp", 16)
                    for c in range(8):
                        wait(GPS, "tp")
                        inc(GPS.dma_start(
                            out=tap_out["stb1t"][:, c * 520:(c + 1) * 520],
                            in_=ab["stb"][c][:]), "tp", 16)
                    tap_sync()
                if s == 0 and l == 0:
                    if "otb" in tap_out:
                        wait_all(GPS, "dv", "ac", "mm")
                        wait(GPS, "tp")
                        inc(GPS.dma_start(out=tap_out["otb"], in_=ab["otb"][:]),
                            "tp", 16)
                        tap_sync()
                    if "qcb" in tap_out:
                        wait_all(GPS, "dv", "ac", "mm")
                        for cc in range(4):
                            wait(GPS, "tp")
                            inc(GPS.dma_start(
                                out=tap_out["qcb"][:, cc * 512:(cc + 1) * 512],
                                in_=ab["qcb"][cc][:]), "tp", 16)
                        for cc in range(4):
                            wait(GPS, "tp")
                            inc(GPS.dma_start(
                                out=tap_out["qcb"][:, 2048 + cc * 512:
                                                 2048 + (cc + 1) * 512],
                                in_=ab["kcb"][0][cc][:]), "tp", 16)
                        for cc in range(4):
                            wait(GPS, "tp")
                            inc(GPS.dma_start(
                                out=tap_out["qcb"][:, 4096 + cc * 512:
                                                 4096 + (cc + 1) * 512],
                                in_=ab["kcb"][1][cc][:]), "tp", 16)
                        tap_sync()
                    if "negm" in tap_out:
                        wait_all(GPS, "dv", "ac", "mm")
                        wait(GPS, "tp")
                        inc(GPS.dma_start(out=tap_out["negm"][:, 0:1],
                                          in_=negMb[:],
                                          allow_slow_non_contiguous=True),
                            "tp", 16)
                        wait(GPS, "tp")
                        inc(GPS.dma_start(out=tap_out["negm"][:, 1:2],
                                          in_=mxc[:],
                                          allow_slow_non_contiguous=True),
                            "tp", 16)
                        tap_sync()
                    if "xnr" in tap_out:
                        wait_all(GPS, "dv", "ac", "mm")
                        for c in range(8):
                            wait(GPS, "tp")
                            inc(GPS.dma_start(
                                out=tap_out["xnr"][:, c * 1032:(c + 1) * 1032],
                                in_=ab["xnr"][c][:]), "tp", 16)
                        tap_sync()
            if l == 1 and "negm1" in tap_out:
                pass
            if l == 1 and "pa1" in tap_out:
                wait_all(GPS, "dv", "ac", "mm")
                for c in range(8):
                    wait(GPS, "tp")
                    inc(GPS.dma_start(out=tap_out["pa1"][:, c * X1N:(c + 1) * X1N],
                                      in_=xst[c][:]), "tp", 16)
                tap_sync()
            if l == 0 and "pa0" in tap_out:
                wait_all(GPS, "dv", "ac", "mm")
                for c in range(8):
                    wait(GPS, "tp")
                    inc(GPS.dma_start(out=tap_out["pa0"][:, c * X1N:(c + 1) * X1N],
                                      in_=xst[c][:]), "tp", 16)
                tap_sync()
            freel(ctxs)
            fb, fctxs = mk_ffn_bufs(l)
            ffn(l, fb)
            freel(fctxs)

        def tap_state(nm):
            if nm in tap_out:
                wait_all(GPS, "dv", "ac", "mm")
                for c in range(8):
                    wait(GPS, "tp")
                    inc(GPS.dma_start(out=tap_out[nm][:, c * X1N:(c + 1) * X1N],
                                      in_=xst[c][:]), "tp", 16)
                tap_sync()

        local_layer(0)
        if upto == "l1":
            tap_state("xst")
            finish_stub()
            return nc
        local_layer(1)

        for ti in range(16):
            for c in range(8):
                wait_all(TEN, "ac", "dv")
                if c == 0:
                    wait(TEN, "st")
                    wait(ACT, "st")
                inc(TEN.transpose(psA[:, 0:128],
                                  xst[c][:, 768 + ti * 128:768 + (ti + 1) * 128],
                                  identf[:]), "mm")
                wait(ACT, "mm")
                inc(ACT.activation(gstage[:, c * 128:(c + 1) * 128],
                                   psA[:, 0:128], AF.Silu, scale=1.0), "ac")
            wait(SYN, "ac")
            wait(SYN, "st")
            inc(SYN.dma_start(out=g_dram[:, ti * 1024:(ti + 1) * 1024],
                              in_=gstage[:]), "st", 16)

        if "g" in tap_out:
            wait_all(GPS, "st", "ac")
            wait(GPS, "tp")
            inc(GPS.dma_start(out=tap_out["g"], in_=g_dram), "tp", 16)
            tap_sync()
        if upto in ("l2", "lb"):
            tap_state("xst")
            finish_stub()
            return nc

        for cm_ in reversed(_tb_ctxs):
            cm_.__exit__(None, None, None)
        for cm_ in reversed(_xst_ctxs):
            cm_.__exit__(None, None, None)
        _actx = []
        def aa(name, shape, dt):
            t, cm_ = alc(name + "A", shape, dt)
            _actx.append(cm_)
            return t
        xnf = [aa(f"xnf{c}", [128, L], BF16) for c in range(8)]
        acs = aa("acs", [128, L], F32)
        asn = aa("asn", [128, L], F32)
        oall = [aa(f"oall{c}", [128, OWN], BF16) for c in range(8)]
        onesA = aa("onesA", [128, 128], F32)
        permb = aa("permb", [128, 128], BF16)
        shf64b = aa("shf64b", [128, 128], BF16)
        _pctx = []
        def pa(name, shape, dt):
            t, cm_ = alc(name + "P", shape, dt)
            _pctx.append(cm_)
            return t
        K2 = pa("K2", [128, L], BF16)
        Q2 = pa("Q2", [128, OWN], BF16)
        Vc = pa("Vc", [128, L], BF16)
        vbh = [pa(f"vbh{h}", [128, 32 * 65], BF16) for h in range(2)]
        ptA = pa("ptA", [128, 1024], BF16)
        wkp = pa("wkp", [128, 1024], BF16)
        wqp = pa("wqp", [128, 1024], BF16)
        wvp = pa("wvp", [128, 1024], BF16)
        xnrA = pa("xnrA", [128, D], BF16)
        repA = pa("repA", [128, 128], F32)
        ivA = pa("ivA", [128, 32], F32)
        ivlA = pa("ivlA", [128, 32], F32)
        posA = pa("posA", [128, 32], F32)
        tmt = pa("tmt", [128, 32], F32)
        tml = pa("tml", [128, 32], F32)
        t32 = pa("t32", [128, 128], F32)
        ost = pa("ost", [128, 512], BF16)
        rowin2 = pa("rowin2", [1, 32], F32)
        kab = pa("kab", [128, L], BF16)
        qab = pa("qab", [128, OWN], BF16)
        nsc = pa("nsc", [1, 4], F32)

        wait_all(GPS, "mm", "dv", "ac", "st")
        wait_all(SYN, "mm", "dv", "ac", "st")
        wait_all(VEC, "mm", "ac")
        wait_all(TEN, "dv", "ac")
        wait(SYN, "dx")
        inc(SYN.dma_start(out=onesA[:], in_=allones), "dx", 16)
        wait(SYN, "dx")
        inc(SYN.dma_start(out=repA[:], in_=repm), "dx", 16)
        wait(SYN, "dx")
        inc(SYN.dma_start(out=posA[:], in_=poska), "dx", 16)
        wait(GPS, "dw")
        inc(GPS.dma_start(out=permb[:], in_=perm_a), "dw", 16)
        wait(GPS, "dw")
        inc(GPS.dma_start(out=shf64b[:], in_=shf64), "dw", 16)
        wait(SYN, "dx")
        inc(SYN.dma_start(out=rowin2[:], in_=invf_hi), "dx", 16)
        inc(VEC.memset(ost[:], 0.0), "dv")
        wait_all(TEN, "dx", "dv", "ac")
        inc(TEN.matmul(psA[:, 0:32], oner[:], rowin2[:], start=True, stop=True),
            "mm")
        wait(VEC, "mm")
        inc(VEC.tensor_copy(ivA[:], psA[:, 0:32]), "dv")
        wait(SYN, "mm")
        wait(SYN, "dx")
        inc(SYN.dma_start(out=rowin2[:], in_=invf_lo), "dx", 16)
        wait_all(TEN, "dx", "dv")
        inc(TEN.matmul(psA[:, 0:32], oner[:], rowin2[:], start=True, stop=True),
            "mm")
        wait(VEC, "mm")
        inc(VEC.tensor_copy(ivlA[:], psA[:, 0:32]), "dv")
        vd()

        for ti in range(32):
            pc = posA[:, ti:ti + 1]
            wait(VEC, "ac")
            inc(VEC.tensor_scalar(tmt[:], ivA[:], pc, None, ALU.mult), "dv")
            inc(VEC.tensor_scalar(tml[:], ivlA[:], pc, None, ALU.mult), "dv")
            vd()
            for tbl, ph in ((asn, None), (acs, HFPI)):
                rrsin(cst[:, 576:608], tmt[:], 32, phase=ph, extra=tml[:])
                wait_all(TEN, "ac", "dv")
                inc(TEN.transpose(psSm[0:32, 0:128], cst[:, 576:608],
                                  identf[:]), "mm")
                wait(VEC, "mm")
                inc(VEC.tensor_copy(t32[0:32, 0:128], psSm[0:32, 0:128]), "dv")
                wait(TEN, "dv")
                inc(TEN.matmul(psSm[:, 128:256], repA[0:32, :], t32[0:32, :],
                               start=True, stop=True), "mm")
                wait(VEC, "mm")
                inc(VEC.tensor_copy(tbl[:, ti * 128:(ti + 1) * 128],
                                    psSm[:, 128:256]), "dv")

        for ti in range(32):
            wait(SYN, "ac")
            wait(SYN, "dx")
            inc(SYN.dma_start(out=xrow[:], in_=x_full[ti * 128:(ti + 1) * 128, :]),
                "dx", 16)
            wait_all(VEC, "dx", "ac")
            ln_tm(xnrA[:], xrow[:])
            tmcm(xnf, ti * 128, xnrA)

        wait(VEC, "dv")
        for h in range(2):
            inc(VEC.memset(vbh[h][:].rearrange("p (t d) -> p t d", t=32,
                                               d=65)[:, :, 64:65], 1.0), "dv")

        wsrc = a_wqkv.rearrange("(c p) e -> p c e", p=128)
        for pr in range(8):
            wait_all(GPS, "mm", "dv")
            wait(GPS, "dw")
            inc(GPS.dma_start(out=wqp[:].rearrange("p (c e) -> p c e", c=8),
                              in_=wsrc[:, :, 128 * pr:128 * pr + 128]),
                "dw", 16)
            wait(GPS, "dw")
            inc(GPS.dma_start(out=wkp[:].rearrange("p (c e) -> p c e", c=8),
                              in_=wsrc[:, :, 1024 + 128 * pr:1024 + 128 * pr + 128]),
                "dw", 16)
            wait(GPS, "dw")
            inc(GPS.dma_start(out=wvp[:].rearrange("p (c e) -> p c e", c=8),
                              in_=wsrc[:, :, 2048 + 128 * pr:2048 + 128 * pr + 128]),
                "dw", 16)
            wait_all(TEN, "dw", "dv", "ac")
            for sp in range(8):
                s0 = sp * 512
                wait_all(TEN, "ac", "dv")
                for c in range(8):
                    inc(TEN.matmul(psA[:, 0:512], wkp[:, c * 128:(c + 1) * 128],
                                   xnf[c][:, s0:s0 + 512],
                                   start=(c == 0), stop=(c == 7)), "mm")
                    inc(TEN.matmul(psB[:, 0:512], wvp[:, c * 128:(c + 1) * 128],
                                   xnf[c][:, s0:s0 + 512],
                                   start=(c == 0), stop=(c == 7)), "mm")
                wait_all(ACT, "mm", "dv")
                inc(ACT.copy(K2[:, s0:s0 + 512], psA[:, 0:512]), "ac")
                inc(ACT.copy(Vc[:, s0:s0 + 512], psB[:, 0:512]), "ac")
                wait_all(TEN, "ac", "dv")
                inc(TEN.matmul(psC[:, 0:512], permb[:], K2[:, s0:s0 + 512],
                               start=True, stop=True), "mm")
                wait_all(VEC, "mm", "ac")
                inc(VEC.tensor_tensor(rop1[:], K2[:, s0:s0 + 512],
                                      acs[:, s0:s0 + 512], ALU.mult), "dv")
                inc(VEC.tensor_tensor(rop2[:], psC[:, 0:512],
                                      asn[:, s0:s0 + 512], ALU.mult), "dv")
                vd()
                inc(VEC.tensor_tensor(K2[:, s0:s0 + 512], rop1[:], rop2[:],
                                      ALU.add), "dv")
            for sp in range(4):
                s0 = sp * 512
                wait_all(TEN, "ac", "dv")
                for c in range(8):
                    inc(TEN.matmul(psA[:, 0:512], wqp[:, c * 128:(c + 1) * 128],
                                   xnf[c][:, s0:s0 + 512],
                                   start=(c == 0), stop=(c == 7)), "mm")
                wait_all(ACT, "mm", "dv")
                inc(ACT.copy(Q2[:, s0:s0 + 512], psA[:, 0:512]), "ac")
                wait_all(TEN, "ac", "dv")
                inc(TEN.matmul(psC[:, 0:512], permb[:], Q2[:, s0:s0 + 512],
                               start=True, stop=True), "mm")
                wait_all(VEC, "mm", "ac")
                inc(VEC.tensor_tensor(rop1[:], Q2[:, s0:s0 + 512],
                                      acs[:, s0:s0 + 512], ALU.mult), "dv")
                inc(VEC.tensor_tensor(rop2[:], psC[:, 0:512],
                                      asn[:, s0:s0 + 512], ALU.mult), "dv")
                vd()
                inc(VEC.tensor_tensor(Q2[:, s0:s0 + 512], rop1[:], rop2[:],
                                      ALU.add), "dv")
            for tt in range(32):
                wait_all(TEN, "dv", "ac")
                inc(TEN.transpose(psT[:, 0:128], Vc[:, tt * 128:(tt + 1) * 128],
                                  identb[:]), "mm")
                wait(VEC, "mm")
                inc(VEC.tensor_copy(vbh[0][:, tt * 65:tt * 65 + 64],
                                    psT[:, 0:64]), "dv")
                inc(VEC.tensor_copy(vbh[1][:, tt * 65:tt * 65 + 64],
                                    psT[:, 64:128]), "dv")
            for h01 in range(2):
                hh = h01 * 64
                wait_all(SYN, "mm", "ac", "dv")
                wait(SYN, "dx")
                inc(SYN.dma_start(out=kab[0:64, :], in_=K2[hh:hh + 64, :]),
                    "dx", 16)
                wait(SYN, "dx")
                inc(SYN.dma_start(out=qab[0:64, :], in_=Q2[hh:hh + 64, :]),
                    "dx", 16)
                wait_all(VEC, "mm", "ac")
                inc(VEC.memset(kab[64:65, :], 1.0), "dv")
                inc(VEC.memset(nsc[:, 0:1], -1.0), "dv")
                for qt in range(16):
                    for sp in range(8):
                        s0 = sp * 512
                        wait_all(TEN, "dx", "dv", "ac")
                        inc(TEN.matmul(psA[:, 0:512],
                                       qab[0:64, qt * 128:(qt + 1) * 128],
                                       kab[0:64, s0:s0 + 512],
                                       start=True, stop=True), "mm")
                        wait(VEC, "mm")
                        inc(VEC.tensor_scalar(
                            rop1[:, 0:512], psA[:, 0:512], 0.0,
                            None if sp == 0 else mxc[:],
                            ALU.add, ALU.max, accum_out=mxc[:]), "dv")
                        vd()
                    wait_all(TEN, "dv", "ac")
                    inc(TEN.transpose(psC[0:1, 0:128], mxc[:], identf[:]),
                        "mm")
                    wait(VEC, "mm")
                    inc(VEC.tensor_copy(rop1[0:1, 0:128], psC[0:1, 0:128]),
                        "dv")
                    vd()
                    wait(TEN, "dv")
                    inc(TEN.matmul(psC[64:65, 0:128], nsc[:, 0:1],
                                   rop1[0:1, 0:128], start=True, stop=True),
                        "mm")
                    wait(ACT, "mm")
                    inc(ACT.copy(qab[64:65, qt * 128:(qt + 1) * 128],
                                 psC[64:65, 0:128]), "ac")
                wait_all(TEN, "dx", "ac", "dv")
                if pr == 0 and h01 == 0 and "adbg" in tap_out:
                    wait_all(GPS, "dx", "ac", "mm", "dv")
                    wait(GPS, "tp")
                    inc(GPS.dma_start(out=tap_out["adbg"][0:128, 0:2048],
                                      in_=qab[:]), "tp", 16)
                    wait(GPS, "tp")
                    inc(GPS.dma_start(out=tap_out["adbg"][0:128, 2048:6144],
                                      in_=kab[:]), "tp", 16)
                    tap_sync()
                Ksl = kab[0:65, :]
                Qsl = qab[0:65, :]
                for qs4 in range(4):
                    q0 = qs4 * 512
                    for kt in range(33):
                        h_ = kt % 2
                        if kt < 32:
                            wait(TEN, "ac", back=1)
                            wait(TEN, "dv")
                            inc(TEN.matmul(psSm[:, h_ * 512:h_ * 512 + 512],
                                           Ksl[:, kt * 128:(kt + 1) * 128],
                                           Qsl[:, q0:q0 + 512],
                                           start=True, stop=True), "mm")
                            wait(ACT, "dv")
                            wait(ACT, "mm")
                            inc(ACT.activation(ptA[:, h_ * 512:h_ * 512 + 512],
                                               psSm[:, h_ * 512:h_ * 512 + 512],
                                               AF.Exp, scale=0.125), "ac")
                        if kt > 0:
                            kp = kt - 1
                            hp = kp % 2
                            wait(TEN, "ac", back=1 if kt < 32 else 0)
                            inc(TEN.matmul(psE[0:65, 0:512],
                                           vbh[h01][:, kp * 65:kp * 65 + 65],
                                           ptA[:, hp * 512:hp * 512 + 512],
                                           start=(kp == 0), stop=(kp == 31)),
                                "mm")
                    wait(VEC, "mm")
                    if pr == 0 and h01 == 0 and qs4 == 0 and "adn" in tap_out:
                        wait_all(GPS, "mm", "ac", "dv")
                        inc(VEC.tensor_copy(rop2[0:65, 0:512],
                                            psE[0:65, 0:512]), "dv")
                        wait(GPS, "dv")
                        wait(GPS, "tp")
                        inc(GPS.dma_start(out=tap_out["adn"][0:65, 0:512],
                                          in_=rop2[0:65, 0:512]), "tp", 16)
                        wait(GPS, "tp")
                        inc(GPS.dma_start(out=tap_out["adn"][0:128, 512:1024],
                                          in_=ptA[:, 0:512]), "tp", 16)
                        tap_sync()
                    inc(VEC.tensor_copy(rop1[64:65, 0:512], psE[64:65, 0:512]),
                        "dv")
                    vd()
                    inc(VEC.reciprocal(rop1[64:65, 0:512], rop1[64:65, 0:512]),
                        "dv")
                    vd()
                    wait_all(TEN, "dv", "ac")
                    inc(TEN.matmul(psSm[:, 0:512], onesA[64:65, :],
                                   rop1[64:65, 0:512], start=True, stop=True),
                        "mm")
                    wait(VEC, "mm")
                    inc(VEC.tensor_copy(rop2[0:64, 0:512], psSm[0:64, 0:512]),
                        "dv")
                    vd()
                    inc(VEC.tensor_tensor(ost[0:64, 0:512], psE[0:64, 0:512],
                                          rop2[0:64, 0:512], ALU.mult), "dv")
                    vd()
                    if h01 == 0:
                        inc(VEC.tensor_copy(oall[pr][0:64, q0:q0 + 512],
                                            ost[0:64, 0:512]), "dv")
                    else:
                        wait_all(TEN, "dv", "ac")
                        inc(TEN.matmul(psD[:, 0:512], shf64b[:],
                                       ost[:, 0:512], start=True, stop=True),
                            "mm")
                        wait(VEC, "mm")
                        inc(VEC.tensor_copy(oall[pr][64:128, q0:q0 + 512],
                                            psD[64:128, 0:512]), "dv")

        if "atab" in tap_out:
            wait_all(GPS, "dv", "ac", "mm")
            wait(GPS, "tp")
            inc(GPS.dma_start(out=tap_out["atab"][:, 0:L], in_=acs[:]), "tp", 16)
            wait(GPS, "tp")
            inc(GPS.dma_start(out=tap_out["atab"][:, L:2 * L], in_=asn[:]),
                "tp", 16)
            for c in range(8):
                wait(GPS, "tp")
                inc(GPS.dma_start(out=tap_out["atab"][:, 2 * L + c * OWN:
                                                      2 * L + (c + 1) * OWN],
                                  in_=oall[c][:]), "tp", 16)
            tap_sync()
        freel(_pctx)
        wob2 = [aa(f"wob2{c}", [128, D], BF16) for c in range(8)]
        wait_all(GPS, "mm", "dv", "ac")
        for c in range(8):
            wait(GPS, "dw")
            inc(GPS.dma_start(out=wob2[c][:],
                              in_=a_wout[c * 128:(c + 1) * 128, :]), "dw", 16)
        for tti in range(16):
            wait(SYN, "dv")
            wait(VEC, "st")
            wait(SYN, "dx")
            inc(SYN.dma_start(out=xrow[:],
                              in_=x_full[tti * 128:(tti + 1) * 128, :]),
                "dx", 16)
            wait(SYN, "dx")
            inc(SYN.dma_start(out=gstage[:],
                              in_=g_dram[:, tti * 1024:(tti + 1) * 1024]),
                "dx", 16)
            for dh in range(2):
                pso = psA if dh == 0 else psB
                wait_all(TEN, "dw", "dv", "ac")
                for c in range(8):
                    inc(TEN.matmul(pso[:, 0:512],
                                   oall[c][:, tti * 128:(tti + 1) * 128],
                                   wob2[c][:, dh * 512:dh * 512 + 512],
                                   start=(c == 0), stop=(c == 7)), "mm")
                wait_all(VEC, "mm", "dx")
                inc(VEC.tensor_tensor(sqg[:, dh * 512:dh * 512 + 512],
                                      pso[:, 0:512],
                                      gstage[:, dh * 512:dh * 512 + 512],
                                      ALU.mult), "dv")
                vd()
                inc(VEC.tensor_tensor(sqg[:, dh * 512:dh * 512 + 512],
                                      sqg[:, dh * 512:dh * 512 + 512],
                                      xrow[:, dh * 512:dh * 512 + 512],
                                      ALU.add), "dv")
            wait(SYN, "dv")
            wait(SYN, "st")
            inc(SYN.dma_start(out=out[tti * 128:(tti + 1) * 128, :],
                              in_=sqg[:]), "st", 16)
        wait(SYN, "st")
        wait(GPS, "st")
        freel(_actx)
        return nc


def _host_inputs(L0_attn_x, L0_attn_w_qkv, L0_attn_w_out, L0_ffn_wqkv,
                 L0_ffn_q_scale, L0_ffn_k_scale, L0_ffn_wo, L0_ffn_ln_g,
                 L0_ffn_ln_b, L0_ffn_w1, L0_ffn_b1, L0_ffn_w2, L0_ffn_b2):
    import ml_dtypes
    x = np.asarray(L0_attn_x, dtype=np.float32)
    f32 = lambda a: np.ascontiguousarray(a, dtype=np.float32)

    a_wqkvT = f32(np.asarray(L0_attn_w_qkv).T)
    a_woutT = f32(np.asarray(L0_attn_w_out).T)
    b_wqkv = f32(L0_ffn_wqkv)
    b_wo = f32(L0_ffn_wo)
    w1 = np.asarray(L0_ffn_w1, dtype=np.float32)
    b1 = np.asarray(L0_ffn_b1, dtype=np.float32)
    w1p = np.zeros((2, D, W1C), dtype=np.float32)
    b1p = np.zeros((2, W1C, 1), dtype=np.float32)
    for j in range(HIDP // 128):
        a0, a1 = j * 128, min((j + 1) * 128, HID)
        n = a1 - a0
        if n > 0:
            w1p[:, :, 256 * j:256 * j + n] = w1[:, :, a0:a1]
            w1p[:, :, 256 * j + 128:256 * j + 128 + n] = w1[:, :, HID + a0:HID + a1]
            b1p[:, 256 * j:256 * j + n, 0] = b1[:, a0:a1]
            b1p[:, 256 * j + 128:256 * j + 128 + n, 0] = b1[:, HID + a0:HID + a1]
    b1p = f32(b1p[:, :, 0].reshape(2, W1C // 128, 128).transpose(0, 2, 1))
    w2p = np.zeros((2, HIDP, D), dtype=np.float32)
    w2p[:, :HID, :] = np.asarray(L0_ffn_w2, dtype=np.float32)
    b_b2 = f32(np.asarray(L0_ffn_b2).reshape(2, 8, 128).transpose(0, 2, 1))
    b_lng = f32(np.asarray(L0_ffn_ln_g).reshape(2, 8, 128).transpose(0, 2, 1))
    b_lnb = f32(np.asarray(L0_ffn_ln_b).reshape(2, 8, 128).transpose(0, 2, 1))
    b_qs = f32(np.tile(np.asarray(L0_ffn_q_scale), (1, HEADS_B))[:, None, :])
    b_ks = f32(np.tile(np.asarray(L0_ffn_k_scale), (1, HEADS_B))[:, None, :])

    _hi = 2.0 * np.arange(32).astype(np.float64)
    _invf64 = 1.0 / 10000.0 ** (_hi / 64.0)
    invf_hi_h = _invf64.astype(np.float32)
    invf_lo_h = (_invf64 - invf_hi_h.astype(np.float64)).astype(np.float32)
    _lnb64 = np.log((_hi + 0.4 * 64.0) / (1.4 * 64.0))
    lnbs_hi_h = _lnb64.astype(np.float32)
    lnbs_lo_h = (_lnb64 - lnbs_hi_h.astype(np.float64)).astype(np.float32)
    onecol = f32(np.ones((128, 1)))
    ident = np.eye(128, dtype=np.float32)
    ident_bf = ident.astype(ml_dtypes.bfloat16)
    P = np.zeros((128, 128), dtype=np.float32)
    for h0 in (0, 64):
        for d in range(32):
            P[h0 + d, h0 + d + 32] = -1.0
            P[h0 + d + 32, h0 + d] = 1.0
    perm_a = np.ascontiguousarray(P.T).astype(ml_dtypes.bfloat16)
    il = np.arange(128)
    mask_lo = np.where(il[:, None] < il[None, :], NEG,
                       0.0).astype(ml_dtypes.bfloat16)
    mask_hi = np.where(il[:, None] > il[None, :], NEG,
                       0.0).astype(ml_dtypes.bfloat16)

    onerow_h = f32(np.ones((1, 128)))
    allones_h = f32(np.ones((128, 128)))
    repm_h = f32((np.arange(128)[None, :] % 32 == np.arange(128)[:, None]) * 1.0)
    shf64_h = (np.arange(128)[None, :] == np.arange(128)[:, None] + 64
               ).astype(ml_dtypes.bfloat16)
    shared = dict(onerow=onerow_h, allones=allones_h, repm=repm_h,
                  shf64=shf64_h,
                  invf_hi=f32(invf_hi_h[None, :]), invf_lo=f32(invf_lo_h[None, :]),
                  lnbs_hi=f32(lnbs_hi_h[None, :]), lnbs_lo=f32(lnbs_lo_h[None, :]),
                  a_wqkv=a_wqkvT, a_wout=a_woutT, b_wqkv=b_wqkv, b_wo=b_wo,
                  b_w1=w1p, b_b1=b1p, b_w2=w2p, b_b2=b_b2, b_lng=b_lng,
                  b_lnb=b_lnb, b_qs=b_qs, b_ks=b_ks, onecol=onecol, ident_bf=ident_bf,
                  ident_f32=ident, perm_a=perm_a, mask_lo=mask_lo,
                  mask_hi=mask_hi)

    in_maps = []
    for c in range(NCORES):
        b, half = c // 2, c % 2
        o = OWN * half
        oth = OWN * (1 - half)
        x_fullm = np.ascontiguousarray(
            np.concatenate([x[b, o:o + OWN], x[b, oth:oth + OWN]], axis=0))
        x_bm = np.zeros((RBN, D), dtype=np.float32)
        lo = max(0, o - HALO)
        x_bm[HALO - (o - lo):] = x[b, lo:o + OWN]
        poskv = np.concatenate([np.arange(o, o + OWN),
                                np.arange(oth, oth + OWN)]).astype(np.float32)
        posk = np.ascontiguousarray(poskv[None, :])
        poska_h = np.ascontiguousarray(poskv.reshape(32, 128).T)
        posbm = np.ascontiguousarray(np.arange(o - HALO, o + OWN, dtype=np.float32).reshape(RBT, 128).T)
        w0 = np.zeros((1, NS1 + NS2), dtype=np.float32)
        if half == 0:
            w0[0, 2] = NEG
            w0[0, NS1 + 0] = NEG
        m = dict(shared)
        m.update(x_full=x_fullm, x_b=np.ascontiguousarray(x_bm), posk=posk,
                 poska=poska_h, posb=posbm, w0row=w0)
        in_maps.append(m)
    return in_maps


def kernel(**inputs):
    in_maps = _host_inputs(**inputs)
    nc = _build()
    res = run_bass_kernel_spmd(nc, in_maps, core_ids=list(range(NCORES)))
    out = np.empty((BATCH, L, D), dtype=np.float32)
    for c in range(NCORES):
        b, half = c // 2, c % 2
        out[b, OWN * half:OWN * (half + 1)] = res.results[c]["out"]
    return out
